# version 71
# baseline (speedup 1.0000x reference)
import numpy as np
import ml_dtypes
import concourse.bass as bass
import concourse.mybir as mybir
from concourse.bass_utils import run_bass_kernel_spmd
from contextlib import ExitStack

F32 = mybir.dt.float32
BF16 = mybir.dt.bfloat16
AT = mybir.AluOpType
AF = mybir.ActivationFunctionType
AX = mybir.AxisListType

D = 1024
NT = 16
FF = 2816
H = 16
WIN = 7
FSL = [(0, 256), (256, 768), (1024, 768), (1792, 768), (2560, 256)]
NQ = len(FSL)
BF_H = True
BF_W = True
BF_X = False
TE_IMPLICIT = False
VELT = 4194305.0
FIXC = 6291456.0


class Buf:
    __slots__ = ("name", "writer", "readers")

    def __init__(self, name):
        self.name = name
        self.writer = None
        self.readers = {}


class Tok:
    __slots__ = ("eng", "seq", "needed", "val")

    def __init__(self, eng, seq):
        self.eng = eng
        self.seq = seq
        self.needed = False
        self.val = None


class Sched:
    ENGS = ("te", "dve", "act", "pool", "sp")

    def __init__(self, nc, ctx):
        self.nc = nc
        self.eng_obj = {"te": nc.tensor, "dve": nc.vector, "act": nc.scalar,
                        "pool": nc.gpsimd, "sp": nc.sync}
        self.items = {e: [] for e in self.ENGS}
        self.sem = {e: ctx.enter_context(nc.semaphore(f"sem_{e}")) for e in self.ENGS}
        self.NDQ = 8
        self.dsem = {e: [ctx.enter_context(nc.semaphore(f"dsem_{e}{i}")) for i in range(self.NDQ)]
                     for e in ("sp", "act", "pool")}
        self.duse = {e: [0] * self.NDQ for e in ("sp", "act", "pool")}
        self.didx = {e: 0 for e in ("sp", "act", "pool")}

    def _collect(self, eng, reads, writes):
        deps = []
        for b in reads:
            if b.writer is not None:
                deps.append(b.writer)
        for b in writes:
            if b.writer is not None:
                deps.append(b.writer)
            deps.extend(b.readers.values())
        out = []
        seen = set()
        for d in deps:
            if isinstance(d, Tok):
                if TE_IMPLICIT and d.eng == "te" and eng == "te":
                    continue
                k = (d.eng, d.seq)
                if k in seen:
                    continue
                seen.add(k)
                d.needed = True
                out.append(d)
            else:
                k = ("dma", id(d[0]), d[1])
                if k in seen:
                    continue
                seen.add(k)
                out.append(d)
        return out

    def emit(self, eng, fn, reads=(), writes=()):
        deps = self._collect(eng, reads, writes)
        tok = Tok(eng, len(self.items[eng]))
        self.items[eng].append(("op", fn, deps, tok))
        for b in writes:
            b.writer = tok
            b.readers = {}
        for b in reads:
            b.readers[eng] = tok
        return tok

    def dma(self, eng, fn, reads=(), writes=()):
        assert eng in ("sp", "act", "pool")
        deps = self._collect(eng, reads, writes)
        i = self.didx[eng]
        self.didx[eng] = (i + 1) % self.NDQ
        dsem = self.dsem[eng][i]
        if self.duse[eng][i] > 0:
            deps.append((dsem, 16 * self.duse[eng][i]))
        self.duse[eng][i] += 1
        dtok = (dsem, 16 * self.duse[eng][i])
        self.items[eng].append(("dma", fn, deps, dtok))
        for b in writes:
            b.writer = dtok
            b.readers = {}
        for b in reads:
            b.readers[("q", id(dtok[0]))] = dtok
        return dtok

    def barrier(self):
        toks = []
        for o in self.ENGS:
            for kind, fn, deps, tk in reversed(self.items[o]):
                if kind == "op":
                    tk.needed = True
                    toks.append(tk)
                    break
        for qe in ("sp", "act", "pool"):
            for i in range(self.NDQ):
                if self.duse[qe][i] > 0:
                    toks.append((self.dsem[qe][i], 16 * self.duse[qe][i]))
        for e in self.ENGS:
            tok = Tok(e, len(self.items[e]))
            self.items[e].append(("barrier", None, list(toks), tok))

    def replay(self):
        nc = self.nc
        for e in self.ENGS:
            cnt = 0
            for kind, fn, deps, tok in self.items[e]:
                if isinstance(tok, Tok) and tok.needed:
                    cnt += 1
                    tok.val = cnt

        def run_engine(eng):
            obj = self.eng_obj[eng]
            waited = {}

            def do_wait(d):
                if isinstance(d, Tok):
                    if d.eng == eng and isinstance(self.items[eng][0], tuple):
                        pass
                    sem, val, key = self.sem[d.eng], d.val, f"e:{d.eng}"
                    assert val is not None, f"unresolved tok {d.eng}:{d.seq}"
                else:
                    sem, val = d
                    key = f"d:{id(sem)}"
                if waited.get(key, 0) >= val:
                    return
                waited[key] = val
                obj.wait_ge(sem, val)

            for kind, fn, deps, tok in self.items[eng]:
                for d in deps:
                    do_wait(d)
                if kind == "op":
                    inst = fn(obj)
                    if tok.needed:
                        inst.then_inc(self.sem[eng], 1)
                elif kind == "dma":
                    fn(obj).then_inc(tok[0], 16)

        with nc.Block() as block:
            for dec, eng in ((block.tensor, "te"), (block.vector, "dve"),
                             (block.scalar, "act"), (block.gpsimd, "pool"),
                             (block.sync, "sp")):
                def run(_e, eng=eng):
                    run_engine(eng)
                dec(run)


def build_program(debug=False):
    nc = bass.Bass("TRN2", target_bir_lowering=False)
    ctx = ExitStack()
    S = Sched(nc, ctx)
    sb = ctx.enter_context

    dr = {}
    dr["x"] = nc.dram_tensor("x", [2048, D], F32, kind="ExternalInput").ap()
    for w in ("wq", "wk", "wv", "wo"):
        dr[w] = nc.dram_tensor(w, [D, D], F32, kind="ExternalInput").ap()
    dr["bo"] = nc.dram_tensor("bo", [D], F32, kind="ExternalInput").ap()
    dr["cw1"] = nc.dram_tensor("cw1", [H, 32], F32, kind="ExternalInput").ap()
    dr["cb1"] = nc.dram_tensor("cb1", [H, 32], F32, kind="ExternalInput").ap()
    dr["cw2"] = nc.dram_tensor("cw2", [H, 32, 32], F32, kind="ExternalInput").ap()
    dr["cb2"] = nc.dram_tensor("cb2", [H, 32], F32, kind="ExternalInput").ap()
    dr["cw3"] = nc.dram_tensor("cw3", [H, 32], F32, kind="ExternalInput").ap()
    dr["cb3"] = nc.dram_tensor("cb3", [H], F32, kind="ExternalInput").ap()
    dr["w_gate"] = nc.dram_tensor("w_gate", [FF, D], F32, kind="ExternalInput").ap()
    dr["w_up"] = nc.dram_tensor("w_up", [FF, D], F32, kind="ExternalInput").ap()
    dr["w_down"] = nc.dram_tensor("w_down", [D, FF], F32, kind="ExternalInput").ap()
    dr["ident_bf"] = nc.dram_tensor("ident_bf", [128, 128], BF16, kind="ExternalInput").ap()
    dr["ident_f"] = nc.dram_tensor("ident_f", [128, 128], F32, kind="ExternalInput").ap()
    dr["out"] = nc.dram_tensor("out", [2048, D], F32, kind="ExternalOutput").ap()
    if debug:
        dr["dbg_ffn1"] = nc.dram_tensor("dbg_ffn1", [2048, D], F32, kind="ExternalOutput").ap()
        dr["dbg_attn"] = nc.dram_tensor("dbg_attn", [2048, D], F32, kind="ExternalOutput").ap()
        dr["dbg_wgT"] = nc.dram_tensor("dbg_wgT", [D, FF], BF16, kind="ExternalOutput").ap()
    dr["wgT"] = nc.dram_tensor("wgT", [D, FF], BF16, kind="Internal").ap()
    dr["wupT"] = nc.dram_tensor("wupT", [D, FF], BF16, kind="Internal").ap()
    dr["wdT"] = nc.dram_tensor("wdT", [FF, D], BF16, kind="Internal").ap()
    dr["connd"] = nc.dram_tensor("connd", [128, H * 134], BF16, kind="Internal").ap()

    x_res = sb(nc.sbuf_tensor("x_res", [128, NT, D], F32))
    ident_bf = sb(nc.sbuf_tensor("identb", [128, 128], BF16))
    ones_bf = sb(nc.sbuf_tensor("ones_bf", [1, 128], BF16))
    ln_mv = sb(nc.sbuf_tensor("ln_mv", [128, NT, 2], F32))
    ln_bn = sb(nc.sbuf_tensor("ln_bn", [128, 2, 12], F32))
    ln_var = sb(nc.sbuf_tensor("ln_var", [128, NT], F32))
    ln_rstd = sb(nc.sbuf_tensor("ln_rstd", [128, NT], F32))
    qconst = sb(nc.sbuf_tensor("qconst", [128, 1], F32))
    qconst2 = sb(nc.sbuf_tensor("qconst2", [128, 1], F32))

    PA = sb(nc.psum_tensor("PA", [128, 2048], F32))
    PB = sb(nc.psum_tensor("PB", [128, 1024], F32))
    PC = sb(nc.psum_tensor("PC", [128, 1024], F32))
    PBANK = [Buf(f"pbank{i}") for i in range(8)]

    def pa(lo_el, hi_el):
        return PBANK[lo_el // 512: (hi_el + 511) // 512]

    def pb(lo_el, hi_el):
        return PBANK[4 + lo_el // 512: 4 + (hi_el + 511) // 512]

    def pc(lo_el, hi_el):
        return PBANK[6 + lo_el // 512: 6 + (hi_el + 511) // 512]

    bufs = {}

    def B(name):
        if name not in bufs:
            bufs[name] = Buf(name)
        return bufs[name]

    xb = [B(f"x_{t}") for t in range(NT)]

    for xi, xq_eng in enumerate(("sp", "act", "pool", "sp")):
        S.dma(xq_eng, lambda e, xi=xi: e.dma_start(
            out=x_res[:, xi * 4:(xi + 1) * 4, :],
            in_=dr["x"].rearrange("(t p) d -> p t d", p=128)[:, xi * 4:(xi + 1) * 4, :]),
            writes=xb[xi * 4:(xi + 1) * 4])
    S.dma("sp", lambda e: e.dma_start(out=ident_bf[:], in_=dr["ident_bf"]), writes=[B("identb")])
    S.emit("dve", lambda v: v.memset(ones_bf[:], 1.0), writes=[B("ones")])
    S.emit("dve", lambda v: v.memset(qconst[:], 65.0), writes=[B("qconst")])
    S.emit("dve", lambda v: v.memset(qconst2[:], VELT), writes=[B("qconst")])

    def make_qt(stk, tag, width, with_t1=True):
        qt = {}
        names = ("t0", "t1", "t2", "xn") if with_t1 else ("t0", "t2", "xn")
        for n in names:
            qt[n] = stk.enter_context(nc.sbuf_tensor(f"{tag}_{n}", [128, width], F32))
        for n in ("bmax", "scale", "inv"):
            qt[n] = stk.enter_context(nc.sbuf_tensor(f"{tag}_{n}", [128, width // 16], F32))
        qt["bufs"] = {n: B(f"{tag}_{n}") for n in names + ("bmax", "scale", "inv")}
        qt["c65"] = qconst
        qt["cVELT"] = qconst2
        return qt

    def quant_chain_ops(qt, in_ap, in_bufs, out_ap, out_buf, c0, c1, tag, bf=False):
        ops = []
        Q = qt["bufs"]

        def key(n):
            return Q[n] if tag == 0 else B(f"{Q[n].name}_h{tag}")
        b0, b1 = c0 // 16, c1 // 16
        nblk = b1 - b0
        ncols = c1 - c0
        bmax = qt["bmax"][:, b0:b1]
        scale = qt["scale"][:, b0:b1]
        inv = qt["inv"][:, b0:b1]
        if bf:
            t0 = qt["t0"].ap().bitcast(BF16)[:, c0:c1]
            t1 = qt["t1"].ap().bitcast(BF16)[:, c0:c1]
            t2 = qt["t2"].ap().bitcast(BF16)[:, c0:c1]
            xn = qt["xn"].ap().bitcast(BF16)[:, c0:c1]
            m_u8 = qt["xn"].ap().bitcast(mybir.dt.uint8)[:, c0 * 2:c0 * 2 + ncols]
            velt_scale = qt["c65"]
            fixc = 96.0
        else:
            t0 = qt["t0"][:, c0:c1]
            t1 = qt["t1"][:, c0:c1]
            t2 = qt["t2"][:, c0:c1]
            xn = qt["xn"][:, c0:c1]
            m_u8 = qt["xn"].ap().bitcast(mybir.dt.uint8)[:, c0 * 4:c0 * 4 + ncols]
            velt_scale = qt["cVELT"]
            fixc = FIXC
        inap = in_ap[:, c0:c1]
        outap = out_ap[:, c0:c1]
        ing = inap.rearrange("p (b s) -> p b s", s=16)
        invb = inv.unsqueeze(-1).broadcast_to([128, nblk, 16])
        scb = scale.unsqueeze(-1).broadcast_to([128, nblk, 16])
        xng = xn.rearrange("p (b s) -> p b s", s=16)
        outg = outap.rearrange("p (b s) -> p b s", s=16)
        qg = t0.rearrange("p (b s) -> p b s", s=16)
        E = ops.append
        E(("dve", lambda v: v.reduce_max(out=bmax, in_=ing, axis=AX.X,
                                         apply_absolute_value=True),
           list(in_bufs) + base("bmax"), [key("bmax")]))
        E(("dve", lambda v: v.tensor_scalar(out=scale, in0=bmax, scalar1=1.0 / 6.0,
                                            scalar2=1e-12, op0=AT.mult, op1=AT.max),
           [key("bmax")] + base("bmax") + base("scale"), [key("scale")]))
        E(("dve", lambda v: v.reciprocal(out=inv, in_=scale),
           [key("scale")] + base("scale") + base("inv"), [key("inv")]))
        E(("dve", lambda v: v.tensor_tensor(out=xng, in0=ing, in1=invb, op=AT.mult),
           list(in_bufs) + [key("inv")], [key("xn")]))
        E(("act", lambda a: a.activation(out=t0, in_=xn, func=AF.Copy,
                                         scale=velt_scale[:, 0:1]),
           [key("xn"), B("qconst")], [key("t0")]))
        E(("dve", lambda v: v.tensor_sub(out=t1, in0=t0, in1=xn),
           [key("t0"), key("xn")], [key("t1")]))
        E(("dve", lambda v: v.tensor_sub(out=t2, in0=t0, in1=t1),
           [key("t0"), key("t1")], [key("t2")]))
        if bf:
            E(("dve", lambda v: v.tensor_single_scalar(out=t0, in_=xn, scalar=fixc, op=AT.add),
               [key("xn")], [key("t0")]))
            E(("dve", lambda v: v.tensor_single_scalar(out=t0, in_=t0, scalar=fixc,
                                                       op=AT.subtract),
               [key("t0")], [key("t0")]))
        else:
            E(("dve", lambda v: v.tensor_scalar(out=t0, in0=xn, scalar1=fixc, scalar2=fixc,
                                                op0=AT.add, op1=AT.subtract),
               [key("xn")], [key("t0")]))
        E(("act", lambda a: a.activation(out=t1, in_=xn, func=AF.Abs),
           [key("xn")], [key("t1")]))
        E(("dve", lambda v: v.tensor_single_scalar(out=m_u8, in_=t1, scalar=1.0, op=AT.is_gt),
           [key("t1")], [key("xn")]))
        E(("dve", lambda v: v.copy_predicated(out=t0, mask=m_u8, data=t2),
           [key("xn"), key("t2"), key("t0")], [key("t0")]))
        E(("dve", lambda v: v.tensor_tensor(out=outg, in0=qg, in1=scb, op=AT.mult),
           [key("t0"), key("scale")], [out_buf]))
        return ops

    def qt_all(qt, n):
        nm = qt["bufs"][n].name
        return [qt["bufs"][n], B(f"{nm}_h1"), B(f"{nm}_h2")]

    def quant_chain_ops_v2(qt, in_ap, in_bufs, out_ap, out_buf, c0, c1, tag, bf=True,
                           pool_heavy=False):
        ops = []
        Q = qt["bufs"]

        def key(n):
            return Q[n] if tag == 0 else B(f"{Q[n].name}_h{tag}")

        def base(n):
            return []
        b0, b1 = c0 // 16, c1 // 16
        nblk = b1 - b0
        bmax = qt["bmax"][:, b0:b1]
        scale = qt["scale"][:, b0:b1]
        inv = qt["inv"][:, b0:b1]
        if bf:
            co2, W = 2 * c0, c1 - c0
            t0i = qt["t0"].ap().bitcast(mybir.dt.int16)[:, co2:co2 + W]
            t0p = qt["t0"].ap().bitcast(BF16)[:, co2:co2 + W]
            t2i = qt["t2"].ap().bitcast(mybir.dt.int16)[:, co2:co2 + W]
            t2x = qt["t2"].ap().bitcast(BF16)[:, co2:co2 + W]
            mask = 32640
        else:
            t0i = qt["t0"].ap().bitcast(mybir.dt.int32)[:, c0:c1]
            t0p = qt["t0"][:, c0:c1]
            t2i = qt["t2"].ap().bitcast(mybir.dt.int32)[:, c0:c1]
            t2x = qt["t2"][:, c0:c1]
            mask = 2139095040
        t0f = qt["t0"][:, c0:c1]
        csf = qt["xn"][:, c0:c1]
        inap = in_ap[:, c0:c1]
        outap = out_ap[:, c0:c1]
        ing = inap.rearrange("p (b s) -> p b s", s=16)
        invb = inv.unsqueeze(-1).broadcast_to([128, nblk, 16])
        scb = scale.unsqueeze(-1).broadcast_to([128, nblk, 16])
        xg = t2x.rearrange("p (b s) -> p b s", s=16)
        outg = outap.rearrange("p (b s) -> p b s", s=16)
        RC = 12582912.0
        E = ops.append
        E(("dve", lambda v: v.reduce_max(out=bmax, in_=ing, axis=AX.X,
                                         apply_absolute_value=True),
           list(in_bufs) + base("bmax"), [key("bmax")]))
        E(("dve", lambda v: v.tensor_scalar(out=scale, in0=bmax, scalar1=1.0 / 6.0,
                                            scalar2=1e-12, op0=AT.mult, op1=AT.max),
           [key("bmax")] + base("bmax") + base("scale"), [key("scale")]))
        E(("dve", lambda v: v.reciprocal(out=inv, in_=scale),
           [key("scale")] + base("scale") + base("inv"), [key("inv")]))
        E(("pool", lambda g: g.tensor_tensor(out=xg, in0=ing, in1=invb, op=AT.mult),
           list(in_bufs) + [key("inv")] + base("inv") + base("t2"), [key("t2")]))
        E(("dve", lambda v: v.tensor_single_scalar(out=t0i, in_=t2i, scalar=mask,
                                                   op=AT.bitwise_and),
           [key("t2")] + base("t2") + base("t0"), [key("t0")]))
        E(("dve", lambda v: v.tensor_scalar(out=csf, in0=t0p, scalar1=RC * 0.5,
                                            scalar2=RC * 0.5, op0=AT.mult, op1=AT.max),
           [key("t0")] + base("t0") + base("xn"), [key("xn")]))
        E(("pool", lambda g: g.tensor_tensor(out=t0f, in0=t2x, in1=csf, op=AT.add),
           [key("t2"), key("xn")] + base("t2") + base("xn") + base("t0"), [key("t0")]))
        e_sub = "pool" if pool_heavy else "dve"
        E((e_sub, lambda v: v.tensor_tensor(out=t2x, in0=t0f, in1=csf, op=AT.subtract),
           [key("t0"), key("xn")] + base("t0") + base("xn") + base("t2"), [key("t2")]))
        E((e_sub, lambda v: v.tensor_tensor(out=outg, in0=xg, in1=scb, op=AT.mult),
           [key("t2"), key("scale")] + base("t2") + base("scale"), [out_buf]))
        return ops

    def emit_chains(*chains):
        chains = [list(c) for c in chains]
        idx = [0] * len(chains)
        while True:
            done = True
            for ci, c in enumerate(chains):
                if idx[ci] < len(c):
                    eng, fn, rd, wr = c[idx[ci]]
                    S.emit(eng, fn, reads=rd, writes=wr)
                    idx[ci] += 1
                    done = False
            if done:
                break

    def quant_emit(qt, in_ap, in_bufs, out_ap, out_buf, ncols, nsplit=2, bf=False,
                   pool_heavy=False):
        nb = ncols // 16
        half_b = (nb // (2 * nsplit)) * 2 if nsplit > 1 else nb
        cuts = [0]
        for i in range(nsplit - 1):
            cuts.append(min(ncols, (i + 1) * (nb // nsplit) * 16))
        cuts.append(ncols)
        chains = []
        for i in range(nsplit):
            if cuts[i] < cuts[i + 1]:
                chains.append(quant_chain_ops_v2(qt, in_ap, in_bufs, out_ap, out_buf,
                                                 cuts[i], cuts[i + 1], i, bf=bf,
                                                 pool_heavy=pool_heavy))
        emit_chains(*chains)

    with (nc.sbuf_tensor("identf", [128, 128], F32) as ident_f,
          nc.sbuf_tensor("c_lhs1", [2, 16], F32) as c_lhs1,
          nc.sbuf_tensor("c_ones7", [1, 7], F32) as c_ones7,
          nc.sbuf_tensor("c_onesf", [1, 128], F32) as c_onesf,
          nc.sbuf_tensor("c_rhs1", [2, 512], F32) as c_rhs1,
          nc.sbuf_tensor("c_h1", [7, 512], F32) as c_h1,
          nc.sbuf_tensor("c_h1T", [32, H, 7], F32) as c_h1T,
          nc.sbuf_tensor("c_w2", [32, H, 32], F32) as c_w2,
          nc.sbuf_tensor("c_w2T", [32, H, 32], F32) as c_w2T,
          nc.sbuf_tensor("c_cb2", [1, 512], F32) as c_cb2,
          nc.sbuf_tensor("c_h2", [32, H, 7], F32) as c_h2,
          nc.sbuf_tensor("c_w3T", [32, H], F32) as c_w3T,
          nc.sbuf_tensor("c_cb3", [1, H], F32) as c_cb3,
          ExitStack() as p_stk):
        c_cwt = p_stk.enter_context(nc.sbuf_tensor("c_cwt", [1, H, 7], F32))
        c_max = p_stk.enter_context(nc.sbuf_tensor("c_max", [1, H], F32))
        c_conn = p_stk.enter_context(nc.sbuf_tensor("c_conn", [1, H, 7], F32))
        c_connb = p_stk.enter_context(nc.sbuf_tensor("c_connb", [128, H, 7], F32))
        c_dj = p_stk.enter_context(nc.sbuf_tensor("c_dj", [128, 134], F32))
        c_mask = p_stk.enter_context(nc.sbuf_tensor("c_mask", [128, 134], F32))
        conn_d = p_stk.enter_context(nc.sbuf_tensor("conn_bld", [128, H, 134], BF16))
        c_ps = PB[:, 0:512]
        c_ps2 = PB[:, 512:1024]
        S.dma("sp", lambda e: e.dma_start(out=ident_f[:], in_=dr["ident_f"]), writes=[B("identf")])
        S.emit("dve", lambda v: v.memset(c_lhs1[:], 1.0), writes=[B("c_lhs1")])
        S.emit("dve", lambda v: v.memset(c_ones7[:], 1.0), writes=[B("c_ones7")])
        S.emit("dve", lambda v: v.memset(c_onesf[:], 1.0), writes=[B("c_onesf")])
        S.emit("pool", lambda g: g.iota(c_dj[0:1, 0:7], pattern=[[1, 7]], base=0,
                                        channel_multiplier=0, allow_small_or_imprecise_dtypes=True),
               writes=[B("c_dj")])
        S.emit("dve", lambda v: v.tensor_single_scalar(out=c_lhs1[0:1, 0:7], in_=c_dj[0:1, 0:7],
                                                       scalar=1.0 / 6.0, op=AT.mult),
               reads=[B("c_dj")], writes=[B("c_lhs1")])
        S.dma("sp", lambda e: e.dma_start(out=c_rhs1[0:1, :],
                                          in_=dr["cw1"].rearrange("h j -> (h j)").rearrange("(o f) -> o f", o=1)),
              writes=[B("c_rhs1")])
        S.dma("sp", lambda e: e.dma_start(out=c_rhs1[1:2, :],
                                          in_=dr["cb1"].rearrange("h j -> (h j)").rearrange("(o f) -> o f", o=1)),
              writes=[B("c_rhs1")])
        S.emit("te", lambda t: t.matmul(c_ps[0:7, :], c_lhs1[:, 0:7], c_rhs1[:],
                                        start=True, stop=True),
               reads=[B("c_lhs1"), B("c_rhs1")], writes=pb(0, 512))
        S.emit("act", lambda a: a.activation(out=c_h1[:], in_=c_ps[0:7, :], func=AF.Gelu),
               reads=pb(0, 512), writes=[B("c_h1")])
        for h in range(H):
            S.emit("te", lambda t, h=h: t.transpose(c_ps2[0:32, h * 7:(h + 1) * 7],
                                                    c_h1[:, h * 32:(h + 1) * 32],
                                                    ident_f[0:7, 0:7]),
                   reads=[B("c_h1"), B("identf")], writes=pb(512, 1024))
        S.emit("dve", lambda v: v.tensor_copy(out=c_h1T[:].rearrange("p h w -> p (h w)"),
                                              in_=c_ps2[0:32, 0:H * 7]),
               reads=pb(512, 1024), writes=[B("c_h1T")])
        S.dma("sp", lambda e: e.dma_start(out=c_w2[:], in_=dr["cw2"].rearrange("h j i -> j h i")),
              writes=[B("c_w2")])
        for h in range(H):
            S.emit("te", lambda t, h=h: t.transpose(c_ps2[0:32, h * 32:(h + 1) * 32],
                                                    c_w2[:, h, :], ident_f[0:32, 0:32]),
                   reads=[B("c_w2"), B("identf")], writes=pb(512, 1024))
        S.emit("dve", lambda v: v.tensor_copy(out=c_w2T[:].rearrange("p h i -> p (h i)"),
                                              in_=c_ps2[0:32, 0:H * 32]),
               reads=pb(512, 1024), writes=[B("c_w2T")])
        S.dma("sp", lambda e: e.dma_start(out=c_cb2[:],
                                          in_=dr["cb2"].rearrange("h j -> (h j)").rearrange("(o f) -> o f", o=1)),
              writes=[B("c_cb2")])
        for h in range(H):
            S.emit("te", lambda t, h=h: t.matmul(c_ps[0:32, h * 7:(h + 1) * 7], c_w2T[:, h, :],
                                                 c_h1T[:, h, :], start=True, stop=False),
                   reads=[B("c_w2T"), B("c_h1T")], writes=pb(0, 512))
            S.emit("te", lambda t, h=h: t.matmul(c_ps[0:32, h * 7:(h + 1) * 7],
                                                 c_cb2[:, h * 32:(h + 1) * 32], c_ones7[:],
                                                 start=False, stop=True),
                   reads=[B("c_cb2"), B("c_ones7")], writes=pb(0, 512))
        S.emit("act", lambda a: a.activation(out=c_h2[:].rearrange("p h w -> p (h w)"),
                                             in_=c_ps[0:32, 0:H * 7], func=AF.Gelu),
               reads=pb(0, 512), writes=[B("c_h2")])
        S.dma("sp", lambda e: e.dma_start(out=c_w3T[:], in_=dr["cw3"].rearrange("h i -> i h"),
                                          allow_slow_non_contiguous=True),
              writes=[B("c_w3T")])
        S.dma("sp", lambda e: e.dma_start(out=c_cb3[:], in_=dr["cb3"].rearrange("(o h) -> o h", o=1)),
              writes=[B("c_cb3")])
        for h in range(H):
            S.emit("te", lambda t, h=h: t.matmul(c_ps2[0:1, h * 7:(h + 1) * 7], c_w3T[:, h:h + 1],
                                                 c_h2[:, h, :], start=True, stop=True),
                   reads=[B("c_w3T"), B("c_h2")], writes=pb(512, 1024))
        cb3b = c_cb3[:].unsqueeze(-1).broadcast_to([1, H, 7])
        S.emit("dve", lambda v: v.tensor_tensor(
            out=c_cwt[:], in0=c_ps2[0:1, 0:H * 7].rearrange("p (h w) -> p h w", w=7),
            in1=cb3b, op=AT.add),
            reads=pb(512, 1024) + [B("c_cb3")], writes=[B("c_cwt")])
        S.emit("dve", lambda v: v.reduce_max(out=c_max[:], in_=c_cwt[:], axis=AX.X),
               reads=[B("c_cwt")], writes=[B("c_max")])
        cmaxb = c_max[:].unsqueeze(-1).broadcast_to([1, H, 7])
        S.emit("dve", lambda v: v.tensor_tensor(out=c_cwt[:], in0=c_cwt[:], in1=cmaxb,
                                                op=AT.subtract),
               reads=[B("c_cwt"), B("c_max")], writes=[B("c_cwt")])
        S.emit("act", lambda a: a.activation(out=c_cwt[:], in_=c_cwt[:], func=AF.Exp),
               reads=[B("c_cwt")], writes=[B("c_cwt")])
        S.emit("dve", lambda v: v.reduce_sum(out=c_max[:], in_=c_cwt[:], axis=AX.X),
               reads=[B("c_cwt")], writes=[B("c_max")])
        S.emit("dve", lambda v: v.reciprocal(out=c_max[:], in_=c_max[:]),
               reads=[B("c_max")], writes=[B("c_max")])
        csumb = c_max[:].unsqueeze(-1).broadcast_to([1, H, 7])
        S.emit("dve", lambda v: v.tensor_tensor(out=c_conn[:], in0=c_cwt[:], in1=csumb,
                                                op=AT.mult),
               reads=[B("c_cwt"), B("c_max")], writes=[B("c_conn")])
        S.emit("te", lambda t: t.matmul(c_ps[:, 0:H * 7], c_onesf[:],
                                        c_conn[:].rearrange("p h w -> p (h w)"),
                                        start=True, stop=True),
               reads=[B("c_onesf"), B("c_conn")], writes=pb(0, 512))
        S.emit("dve", lambda v: v.tensor_copy(out=c_connb[:].rearrange("p h w -> p (h w)"),
                                              in_=c_ps[:, 0:H * 7]),
               reads=pb(0, 512), writes=[B("c_connb")])
        S.emit("pool", lambda g: g.iota(c_dj[:], pattern=[[1, 134]], base=0,
                                        channel_multiplier=-1, allow_small_or_imprecise_dtypes=True),
               writes=[B("c_dj")])
        S.emit("dve", lambda v: v.memset(conn_d[:], 0.0),
               writes=[B("conn_d")])
        for w in range(WIN):
            S.emit("dve", lambda v, w=w: v.tensor_single_scalar(out=c_mask[:], in_=c_dj[:],
                                                                scalar=float(w), op=AT.is_equal),
                   reads=[B("c_dj")], writes=[B("c_mask")])
            for h in range(H):
                S.emit("dve", lambda v, w=w, h=h: v.scalar_tensor_tensor(
                    out=conn_d[:, h, :], in0=c_mask[:], scalar=c_connb[:, h, w:w + 1],
                    in1=conn_d[:, h, :], op0=AT.mult, op1=AT.add),
                    reads=[B("c_mask"), B("c_connb"), B("conn_d")], writes=[B("conn_d")])
        S.dma("sp", lambda e: e.dma_start(out=dr["connd"],
                                          in_=conn_d[:].rearrange("p h w -> p (h w)")),
              reads=[B("conn_d")])

    S.barrier()

    wgT_s = [B(f"wgT_s{s}") for s in range(22)]
    wupT_s = [B(f"wupT_s{s}") for s in range(22)]
    wdT_s = [[B(f"wdT_s{s}_{seg}") for seg in range(NQ)] for s in range(8)]

    def emit_ln_tile(t):
        par = t % 2
        S.emit("dve", lambda v, t=t, par=par: v.bn_stats(
            out=ln_bn[:, par, 0:6], in_=x_res[:, t, 0:512]),
            reads=[xb[t]], writes=[B(f"ln_bn{par}")])
        S.emit("dve", lambda v, t=t, par=par: v.bn_stats(
            out=ln_bn[:, par, 6:12], in_=x_res[:, t, 512:1024]),
            reads=[xb[t]], writes=[B(f"ln_bn{par}")])
        S.emit("dve", lambda v, t=t, par=par: v.bn_aggr(
            out=ln_mv[:, t, :], in_=ln_bn[:, par, :]),
            reads=[B(f"ln_bn{par}")], writes=[B("ln_mv")])

    def emit_ln_final():
        S.emit("dve", lambda v: v.tensor_single_scalar(
            out=ln_var[:], in_=ln_mv[:, :, 1], scalar=1e-5, op=AT.add),
            reads=[B("ln_mv")], writes=[B("ln_var")])
        S.emit("act", lambda a: a.activation(out=ln_rstd[:], in_=ln_var[:], func=AF.Ln),
               reads=[B("ln_var")], writes=[B("ln_rstd")])
        S.emit("dve", lambda v: v.tensor_single_scalar(out=ln_rstd[:], in_=ln_rstd[:],
                                                       scalar=-0.5, op=AT.mult),
               reads=[B("ln_rstd")], writes=[B("ln_rstd")])
        S.emit("act", lambda a: a.activation(out=ln_rstd[:], in_=ln_rstd[:], func=AF.Exp),
               reads=[B("ln_rstd")], writes=[B("ln_rstd")])

    def emit_ln_stats():
        for t in range(NT):
            emit_ln_tile(t)
        emit_ln_final()

    def store_tile(t):
        qe = ("sp", "act", "pool")[t % 3]
        S.dma(qe, lambda e, t=t: e.dma_start(
            out=dr["out"].rearrange("(t p) d -> p t d", p=128)[:, t, :],
            in_=x_res[:, t, :]), reads=[xb[t]])

    def emit_ffn(phase, with_prep=False, store_out=False, skip_ln=False):
        with (nc.sbuf_tensor(f"f{phase}_wg", [128, 2, 8, 768], BF16) as wg_sb,
              nc.sbuf_tensor(f"f{phase}_wu", [128, 2, 8, 768], BF16) as wu_sb,
              nc.sbuf_tensor(f"f{phase}_wd", [128, 6, D], BF16) as wd_sb,
              nc.sbuf_tensor(f"f{phase}_xqT", [128, NT, 8, 128], BF16) as xqT_sb,
              nc.sbuf_tensor(f"f{phase}_xq", [128, D], BF16) as xq_sb,
              nc.sbuf_tensor(f"f{phase}_h", [128, 2, 768], BF16) as h_sb,
              nc.sbuf_tensor(f"f{phase}_hq", [128, 2, 768], BF16) as hq_sb,
              nc.sbuf_tensor(f"f{phase}_hqT", [128, 2, 6, 128], BF16) as hqT_sb,
              ExitStack() as f_stk):
            qt2 = [make_qt(f_stk, f"f{phase}qa", D, with_t1=False),
                   make_qt(f_stk, f"f{phase}qb", 768, with_t1=False)]
            qt = qt2[0]
            if with_prep:
                qt_wp = qt2[1]
                WPD = 4
                wp_in = f_stk.enter_context(nc.sbuf_tensor("wp_in", [128, WPD, 768], F32))
                wp_q = f_stk.enter_context(nc.sbuf_tensor("wp_q", [128, 2, 768], BF16))
                wp_st = f_stk.enter_context(nc.sbuf_tensor("wp_st", [128, 2, 4, 128], BF16))
                win_b = [B(f"wp_in{i}") for i in range(WPD)]
                wq_b = [B("wp_q0"), B("wp_q1")]
                wp_tps = PC[:, 512:1024].bitcast(BF16)
                piece_ctr = [0]
                st_ctr = [0]
                slice_jobs = []
                for s in range(2):
                    slice_jobs.append(("w_gate", s))
                    slice_jobs.append(("w_up", s))
                for s in range(8):
                    slice_jobs.append(("w_down", (s, 0)))
                need_at = {}
                gu_done = 2
                for seg in range(1, NQ):
                    for s in range(8):
                        slice_jobs.append(("w_down", (s, seg)))
                    es = (FSL[seg][0] + FSL[seg][1]) // 128
                    for s in range(gu_done, es):
                        slice_jobs.append(("w_gate", s))
                        slice_jobs.append(("w_up", s))
                    gu_done = es
                    need_at[seg] = len(slice_jobs)

                def prep_job(j):
                    name, s = slice_jobs[j]
                    coff = 0
                    if name == "w_down":
                        s, seg = s
                        coff, ncols = FSL[seg]
                        src_t, dst, dstb = dr["w_down"], dr["wdT"], wdT_s[s][seg]
                    elif name == "w_gate":
                        src_t, dst, dstb, ncols = dr["w_gate"], dr["wgT"], wgT_s[s], D
                    else:
                        src_t, dst, dstb, ncols = dr["w_up"], dr["wupT"], wupT_s[s], D
                    pieces = []
                    for p0 in range(0, ncols, 384):
                        p1 = min(ncols, p0 + 384)
                        pn = p1 - p0
                        wi = piece_ctr[0] % WPD
                        k = piece_ctr[0] % 2
                        s2 = (piece_ctr[0] // 2) % 2
                        piece_ctr[0] += 1
                        cq0 = k * 384
                        qe = ("sp", "act", "pool")[piece_ctr[0] % 3]
                        S.dma(qe, lambda e, src_t=src_t, s=s, p0=p0 + coff, pn=pn,
                              wi=wi, cq0=cq0: e.dma_start(
                            out=wp_in[:, wi, cq0:cq0 + pn],
                            in_=src_t[s * 128:(s + 1) * 128, p0:p0 + pn]),
                            writes=[win_b[wi]])
                        wqk = B(f"wp_q{s2}_{k}")
                        chain = quant_chain_ops_v2(
                            qt_wp, wp_in[:, wi, :], [win_b[wi]], wp_q[:, s2, :],
                            wqk, cq0, cq0 + pn, k, bf=BF_W)

                        def post(s=s, p0=p0, pn=pn, coff=coff, s2=s2, cq0=cq0,
                                 wqk=wqk, dst=dst, dstb=dstb):
                            nch = pn // 128
                            for c in range(nch):
                                S.emit("te", lambda t, c=c, s2=s2, cq0=cq0: t.transpose(
                                    wp_tps[:, c * 128:(c + 1) * 128],
                                    wp_q[:, s2, cq0 + c * 128:cq0 + (c + 1) * 128],
                                    ident_bf[:]),
                                    reads=[wqk, B("identb")],
                                    writes=pc(512, 1024) if (c == 0 or c == nch - 1) else ())
                            gp = st_ctr[0] % 2
                            st_ctr[0] += 1
                            wstb2 = B(f"wp_st{gp}")
                            S.emit("dve", lambda v, nch=nch, gp=gp: v.tensor_copy(
                                out=wp_st[:, gp, 0:nch, :].rearrange("p c n -> p (c n)"),
                                in_=wp_tps[:, 0:nch * 128]),
                                reads=pc(512, 1024), writes=[wstb2])
                            ca = (coff + p0) // 128
                            dst_ap = dst[ca * 128:(ca + nch) * 128,
                                         s * 128:(s + 1) * 128].rearrange("(c p) n -> p c n", p=128)
                            S.dma(("sp", "act", "pool")[st_ctr[0] % 3],
                                  lambda e, dst_ap=dst_ap, nch=nch, gp=gp: e.dma_start(
                                out=dst_ap, in_=wp_st[:, gp, 0:nch, :]),
                                reads=[wstb2], writes=[dstb])
                        pieces.append((chain, post))
                    return pieces
            if not skip_ln:
                emit_ln_stats()
            g_ps = PA[:, 0:1024]
            u_ps = PA[:, 1024:2048]
            d_ps = PB[:]
            t_ps = PC[:, 0:512].bitcast(BF16)
            t_ps2 = PC[:, 512:1024].bitcast(BF16)
            wgb = [[B(f"f{phase}_wg{pp}_{qi}") for qi in range(3)] for pp in range(2)]
            wub = [[B(f"f{phase}_wu{pp}_{qi}") for qi in range(3)] for pp in range(2)]
            wdb = [B(f"f{phase}_wd{qi}") for qi in range(3)]
            xqTb = [B(f"f{phase}_xqT{t}") for t in range(NT)]
            hb = [B(f"f{phase}_h0"), B(f"f{phase}_h1")]
            hqb = [B(f"f{phase}_hq0"), B(f"f{phase}_hq1")]

            def load_gu(q, pp):
                off, ncols = FSL[q]
                sl0, sl1 = off // 128, (off + ncols) // 128
                for dst, bl, src_, sbl in ((wg_sb, wgb, dr["wgT"], wgT_s),
                                           (wu_sb, wub, dr["wupT"], wupT_s)):
                    src_ap = src_[:, off:off + ncols].rearrange("(k p) n -> p k n", p=128)
                    c3 = ncols // 3 // 16 * 16
                    for qi, (qe, d0, d1) in enumerate((("sp", 0, c3), ("act", c3, 2 * c3),
                                                       ("pool", 2 * c3, ncols))):
                        S.dma(qe, lambda e, dst=dst, pp=pp, src_ap=src_ap, d0=d0, d1=d1: e.dma_start(
                            out=dst[:, pp, :, d0:d1], in_=src_ap[:, :, d0:d1]),
                            reads=sbl[sl0:sl1], writes=[bl[pp][qi]])

            def load_wd(q):
                off, ncols = FSL[q]
                nch = ncols // 128
                src_ap = dr["wdT"][off:off + ncols, :].rearrange("(c p) n -> p c n", p=128)
                for qi, (qe, d0, d1) in enumerate((("sp", 0, nch // 3),
                                                   ("act", nch // 3, 2 * nch // 3),
                                                   ("pool", 2 * nch // 3, nch))):
                    if d0 < d1:
                        S.dma(qe, lambda e, src_ap=src_ap, d0=d0, d1=d1: e.dma_start(
                            out=wd_sb[:, d0:d1, :], in_=src_ap[:, d0:d1, :]),
                            reads=[wdT_s[s2][q] for s2 in range(8)], writes=[wdb[qi]])

            def load_ws(q, pp):
                load_gu(q, pp)
                load_wd(q)

            def s1_quant(t):
                qtp = qt2[0]
                xqb = B(f"f{phase}_xqb")
                S.emit("dve", lambda v, t=t, qtp=qtp: v.tensor_scalar(
                    out=qtp["xn"][:, 0:D], in0=x_res[:, t, :],
                    scalar1=ln_mv[:, t, 0:1], scalar2=ln_rstd[:, t:t + 1],
                    op0=AT.subtract, op1=AT.mult),
                    reads=[xb[t], B("ln_mv"), B("ln_rstd")],
                    writes=[qtp["bufs"]["xn"]])
                quant_emit(qtp, qtp["xn"][:, 0:D], [qtp["bufs"]["xn"]], xq_sb[:],
                           xqb, D, bf=BF_X)
                for c in range(8):
                    S.emit("te", lambda te, c=c: te.transpose(
                        t_ps2[:, c * 128:(c + 1) * 128], xq_sb[:, c * 128:(c + 1) * 128],
                        ident_bf[:]),
                        reads=[xqb, B("identb")],
                        writes=pc(512, 1024) if c in (0, 7) else ())
                S.emit("dve", lambda v, t=t: v.tensor_copy(
                    out=xqT_sb[:, t, :, :].rearrange("p k n -> p (k n)"),
                    in_=t_ps2[:, 0:1024]),
                    reads=pc(512, 1024), writes=[xqTb[t]])

            def s2_gateup(q, t, pp):
                off, ncols = FSL[q]
                nmm = (ncols + 511) // 512
                for mi in range(nmm):
                    n0 = mi * 512
                    n1 = min(ncols, n0 + 512)
                    for kc in range(8):
                        edge = (kc == 0 or kc == 7)
                        S.emit("te", lambda te, t=t, kc=kc, n0=n0, n1=n1, pp=pp: te.matmul(
                            g_ps[:, n0:n1], xqT_sb[:, t, kc, :], wg_sb[:, pp, kc, n0:n1],
                            start=(kc == 0), stop=(kc == 7)),
                            reads=[xqTb[t]] + wgb[pp], writes=pa(n0, n1) if edge else ())
                        S.emit("te", lambda te, t=t, kc=kc, n0=n0, n1=n1, pp=pp: te.matmul(
                            u_ps[:, n0:n1], xqT_sb[:, t, kc, :], wu_sb[:, pp, kc, n0:n1],
                            start=(kc == 0), stop=(kc == 7)),
                            reads=[xqTb[t]] + wub[pp], writes=pa(1024 + n0, 1024 + n1) if edge else ())

            def s3_silumul(q, t):
                off, ncols = FSL[q]
                par = t % 2
                for n0 in range(0, ncols, 512):
                    n1 = min(ncols, n0 + 512)
                    S.emit("act", lambda a, n0=n0, n1=n1, par=par: a.activation(
                        out=h_sb[:, par, n0:n1], in_=g_ps[:, n0:n1], func=AF.Silu),
                        reads=pa(n0, n1), writes=[hb[par]])
                    S.emit("dve", lambda v, n0=n0, n1=n1, par=par: v.tensor_mul(
                        out=h_sb[:, par, n0:n1], in0=h_sb[:, par, n0:n1],
                        in1=u_ps[:, n0:n1]),
                        reads=[hb[par]] + pa(1024 + n0, 1024 + n1), writes=[hb[par]])

            def stage_a(q, t):
                off, ncols = FSL[q]
                par = t % 2
                qtp = qt2[par]
                quant_emit(qtp, h_sb[:, par, 0:ncols], [hb[par]], hq_sb[:, par, 0:ncols],
                           hqb[par], ncols, bf=BF_H)

            def stage_c1(q, t, pp):
                off, ncols = FSL[q]
                nch = ncols // 128
                par = t % 2
                hqTp = B(f"f{phase}_hqT{par}")
                for c in range(nch):
                    S.emit("te", lambda te, c=c, par=par: te.transpose(
                        t_ps[:, c * 128:(c + 1) * 128], hq_sb[:, par, c * 128:(c + 1) * 128],
                        ident_bf[:]),
                        reads=[hqb[par], B("identb")],
                        writes=pc(0, 512) if c in (0, nch - 1) else ())
                S.emit("act", lambda a, nch=nch, par=par: a.activation(
                    out=hqT_sb[:, par, 0:nch, :].rearrange("p c n -> p (c n)"),
                    in_=t_ps[:, 0:nch * 128], func=AF.Copy),
                    reads=pc(0, 512), writes=[hqTp])
                for c in range(nch):
                    for mi in range(2):
                        n0 = mi * 512
                        edge = (c == 0 or c == nch - 1)
                        S.emit("te", lambda te, c=c, n0=n0, pp=pp, par=par, st=(c == 0),
                               sp_=(c == nch - 1): te.matmul(
                            d_ps[:, n0:n0 + 512], hqT_sb[:, par, c, :],
                            wd_sb[:, c, n0:n0 + 512],
                            start=st, stop=sp_),
                            reads=[hqTp] + wdb, writes=pb(n0, n0 + 512) if edge else ())

            def stage_res(q, t):
                S.emit("dve", lambda v, t=t: v.scalar_tensor_tensor(
                    out=x_res[:, t, :], in0=d_ps[:], scalar=0.5, in1=x_res[:, t, :],
                    op0=AT.mult, op1=AT.add),
                    reads=pb(0, 1024) + [xb[t]], writes=[xb[t]])

            next_job = [0]

            pend = []

            def prep_flush(all_=False):
                while len(pend) >= 2 or (all_ and pend):
                    pair = [pend.pop(0) for _ in range(min(2, len(pend)))]
                    emit_chains(*[c for c, _ in pair])
                    for _, post in pair:
                        post()

            def prep_steps(n, drain=False):
                if not with_prep:
                    return
                for _ in range(n):
                    if next_job[0] < len(slice_jobs):
                        pend.extend(prep_job(next_job[0]))
                        next_job[0] += 1
                        prep_flush()
                if drain:
                    prep_flush(all_=True)

            if with_prep:
                prep_steps(4, drain=True)
                load_gu(0, 0)
                for t in (0, 1, 2):
                    s1_quant(t)
                prep_steps(8, drain=True)
                load_wd(0)
            else:
                load_ws(0, 0)
                for t in (0, 1, 2):
                    s1_quant(t)
            for q in range(NQ):
                pp = q % 2
                s2_gateup(q, 0, pp)
                for t in range(NT):
                    s3_silumul(q, t)
                    if t + 1 < NT:
                        s2_gateup(q, t + 1, pp)
                    stage_a(q, t)
                    if t >= 1:
                        stage_c1(q, t - 1, pp)
                        stage_res(q, t - 1)
                        if store_out and q == NQ - 1:
                            store_tile(t - 1)
                    if q == 0 and t + 3 < NT:
                        s1_quant(t + 3)
                    prep_steps(1)
                    if q < NQ - 1 and t == 11:
                        if with_prep:
                            prep_steps(max(0, need_at[q + 1] - next_job[0]), drain=True)
                        load_gu(q + 1, (q + 1) % 2)
                stage_c1(q, NT - 1, pp)
                stage_res(q, NT - 1)
                if store_out and q == NQ - 1:
                    store_tile(NT - 1)
                if q < NQ - 1:
                    load_wd(q + 1)

    emit_ffn(1, with_prep=True)
    S.barrier()
    if debug:
        S.dma("act", lambda e: e.dma_start(out=dr["dbg_wgT"], in_=dr["wgT"]),
              reads=wgT_s)
        S.dma("sp", lambda e: e.dma_start(out=dr["dbg_ffn1"].rearrange("(t p) d -> p t d", p=128),
                                          in_=x_res[:]), reads=list(xb))

    with (nc.sbuf_tensor("a_wT", [128, 8, 1024], BF16) as wT_sb,
          nc.sbuf_tensor("a_K", [128, 8, 2056], BF16) as K_sb,
          nc.sbuf_tensor("a_Q", [128, 8, 2048], BF16) as Q_sb,
          nc.sbuf_tensor("a_Vs", [128, 17, 1024], BF16) as Vs_sb,
          ExitStack() as a_stk):
        emit_ln_stats()
        wTb = B("a_wT")
        Kb = B("a_K")
        Qb = B("a_Q")
        Vsb = B("a_Vs")
        a_tps = PC[:, 0:512].bitcast(BF16)

        S.emit("dve", lambda v: v.memset(K_sb[:, :, 0:3], 0.0),
               writes=[Kb])
        S.emit("dve", lambda v: v.memset(K_sb[:, :, 2051:2056], 0.0),
               writes=[Kb])
        Vsb_t = [B(f"a_Vs_{t}") for t in range(17)]
        S.emit("dve", lambda v: v.memset(Vs_sb[0:3, 0, :], 0.0), writes=[Vsb_t[0]])
        S.emit("dve", lambda v: v.memset(Vs_sb[:, 16, :], 0.0),
               writes=[Vsb_t[16]])

        def load_wT(name, wload_sb, wloadb, dst_sb=None, dstb=None):
            dst_sb = wT_sb if dst_sb is None else dst_sb
            dstb = wTb if dstb is None else dstb
            wlb2 = [B(f"{wloadb.name}_h0"), B(f"{wloadb.name}_h1")]
            for s in range(8):
                for qi, (qe, d0, d1) in enumerate((("pool", 0, 512), ("pool", 512, 1024))):
                    S.dma(qe, lambda e, name=name, s=s, d0=d0, d1=d1: e.dma_start(
                        out=wload_sb[:, d0:d1], in_=dr[name][s * 128:(s + 1) * 128, d0:d1]),
                        writes=[wlb2[qi]])
                for c in range(8):
                    S.emit("te", lambda te, c=c: te.transpose(
                        a_tps[:, c * 128:(c + 1) * 128], wload_sb[:, c * 128:(c + 1) * 128],
                        ident_bf[:]),
                        reads=[wlb2[c // 4], B("identb")],
                        writes=pc(0, 512) if c in (0, 7) else ())
                S.emit("dve", lambda v, s=s, dst_sb=dst_sb: v.tensor_copy(
                    out=dst_sb[:, :, s * 128:(s + 1) * 128],
                    in_=a_tps[:, 0:1024].rearrange("p (k n) -> p k n", n=128)),
                    reads=pc(0, 512), writes=[dstb])

        with (nc.sbuf_tensor("a_wload", [128, 1024], BF16) as wload_sb,
              nc.sbuf_tensor("a_xn2", [128, 1024], BF16) as xn2_sb,
              nc.sbuf_tensor("a_wT2", [128, 8, 1024], BF16) as wT2_sb,
              nc.sbuf_tensor("a_xn2T", [128, 8, 256], BF16) as xn2T_sb,
              nc.sbuf_tensor("a_vt", [128, 2, 1024], BF16) as vt_sb):
            wT2b = B("a_wT2")
            vtb = B("a_vt")
            wloadb = B("a_wload")
            xn2b = B("a_xn2")
            xn2Tb = B("a_xn2T")
            ps1 = PB[:, 0:512]
            ps2 = PB[:, 512:1024]

            def emit_xn2T(t, ti):
                S.emit("dve", lambda v, t=t: v.tensor_scalar(
                    out=xn2_sb[:], in0=x_res[:, t, :],
                    scalar1=ln_mv[:, t, 0:1], scalar2=ln_rstd[:, t:t + 1],
                    op0=AT.subtract, op1=AT.mult),
                    reads=[xb[t], B("ln_mv"), B("ln_rstd")], writes=[xn2b])
                for c in range(8):
                    S.emit("te", lambda te, c=c: te.transpose(
                        a_tps[:, c * 128:(c + 1) * 128], xn2_sb[:, c * 128:(c + 1) * 128],
                        ident_bf[:]),
                        reads=[xn2b, B("identb")],
                        writes=pc(0, 512) if c in (0, 7) else ())
                S.emit("dve", lambda v, ti=ti: v.tensor_copy(
                    out=xn2T_sb[:, :, ti * 128:(ti + 1) * 128],
                    in_=a_tps[:, 0:1024].rearrange("p (k n) -> p k n", n=128)),
                    reads=pc(0, 512), writes=[xn2Tb])

            def kq_pass():
                load_wT("wk", wload_sb, wloadb)
                load_wT("wq", wload_sb, wloadb, wT2_sb, wT2b)
                for g in range(8):
                    for ti in range(2):
                        emit_xn2T(g * 2 + ti, ti)
                    for sl in range(8):
                        for kc in range(8):
                            S.emit("te", lambda te, sl=sl, kc=kc: te.matmul(
                                ps1[:, 0:256], wT_sb[:, kc, sl * 128:(sl + 1) * 128],
                                xn2T_sb[:, kc, :],
                                start=(kc == 0), stop=(kc == 7)),
                                reads=[wTb, xn2Tb], writes=pb(0, 512) if kc in (0, 7) else ())
                        for kc in range(8):
                            S.emit("te", lambda te, sl=sl, kc=kc: te.matmul(
                                ps2[:, 0:256], wT2_sb[:, kc, sl * 128:(sl + 1) * 128],
                                xn2T_sb[:, kc, :],
                                start=(kc == 0), stop=(kc == 7)),
                                reads=[wT2b, xn2Tb], writes=pb(512, 1024) if kc in (0, 7) else ())
                        S.emit("dve", lambda v, sl=sl, g=g: v.tensor_copy(
                            out=K_sb[:, sl, 3 + g * 256: 3 + (g + 1) * 256], in_=ps1[:, 0:256]),
                            reads=pb(0, 512), writes=[Kb])
                        S.emit("dve", lambda v, sl=sl, g=g: v.tensor_scalar(
                            out=Q_sb[:, sl, g * 256:(g + 1) * 256], in0=ps2[:, 0:256],
                            scalar1=0.125, scalar2=None, op0=AT.mult),
                            reads=pb(512, 1024), writes=[Qb])

            kq_pass()
            load_wT("wv", wload_sb, wloadb)
            for g in range(8):
                for ti in range(2):
                    t = g * 2 + ti
                    emit_xn2T(t, ti)
                    vp = t % 2
                    vtb2 = B(f"a_vt{vp}")
                    for half in range(2):
                        ps = ps1 if half == 0 else ps2
                        bnk = pb(0, 512) if half == 0 else pb(512, 1024)
                        for kc in range(8):
                            S.emit("te", lambda te, ti=ti, kc=kc, half=half, ps=ps: te.matmul(
                                ps[:], xn2T_sb[:, kc, ti * 128:(ti + 1) * 128],
                                wT_sb[:, kc, half * 512:(half + 1) * 512],
                                start=(kc == 0), stop=(kc == 7)),
                                reads=[xn2Tb, wTb], writes=bnk if kc in (0, 7) else ())
                        S.emit("dve", lambda v, half=half, ps=ps, vp=vp: v.tensor_copy(
                            out=vt_sb[:, vp, half * 512:(half + 1) * 512], in_=ps[:]),
                            reads=bnk, writes=[vtb2])
                    S.dma("act", lambda e, t=t, vp=vp: e.dma_start(out=Vs_sb[3:128, t, :],
                                                                   in_=vt_sb[0:125, vp, :]),
                          reads=[vtb2], writes=[Vsb_t[t]])
                    S.dma("pool", lambda e, t=t, vp=vp: e.dma_start(out=Vs_sb[0:3, t + 1, :],
                                                                    in_=vt_sb[125:128, vp, :]),
                          reads=[vtb2], writes=[Vsb_t[t + 1]])

        S.barrier()
        with (nc.sbuf_tensor("a_sc", [128, H, 144], BF16) as sc_sb,
              nc.sbuf_tensor("a_connd", [128, H, 134], BF16) as conn_d,
              nc.sbuf_tensor("a_bo", [1, D], BF16) as bo_bf,
              nc.sbuf_tensor("a_m", [128, H], F32) as m_sb,
              nc.sbuf_tensor("a_den", [128, H], F32) as den_sb,
              nc.sbuf_tensor("a_den2", [128, H], F32) as den2_sb,
              nc.sbuf_tensor("a_fwT1", [128, H, 128], BF16) as fwT1_sb,
              nc.sbuf_tensor("a_fwT2", [16, H, 128], BF16) as fwT2_sb,
              nc.sbuf_tensor("a_O", [128, 8, 128], BF16) as O_sb,
              nc.sbuf_tensor("a_wl2", [128, 1024], BF16) as wl2_sb):
            load_wT("wo", wl2_sb, B("a_wl2"))
            S.dma("sp", lambda e: e.dma_start(out=conn_d[:].rearrange("p h w -> p (h w)"),
                                              in_=dr["connd"]), writes=[B("conn_d")])
            S.dma("pool", lambda e: e.dma_start(out=bo_bf[:],
                                                in_=dr["bo"].rearrange("(o d) -> o d", o=1)),
                  writes=[B("bo")])
            scb = [B("a_sc0"), B("a_sc1")]
            mb = [B("a_m0"), B("a_m1")]
            denb = [B("a_den0"), B("a_den1")]
            den2b = [B("a_den20"), B("a_den21")]
            fwT1b = [B("a_fwT10"), B("a_fwT11")]
            fwT2b = [B("a_fwT20"), B("a_fwT21")]
            Ob = B("a_O")
            conn_b = B("conn_d")
            S.emit("dve", lambda v: v.memset(sc_sb[:, :, 134:144], 0.0),
                   writes=[scb[0], scb[1]])
            sc_ps = PA[:]
            sc_heads = PA[:].rearrange("p (x j) -> p x j", j=256)[:, 0:8, 0:134]
            wo_ps = PC[:, 512:1024]

            def scores(t, hh):
                for h8 in range(8):
                    h = hh * 8 + h8
                    off = h8 * 256
                    hb2 = (h % 2) * 64
                    S.emit("te", lambda te, h=h, off=off, hb2=hb2, t=t: te.matmul(
                        sc_ps[:, off:off + 134],
                        Q_sb[hb2:hb2 + 64, h // 2, t * 128:(t + 1) * 128],
                        K_sb[hb2:hb2 + 64, h // 2, t * 128:t * 128 + 134],
                        start=True, stop=True),
                        reads=[Qb, Kb], writes=pa(off, off + 134))

            def softmax_pre(t, hh):
                sl = slice(hh * 8, (hh + 1) * 8)
                S.emit("dve", lambda v, sl=sl: v.reduce_max(
                    out=m_sb[:, sl], in_=sc_heads, axis=AX.X),
                    reads=pa(0, 2048), writes=[mb[hh]])
                mbc = m_sb[:, sl].unsqueeze(-1).broadcast_to([128, 8, 134])
                S.emit("dve", lambda v, sl=sl, mbc=mbc: v.tensor_tensor(
                    out=sc_sb[:, sl, 0:134], in0=sc_heads, in1=mbc, op=AT.subtract),
                    reads=pa(0, 2048) + [mb[hh]], writes=[scb[hh]])
                S.emit("act", lambda a, sl=sl: a.activation(
                    out=sc_sb[:, sl, 0:134], in_=sc_sb[:, sl, 0:134], func=AF.Exp),
                    reads=[scb[hh]], writes=[scb[hh]])

            def softmax_post(t, hh):
                sl = slice(hh * 8, (hh + 1) * 8)
                S.emit("dve", lambda v, sl=sl: v.reduce_sum(
                    out=den2_sb[:, sl], in_=sc_sb[:, sl, 0:134], axis=AX.X),
                    reads=[scb[hh]], writes=[den2b[hh]])
                S.emit("dve", lambda v, sl=sl: v.tensor_tensor(
                    out=sc_sb[:, sl, 0:134], in0=sc_sb[:, sl, 0:134],
                    in1=conn_d[:, sl, :], op=AT.mult),
                    reads=[scb[hh], conn_b], writes=[scb[hh]])
                S.emit("dve", lambda v, sl=sl: v.reduce_sum(
                    out=den_sb[:, sl], in_=sc_sb[:, sl, 0:134], axis=AX.X),
                    reads=[scb[hh]], writes=[denb[hh]])
                S.emit("dve", lambda v, sl=sl: v.scalar_tensor_tensor(
                    out=den_sb[:, sl], in0=den2_sb[:, sl], scalar=1e-9,
                    in1=den_sb[:, sl], op0=AT.mult, op1=AT.add),
                    reads=[den2b[hh], denb[hh]], writes=[denb[hh]])
                S.emit("dve", lambda v, sl=sl: v.reciprocal(
                    out=den_sb[:, sl], in_=den_sb[:, sl]),
                    reads=[denb[hh]], writes=[denb[hh]])
                rbc = den_sb[:, sl].unsqueeze(-1).broadcast_to([128, 8, 134])
                S.emit("dve", lambda v, sl=sl, rbc=rbc: v.tensor_tensor(
                    out=sc_sb[:, sl, 0:134], in0=sc_sb[:, sl, 0:134], in1=rbc, op=AT.mult),
                    reads=[scb[hh], denb[hh]], writes=[scb[hh]])

            def fwtrans(t, hh):
                for hg in range(2):
                    for hi in range(4):
                        h = hh * 8 + hg * 4 + hi
                        S.emit("te", lambda te, h=h, hi=hi: te.transpose(
                            a_tps[:, hi * 128:(hi + 1) * 128], sc_sb[:, h, 0:128], ident_bf[:]),
                            reads=[scb[hh], B("identb")], writes=pc(0, 512))
                        S.emit("te", lambda te, h=h, hi=hi: te.transpose(
                            a_tps[0:16, 512 + hi * 128: 512 + (hi + 1) * 128],
                            sc_sb[:, h, 128:144], ident_bf[:]),
                            reads=[scb[hh], B("identb")], writes=pc(0, 512))
                    h0 = hh * 8 + hg * 4
                    S.emit("dve", lambda v, h0=h0: v.tensor_copy(
                        out=fwT1_sb[:, h0:h0 + 4, :].rearrange("p h n -> p (h n)"),
                        in_=a_tps[:, 0:512]),
                        reads=pc(0, 512), writes=[fwT1b[hh]])
                    S.emit("dve", lambda v, h0=h0: v.tensor_copy(
                        out=fwT2_sb[:, h0:h0 + 4, :].rearrange("p h n -> p (h n)"),
                        in_=a_tps[0:16, 512:1024]),
                        reads=pc(0, 512), writes=[fwT2b[hh]])

            def av(t, hh):
                bnk = pb(0, 512) if hh == 0 else pb(512, 1024)
                for h8 in range(8):
                    h = hh * 8 + h8
                    hb2 = (h % 2) * 64
                    S.emit("te", lambda te, h=h, hb2=hb2, t=t: te.matmul(
                        av_ps[hb2:hb2 + 64, (h // 2) * 128:(h // 2) * 128 + 128],
                        Vs_sb[:, t, h * 64:(h + 1) * 64], fwT1_sb[:, h, :],
                        start=True, stop=False, tile_position=(0, hb2)),
                        reads=[Vsb_t[t], fwT1b[hh]], writes=pb(0, 1024))
                    S.emit("te", lambda te, h=h, hb2=hb2, t=t: te.matmul(
                        av_ps[hb2:hb2 + 64, (h // 2) * 128:(h // 2) * 128 + 128],
                        Vs_sb[0:16, t + 1, h * 64:(h + 1) * 64], fwT2_sb[:, h, :],
                        start=False, stop=True, tile_position=(0, hb2)),
                        reads=[Vsb_t[t + 1], fwT2b[hh]], writes=pb(0, 1024))

            def wo_block(t):
                S.emit("dve", lambda v: v.tensor_copy(
                    out=O_sb[:].rearrange("p k n -> p (k n)"), in_=av_ps[:]),
                    reads=pb(0, 1024), writes=[Ob])
                wo_ps2 = PC[:, 512:1024]
                for mi in range(2):
                    S.emit("te", lambda te, mi=mi: te.matmul(
                        wo_ps2[:], ones_bf[:],
                        bo_bf[:, mi * 512:(mi + 1) * 512], start=True, stop=False),
                        reads=[B("ones"), B("bo")], writes=pc(512, 1024))
                    for kc in range(8):
                        S.emit("te", lambda te, kc=kc, mi=mi: te.matmul(
                            wo_ps2[:], O_sb[:, kc, :],
                            wT_sb[:, kc, mi * 512:(mi + 1) * 512],
                            start=False, stop=(kc == 7)),
                            reads=[Ob, wTb],
                            writes=pc(512, 1024) if kc == 7 else ())
                    S.emit("dve", lambda v, t=t, mi=mi: v.tensor_add(
                        out=x_res[:, t, mi * 512:(mi + 1) * 512], in0=wo_ps2[:],
                        in1=x_res[:, t, mi * 512:(mi + 1) * 512]),
                        reads=pc(512, 1024) + [xb[t]], writes=[xb[t]])

            av_ps = PB[:]
            units = [(t, hh) for t in range(NT) for hh in range(2)]
            scores(*units[0])
            for ui, (t, hh) in enumerate(units):
                softmax_pre(t, hh)
                if ui + 1 < len(units):
                    scores(*units[ui + 1])
                if ui >= 1:
                    pt, ph = units[ui - 1]
                    softmax_post(pt, ph)
                    fwtrans(pt, ph)
                    av(pt, ph)
                    if ph == 1:
                        wo_block(pt)
                        emit_ln_tile(pt)
            pt, ph = units[-1]
            softmax_post(pt, ph)
            fwtrans(pt, ph)
            av(pt, ph)
            wo_block(pt)
            emit_ln_tile(pt)
            emit_ln_final()
    S.barrier()
    if debug:
        S.dma("sp", lambda e: e.dma_start(out=dr["dbg_attn"].rearrange("(t p) d -> p t d", p=128),
                                          in_=x_res[:]), reads=list(xb))

    emit_ffn(2, store_out=True, skip_ln=True)
    S.barrier()

    S.barrier()

    S.replay()
    ctx.close()
    return nc


_NC_CACHE = {}


def _get_nc(debug=False):
    if debug not in _NC_CACHE:
        _NC_CACHE[debug] = build_program(debug=debug)
    return _NC_CACHE[debug]


def kernel(L0_attn_x, L0_attn_wq, L0_attn_wk, L0_attn_wv, L0_attn_wo, L0_attn_bo,
           L0_attn_cw1, L0_attn_cb1, L0_attn_cw2, L0_attn_cb2, L0_attn_cw3, L0_attn_cb3,
           L0_ffn_w_gate, L0_ffn_w_up, L0_ffn_w_down, _debug=False):
    x = np.ascontiguousarray(np.asarray(L0_attn_x, dtype=np.float32))
    common = {
        "wq": np.ascontiguousarray(np.asarray(L0_attn_wq, np.float32)),
        "wk": np.ascontiguousarray(np.asarray(L0_attn_wk, np.float32)),
        "wv": np.ascontiguousarray(np.asarray(L0_attn_wv, np.float32)),
        "wo": np.ascontiguousarray(np.asarray(L0_attn_wo, np.float32)),
        "bo": np.ascontiguousarray(np.asarray(L0_attn_bo, np.float32)),
        "cw1": np.ascontiguousarray(np.asarray(L0_attn_cw1, np.float32).reshape(H, 32)),
        "cb1": np.ascontiguousarray(np.asarray(L0_attn_cb1, np.float32)),
        "cw2": np.ascontiguousarray(np.asarray(L0_attn_cw2, np.float32)),
        "cb2": np.ascontiguousarray(np.asarray(L0_attn_cb2, np.float32)),
        "cw3": np.ascontiguousarray(np.asarray(L0_attn_cw3, np.float32).reshape(H, 32)),
        "cb3": np.ascontiguousarray(np.asarray(L0_attn_cb3, np.float32).reshape(H)),
        "w_gate": np.ascontiguousarray(np.asarray(L0_ffn_w_gate, np.float32)),
        "w_up": np.ascontiguousarray(np.asarray(L0_ffn_w_up, np.float32)),
        "w_down": np.ascontiguousarray(np.asarray(L0_ffn_w_down, np.float32)),
        "ident_bf": np.eye(128, dtype=ml_dtypes.bfloat16),
        "ident_f": np.eye(128, dtype=np.float32),
    }
    nc = _get_nc(debug=_debug)
    in_maps = []
    for c in range(8):
        m = dict(common)
        m["x"] = np.ascontiguousarray(x[c])
        in_maps.append(m)
    res = run_bass_kernel_spmd(nc, in_maps, core_ids=list(range(8)))
    out = np.stack([np.asarray(res.results[c]["out"]) for c in range(8)], axis=0)
    if _debug:
        dbg = {
            "ffn1": np.stack([np.asarray(res.results[c]["dbg_ffn1"]) for c in range(8)]),
            "wgT": np.asarray(res.results[0]["dbg_wgT"]),
            "attn": np.stack([np.asarray(res.results[c]["dbg_attn"]) for c in range(8)]),
        }
        return out, dbg
    return out


# version 72
# speedup vs baseline: 1.0106x; 1.0106x over previous
import numpy as np
import ml_dtypes
import concourse.bass as bass
import concourse.mybir as mybir
from concourse.bass_utils import run_bass_kernel_spmd
from contextlib import ExitStack

F32 = mybir.dt.float32
BF16 = mybir.dt.bfloat16
AT = mybir.AluOpType
AF = mybir.ActivationFunctionType
AX = mybir.AxisListType

D = 1024
NT = 16
FF = 2816
H = 16
WIN = 7
FSL = [(0, 256), (256, 768), (1024, 768), (1792, 768), (2560, 256)]
NQ = len(FSL)
BF_H = True
BF_W = True
BF_X = False
TE_IMPLICIT = False
VELT = 4194305.0
FIXC = 6291456.0


class Buf:
    __slots__ = ("name", "writer", "readers")

    def __init__(self, name):
        self.name = name
        self.writer = None
        self.readers = {}


class Tok:
    __slots__ = ("eng", "seq", "needed", "val")

    def __init__(self, eng, seq):
        self.eng = eng
        self.seq = seq
        self.needed = False
        self.val = None


class Sched:
    ENGS = ("te", "dve", "act", "pool", "sp")

    def __init__(self, nc, ctx):
        self.nc = nc
        self.eng_obj = {"te": nc.tensor, "dve": nc.vector, "act": nc.scalar,
                        "pool": nc.gpsimd, "sp": nc.sync}
        self.items = {e: [] for e in self.ENGS}
        self.sem = {e: ctx.enter_context(nc.semaphore(f"sem_{e}")) for e in self.ENGS}
        self.NDQ = 8
        self.dsem = {e: [ctx.enter_context(nc.semaphore(f"dsem_{e}{i}")) for i in range(self.NDQ)]
                     for e in ("sp", "act", "pool")}
        self.duse = {e: [0] * self.NDQ for e in ("sp", "act", "pool")}
        self.didx = {e: 0 for e in ("sp", "act", "pool")}

    def _collect(self, eng, reads, writes):
        deps = []
        for b in reads:
            if b.writer is not None:
                deps.append(b.writer)
        for b in writes:
            if b.writer is not None:
                deps.append(b.writer)
            deps.extend(b.readers.values())
        out = []
        seen = set()
        for d in deps:
            if isinstance(d, Tok):
                if TE_IMPLICIT and d.eng == "te" and eng == "te":
                    continue
                k = (d.eng, d.seq)
                if k in seen:
                    continue
                seen.add(k)
                d.needed = True
                out.append(d)
            else:
                k = ("dma", id(d[0]), d[1])
                if k in seen:
                    continue
                seen.add(k)
                out.append(d)
        return out

    def emit(self, eng, fn, reads=(), writes=()):
        deps = self._collect(eng, reads, writes)
        tok = Tok(eng, len(self.items[eng]))
        self.items[eng].append(("op", fn, deps, tok))
        for b in writes:
            b.writer = tok
            b.readers = {}
        for b in reads:
            b.readers[eng] = tok
        return tok

    def dma(self, eng, fn, reads=(), writes=()):
        assert eng in ("sp", "act", "pool")
        deps = self._collect(eng, reads, writes)
        i = self.didx[eng]
        self.didx[eng] = (i + 1) % self.NDQ
        dsem = self.dsem[eng][i]
        if self.duse[eng][i] > 0:
            deps.append((dsem, 16 * self.duse[eng][i]))
        self.duse[eng][i] += 1
        dtok = (dsem, 16 * self.duse[eng][i])
        self.items[eng].append(("dma", fn, deps, dtok))
        for b in writes:
            b.writer = dtok
            b.readers = {}
        for b in reads:
            b.readers[("q", id(dtok[0]))] = dtok
        return dtok

    def barrier(self):
        toks = []
        for o in self.ENGS:
            for kind, fn, deps, tk in reversed(self.items[o]):
                if kind == "op":
                    tk.needed = True
                    toks.append(tk)
                    break
        for qe in ("sp", "act", "pool"):
            for i in range(self.NDQ):
                if self.duse[qe][i] > 0:
                    toks.append((self.dsem[qe][i], 16 * self.duse[qe][i]))
        for e in self.ENGS:
            tok = Tok(e, len(self.items[e]))
            self.items[e].append(("barrier", None, list(toks), tok))

    def replay(self):
        nc = self.nc
        for e in self.ENGS:
            cnt = 0
            for kind, fn, deps, tok in self.items[e]:
                if isinstance(tok, Tok) and tok.needed:
                    cnt += 1
                    tok.val = cnt

        def run_engine(eng):
            obj = self.eng_obj[eng]
            waited = {}

            def do_wait(d):
                if isinstance(d, Tok):
                    if d.eng == eng and isinstance(self.items[eng][0], tuple):
                        pass
                    sem, val, key = self.sem[d.eng], d.val, f"e:{d.eng}"
                    assert val is not None, f"unresolved tok {d.eng}:{d.seq}"
                else:
                    sem, val = d
                    key = f"d:{id(sem)}"
                if waited.get(key, 0) >= val:
                    return
                waited[key] = val
                obj.wait_ge(sem, val)

            for kind, fn, deps, tok in self.items[eng]:
                for d in deps:
                    do_wait(d)
                if kind == "op":
                    inst = fn(obj)
                    if tok.needed:
                        inst.then_inc(self.sem[eng], 1)
                elif kind == "dma":
                    fn(obj).then_inc(tok[0], 16)

        with nc.Block() as block:
            for dec, eng in ((block.tensor, "te"), (block.vector, "dve"),
                             (block.scalar, "act"), (block.gpsimd, "pool"),
                             (block.sync, "sp")):
                def run(_e, eng=eng):
                    run_engine(eng)
                dec(run)


def build_program(debug=False):
    nc = bass.Bass("TRN2", target_bir_lowering=False)
    ctx = ExitStack()
    S = Sched(nc, ctx)
    sb = ctx.enter_context

    dr = {}
    dr["x"] = nc.dram_tensor("x", [2048, D], F32, kind="ExternalInput").ap()
    for w in ("wq", "wk", "wv", "wo"):
        dr[w] = nc.dram_tensor(w, [D, D], F32, kind="ExternalInput").ap()
    dr["bo"] = nc.dram_tensor("bo", [D], F32, kind="ExternalInput").ap()
    dr["cw1"] = nc.dram_tensor("cw1", [H, 32], F32, kind="ExternalInput").ap()
    dr["cb1"] = nc.dram_tensor("cb1", [H, 32], F32, kind="ExternalInput").ap()
    dr["cw2"] = nc.dram_tensor("cw2", [H, 32, 32], F32, kind="ExternalInput").ap()
    dr["cb2"] = nc.dram_tensor("cb2", [H, 32], F32, kind="ExternalInput").ap()
    dr["cw3"] = nc.dram_tensor("cw3", [H, 32], F32, kind="ExternalInput").ap()
    dr["cb3"] = nc.dram_tensor("cb3", [H], F32, kind="ExternalInput").ap()
    dr["w_gate"] = nc.dram_tensor("w_gate", [FF, D], F32, kind="ExternalInput").ap()
    dr["w_up"] = nc.dram_tensor("w_up", [FF, D], F32, kind="ExternalInput").ap()
    dr["w_down"] = nc.dram_tensor("w_down", [D, FF], F32, kind="ExternalInput").ap()
    dr["ident_bf"] = nc.dram_tensor("ident_bf", [128, 128], BF16, kind="ExternalInput").ap()
    dr["ident_f"] = nc.dram_tensor("ident_f", [128, 128], F32, kind="ExternalInput").ap()
    dr["out"] = nc.dram_tensor("out", [2048, D], F32, kind="ExternalOutput").ap()
    if debug:
        dr["dbg_ffn1"] = nc.dram_tensor("dbg_ffn1", [2048, D], F32, kind="ExternalOutput").ap()
        dr["dbg_attn"] = nc.dram_tensor("dbg_attn", [2048, D], F32, kind="ExternalOutput").ap()
        dr["dbg_wgT"] = nc.dram_tensor("dbg_wgT", [D, FF], BF16, kind="ExternalOutput").ap()
    dr["wgT"] = nc.dram_tensor("wgT", [D, FF], BF16, kind="Internal").ap()
    dr["wupT"] = nc.dram_tensor("wupT", [D, FF], BF16, kind="Internal").ap()
    dr["wdT"] = nc.dram_tensor("wdT", [FF, D], BF16, kind="Internal").ap()
    dr["connd"] = nc.dram_tensor("connd", [128, H * 134], BF16, kind="Internal").ap()

    x_res = sb(nc.sbuf_tensor("x_res", [128, NT, D], F32))
    ident_bf = sb(nc.sbuf_tensor("identb", [128, 128], BF16))
    ones_bf = sb(nc.sbuf_tensor("ones_bf", [1, 128], BF16))
    ln_mv = sb(nc.sbuf_tensor("ln_mv", [128, NT, 2], F32))
    ln_bn = sb(nc.sbuf_tensor("ln_bn", [128, 2, 12], F32))
    ln_var = sb(nc.sbuf_tensor("ln_var", [128, NT], F32))
    ln_rstd = sb(nc.sbuf_tensor("ln_rstd", [128, NT], F32))
    qconst = sb(nc.sbuf_tensor("qconst", [128, 1], F32))
    qconst2 = sb(nc.sbuf_tensor("qconst2", [128, 1], F32))

    PA = sb(nc.psum_tensor("PA", [128, 2048], F32))
    PB = sb(nc.psum_tensor("PB", [128, 1024], F32))
    PC = sb(nc.psum_tensor("PC", [128, 1024], F32))
    PBANK = [Buf(f"pbank{i}") for i in range(8)]

    def pa(lo_el, hi_el):
        return PBANK[lo_el // 512: (hi_el + 511) // 512]

    def pb(lo_el, hi_el):
        return PBANK[4 + lo_el // 512: 4 + (hi_el + 511) // 512]

    def pc(lo_el, hi_el):
        return PBANK[6 + lo_el // 512: 6 + (hi_el + 511) // 512]

    bufs = {}

    def B(name):
        if name not in bufs:
            bufs[name] = Buf(name)
        return bufs[name]

    xb = [B(f"x_{t}") for t in range(NT)]

    for xi, xq_eng in enumerate(("sp", "act", "pool", "sp")):
        S.dma(xq_eng, lambda e, xi=xi: e.dma_start(
            out=x_res[:, xi * 4:(xi + 1) * 4, :],
            in_=dr["x"].rearrange("(t p) d -> p t d", p=128)[:, xi * 4:(xi + 1) * 4, :]),
            writes=xb[xi * 4:(xi + 1) * 4])
    S.dma("sp", lambda e: e.dma_start(out=ident_bf[:], in_=dr["ident_bf"]), writes=[B("identb")])
    S.emit("dve", lambda v: v.memset(ones_bf[:], 1.0), writes=[B("ones")])
    S.emit("dve", lambda v: v.memset(qconst[:], 65.0), writes=[B("qconst")])
    S.emit("dve", lambda v: v.memset(qconst2[:], VELT), writes=[B("qconst")])

    def make_qt(stk, tag, width, with_t1=True):
        qt = {}
        names = ("t0", "t1", "t2", "xn") if with_t1 else ("t0", "t2", "xn")
        for n in names:
            qt[n] = stk.enter_context(nc.sbuf_tensor(f"{tag}_{n}", [128, width], F32))
        for n in ("bmax", "scale", "inv"):
            qt[n] = stk.enter_context(nc.sbuf_tensor(f"{tag}_{n}", [128, width // 16], F32))
        qt["bufs"] = {n: B(f"{tag}_{n}") for n in names + ("bmax", "scale", "inv")}
        qt["c65"] = qconst
        qt["cVELT"] = qconst2
        return qt

    def quant_chain_ops(qt, in_ap, in_bufs, out_ap, out_buf, c0, c1, tag, bf=False):
        ops = []
        Q = qt["bufs"]

        def key(n):
            return Q[n] if tag == 0 else B(f"{Q[n].name}_h{tag}")
        b0, b1 = c0 // 16, c1 // 16
        nblk = b1 - b0
        ncols = c1 - c0
        bmax = qt["bmax"][:, b0:b1]
        scale = qt["scale"][:, b0:b1]
        inv = qt["inv"][:, b0:b1]
        if bf:
            t0 = qt["t0"].ap().bitcast(BF16)[:, c0:c1]
            t1 = qt["t1"].ap().bitcast(BF16)[:, c0:c1]
            t2 = qt["t2"].ap().bitcast(BF16)[:, c0:c1]
            xn = qt["xn"].ap().bitcast(BF16)[:, c0:c1]
            m_u8 = qt["xn"].ap().bitcast(mybir.dt.uint8)[:, c0 * 2:c0 * 2 + ncols]
            velt_scale = qt["c65"]
            fixc = 96.0
        else:
            t0 = qt["t0"][:, c0:c1]
            t1 = qt["t1"][:, c0:c1]
            t2 = qt["t2"][:, c0:c1]
            xn = qt["xn"][:, c0:c1]
            m_u8 = qt["xn"].ap().bitcast(mybir.dt.uint8)[:, c0 * 4:c0 * 4 + ncols]
            velt_scale = qt["cVELT"]
            fixc = FIXC
        inap = in_ap[:, c0:c1]
        outap = out_ap[:, c0:c1]
        ing = inap.rearrange("p (b s) -> p b s", s=16)
        invb = inv.unsqueeze(-1).broadcast_to([128, nblk, 16])
        scb = scale.unsqueeze(-1).broadcast_to([128, nblk, 16])
        xng = xn.rearrange("p (b s) -> p b s", s=16)
        outg = outap.rearrange("p (b s) -> p b s", s=16)
        qg = t0.rearrange("p (b s) -> p b s", s=16)
        E = ops.append
        E(("dve", lambda v: v.reduce_max(out=bmax, in_=ing, axis=AX.X,
                                         apply_absolute_value=True),
           list(in_bufs) + base("bmax"), [key("bmax")]))
        E(("dve", lambda v: v.tensor_scalar(out=scale, in0=bmax, scalar1=1.0 / 6.0,
                                            scalar2=1e-12, op0=AT.mult, op1=AT.max),
           [key("bmax")] + base("bmax") + base("scale"), [key("scale")]))
        E(("dve", lambda v: v.reciprocal(out=inv, in_=scale),
           [key("scale")] + base("scale") + base("inv"), [key("inv")]))
        E(("dve", lambda v: v.tensor_tensor(out=xng, in0=ing, in1=invb, op=AT.mult),
           list(in_bufs) + [key("inv")], [key("xn")]))
        E(("act", lambda a: a.activation(out=t0, in_=xn, func=AF.Copy,
                                         scale=velt_scale[:, 0:1]),
           [key("xn"), B("qconst")], [key("t0")]))
        E(("dve", lambda v: v.tensor_sub(out=t1, in0=t0, in1=xn),
           [key("t0"), key("xn")], [key("t1")]))
        E(("dve", lambda v: v.tensor_sub(out=t2, in0=t0, in1=t1),
           [key("t0"), key("t1")], [key("t2")]))
        if bf:
            E(("dve", lambda v: v.tensor_single_scalar(out=t0, in_=xn, scalar=fixc, op=AT.add),
               [key("xn")], [key("t0")]))
            E(("dve", lambda v: v.tensor_single_scalar(out=t0, in_=t0, scalar=fixc,
                                                       op=AT.subtract),
               [key("t0")], [key("t0")]))
        else:
            E(("dve", lambda v: v.tensor_scalar(out=t0, in0=xn, scalar1=fixc, scalar2=fixc,
                                                op0=AT.add, op1=AT.subtract),
               [key("xn")], [key("t0")]))
        E(("act", lambda a: a.activation(out=t1, in_=xn, func=AF.Abs),
           [key("xn")], [key("t1")]))
        E(("dve", lambda v: v.tensor_single_scalar(out=m_u8, in_=t1, scalar=1.0, op=AT.is_gt),
           [key("t1")], [key("xn")]))
        E(("dve", lambda v: v.copy_predicated(out=t0, mask=m_u8, data=t2),
           [key("xn"), key("t2"), key("t0")], [key("t0")]))
        E(("dve", lambda v: v.tensor_tensor(out=outg, in0=qg, in1=scb, op=AT.mult),
           [key("t0"), key("scale")], [out_buf]))
        return ops

    def qt_all(qt, n):
        nm = qt["bufs"][n].name
        return [qt["bufs"][n], B(f"{nm}_h1"), B(f"{nm}_h2")]

    def quant_chain_ops_v2(qt, in_ap, in_bufs, out_ap, out_buf, c0, c1, tag, bf=True,
                           pool_heavy=False):
        ops = []
        Q = qt["bufs"]

        def key(n):
            return Q[n] if tag == 0 else B(f"{Q[n].name}_h{tag}")

        def base(n):
            return []
        b0, b1 = c0 // 16, c1 // 16
        nblk = b1 - b0
        bmax = qt["bmax"][:, b0:b1]
        scale = qt["scale"][:, b0:b1]
        inv = qt["inv"][:, b0:b1]
        if bf:
            co2, W = 2 * c0, c1 - c0
            t0i = qt["t0"].ap().bitcast(mybir.dt.int16)[:, co2:co2 + W]
            t0p = qt["t0"].ap().bitcast(BF16)[:, co2:co2 + W]
            t2i = qt["t2"].ap().bitcast(mybir.dt.int16)[:, co2:co2 + W]
            t2x = qt["t2"].ap().bitcast(BF16)[:, co2:co2 + W]
            mask = 32640
        else:
            t0i = qt["t0"].ap().bitcast(mybir.dt.int32)[:, c0:c1]
            t0p = qt["t0"][:, c0:c1]
            t2i = qt["t2"].ap().bitcast(mybir.dt.int32)[:, c0:c1]
            t2x = qt["t2"][:, c0:c1]
            mask = 2139095040
        t0f = qt["t0"][:, c0:c1]
        csf = qt["xn"][:, c0:c1]
        inap = in_ap[:, c0:c1]
        outap = out_ap[:, c0:c1]
        ing = inap.rearrange("p (b s) -> p b s", s=16)
        invb = inv.unsqueeze(-1).broadcast_to([128, nblk, 16])
        scb = scale.unsqueeze(-1).broadcast_to([128, nblk, 16])
        xg = t2x.rearrange("p (b s) -> p b s", s=16)
        outg = outap.rearrange("p (b s) -> p b s", s=16)
        RC = 12582912.0
        E = ops.append
        E(("dve", lambda v: v.reduce_max(out=bmax, in_=ing, axis=AX.X,
                                         apply_absolute_value=True),
           list(in_bufs) + base("bmax"), [key("bmax")]))
        E(("dve", lambda v: v.tensor_scalar(out=scale, in0=bmax, scalar1=1.0 / 6.0,
                                            scalar2=1e-12, op0=AT.mult, op1=AT.max),
           [key("bmax")] + base("bmax") + base("scale"), [key("scale")]))
        E(("dve", lambda v: v.reciprocal(out=inv, in_=scale),
           [key("scale")] + base("scale") + base("inv"), [key("inv")]))
        E(("pool", lambda g: g.tensor_tensor(out=xg, in0=ing, in1=invb, op=AT.mult),
           list(in_bufs) + [key("inv")] + base("inv") + base("t2"), [key("t2")]))
        E(("dve", lambda v: v.tensor_single_scalar(out=t0i, in_=t2i, scalar=mask,
                                                   op=AT.bitwise_and),
           [key("t2")] + base("t2") + base("t0"), [key("t0")]))
        E(("dve", lambda v: v.tensor_scalar(out=csf, in0=t0p, scalar1=RC * 0.5,
                                            scalar2=RC * 0.5, op0=AT.mult, op1=AT.max),
           [key("t0")] + base("t0") + base("xn"), [key("xn")]))
        E(("pool", lambda g: g.tensor_tensor(out=t0f, in0=t2x, in1=csf, op=AT.add),
           [key("t2"), key("xn")] + base("t2") + base("xn") + base("t0"), [key("t0")]))
        e_sub = "pool" if pool_heavy else "dve"
        E((e_sub, lambda v: v.tensor_tensor(out=t2x, in0=t0f, in1=csf, op=AT.subtract),
           [key("t0"), key("xn")] + base("t0") + base("xn") + base("t2"), [key("t2")]))
        E((e_sub, lambda v: v.tensor_tensor(out=outg, in0=xg, in1=scb, op=AT.mult),
           [key("t2"), key("scale")] + base("t2") + base("scale"), [out_buf]))
        return ops

    def emit_chains(*chains):
        chains = [list(c) for c in chains]
        idx = [0] * len(chains)
        while True:
            done = True
            for ci, c in enumerate(chains):
                if idx[ci] < len(c):
                    eng, fn, rd, wr = c[idx[ci]]
                    S.emit(eng, fn, reads=rd, writes=wr)
                    idx[ci] += 1
                    done = False
            if done:
                break

    def quant_emit(qt, in_ap, in_bufs, out_ap, out_buf, ncols, nsplit=2, bf=False,
                   pool_heavy=False):
        nb = ncols // 16
        half_b = (nb // (2 * nsplit)) * 2 if nsplit > 1 else nb
        cuts = [0]
        for i in range(nsplit - 1):
            cuts.append(min(ncols, (i + 1) * (nb // nsplit) * 16))
        cuts.append(ncols)
        chains = []
        for i in range(nsplit):
            if cuts[i] < cuts[i + 1]:
                chains.append(quant_chain_ops_v2(qt, in_ap, in_bufs, out_ap, out_buf,
                                                 cuts[i], cuts[i + 1], i, bf=bf,
                                                 pool_heavy=pool_heavy))
        emit_chains(*chains)

    with (nc.sbuf_tensor("identf", [128, 128], F32) as ident_f,
          nc.sbuf_tensor("c_lhs1", [2, 16], F32) as c_lhs1,
          nc.sbuf_tensor("c_ones7", [1, 7], F32) as c_ones7,
          nc.sbuf_tensor("c_onesf", [1, 128], F32) as c_onesf,
          nc.sbuf_tensor("c_rhs1", [2, 512], F32) as c_rhs1,
          nc.sbuf_tensor("c_h1", [7, 512], F32) as c_h1,
          nc.sbuf_tensor("c_h1T", [32, H, 7], F32) as c_h1T,
          nc.sbuf_tensor("c_w2", [32, H, 32], F32) as c_w2,
          nc.sbuf_tensor("c_w2T", [32, H, 32], F32) as c_w2T,
          nc.sbuf_tensor("c_cb2", [1, 512], F32) as c_cb2,
          nc.sbuf_tensor("c_h2", [32, H, 7], F32) as c_h2,
          nc.sbuf_tensor("c_w3T", [32, H], F32) as c_w3T,
          nc.sbuf_tensor("c_cb3", [1, H], F32) as c_cb3,
          ExitStack() as p_stk):
        c_cwt = p_stk.enter_context(nc.sbuf_tensor("c_cwt", [1, H, 7], F32))
        c_max = p_stk.enter_context(nc.sbuf_tensor("c_max", [1, H], F32))
        c_conn = p_stk.enter_context(nc.sbuf_tensor("c_conn", [1, H, 7], F32))
        c_connb = p_stk.enter_context(nc.sbuf_tensor("c_connb", [128, H, 7], F32))
        c_dj = p_stk.enter_context(nc.sbuf_tensor("c_dj", [128, 134], F32))
        c_mask = p_stk.enter_context(nc.sbuf_tensor("c_mask", [128, 134], F32))
        conn_d = p_stk.enter_context(nc.sbuf_tensor("conn_bld", [128, H, 134], BF16))
        c_ps = PB[:, 0:512]
        c_ps2 = PB[:, 512:1024]
        S.dma("sp", lambda e: e.dma_start(out=ident_f[:], in_=dr["ident_f"]), writes=[B("identf")])
        S.emit("dve", lambda v: v.memset(c_lhs1[:], 1.0), writes=[B("c_lhs1")])
        S.emit("dve", lambda v: v.memset(c_ones7[:], 1.0), writes=[B("c_ones7")])
        S.emit("dve", lambda v: v.memset(c_onesf[:], 1.0), writes=[B("c_onesf")])
        S.emit("pool", lambda g: g.iota(c_dj[0:1, 0:7], pattern=[[1, 7]], base=0,
                                        channel_multiplier=0, allow_small_or_imprecise_dtypes=True),
               writes=[B("c_dj")])
        S.emit("dve", lambda v: v.tensor_single_scalar(out=c_lhs1[0:1, 0:7], in_=c_dj[0:1, 0:7],
                                                       scalar=1.0 / 6.0, op=AT.mult),
               reads=[B("c_dj")], writes=[B("c_lhs1")])
        S.dma("sp", lambda e: e.dma_start(out=c_rhs1[0:1, :],
                                          in_=dr["cw1"].rearrange("h j -> (h j)").rearrange("(o f) -> o f", o=1)),
              writes=[B("c_rhs1")])
        S.dma("sp", lambda e: e.dma_start(out=c_rhs1[1:2, :],
                                          in_=dr["cb1"].rearrange("h j -> (h j)").rearrange("(o f) -> o f", o=1)),
              writes=[B("c_rhs1")])
        S.emit("te", lambda t: t.matmul(c_ps[0:7, :], c_lhs1[:, 0:7], c_rhs1[:],
                                        start=True, stop=True),
               reads=[B("c_lhs1"), B("c_rhs1")], writes=pb(0, 512))
        S.emit("act", lambda a: a.activation(out=c_h1[:], in_=c_ps[0:7, :], func=AF.Gelu),
               reads=pb(0, 512), writes=[B("c_h1")])
        for h in range(H):
            S.emit("te", lambda t, h=h: t.transpose(c_ps2[0:32, h * 7:(h + 1) * 7],
                                                    c_h1[:, h * 32:(h + 1) * 32],
                                                    ident_f[0:7, 0:7]),
                   reads=[B("c_h1"), B("identf")], writes=pb(512, 1024))
        S.emit("dve", lambda v: v.tensor_copy(out=c_h1T[:].rearrange("p h w -> p (h w)"),
                                              in_=c_ps2[0:32, 0:H * 7]),
               reads=pb(512, 1024), writes=[B("c_h1T")])
        S.dma("sp", lambda e: e.dma_start(out=c_w2[:], in_=dr["cw2"].rearrange("h j i -> j h i")),
              writes=[B("c_w2")])
        for h in range(H):
            S.emit("te", lambda t, h=h: t.transpose(c_ps2[0:32, h * 32:(h + 1) * 32],
                                                    c_w2[:, h, :], ident_f[0:32, 0:32]),
                   reads=[B("c_w2"), B("identf")], writes=pb(512, 1024))
        S.emit("dve", lambda v: v.tensor_copy(out=c_w2T[:].rearrange("p h i -> p (h i)"),
                                              in_=c_ps2[0:32, 0:H * 32]),
               reads=pb(512, 1024), writes=[B("c_w2T")])
        S.dma("sp", lambda e: e.dma_start(out=c_cb2[:],
                                          in_=dr["cb2"].rearrange("h j -> (h j)").rearrange("(o f) -> o f", o=1)),
              writes=[B("c_cb2")])
        for h in range(H):
            S.emit("te", lambda t, h=h: t.matmul(c_ps[0:32, h * 7:(h + 1) * 7], c_w2T[:, h, :],
                                                 c_h1T[:, h, :], start=True, stop=False),
                   reads=[B("c_w2T"), B("c_h1T")], writes=pb(0, 512))
            S.emit("te", lambda t, h=h: t.matmul(c_ps[0:32, h * 7:(h + 1) * 7],
                                                 c_cb2[:, h * 32:(h + 1) * 32], c_ones7[:],
                                                 start=False, stop=True),
                   reads=[B("c_cb2"), B("c_ones7")], writes=pb(0, 512))
        S.emit("act", lambda a: a.activation(out=c_h2[:].rearrange("p h w -> p (h w)"),
                                             in_=c_ps[0:32, 0:H * 7], func=AF.Gelu),
               reads=pb(0, 512), writes=[B("c_h2")])
        S.dma("sp", lambda e: e.dma_start(out=c_w3T[:], in_=dr["cw3"].rearrange("h i -> i h"),
                                          allow_slow_non_contiguous=True),
              writes=[B("c_w3T")])
        S.dma("sp", lambda e: e.dma_start(out=c_cb3[:], in_=dr["cb3"].rearrange("(o h) -> o h", o=1)),
              writes=[B("c_cb3")])
        for h in range(H):
            S.emit("te", lambda t, h=h: t.matmul(c_ps2[0:1, h * 7:(h + 1) * 7], c_w3T[:, h:h + 1],
                                                 c_h2[:, h, :], start=True, stop=True),
                   reads=[B("c_w3T"), B("c_h2")], writes=pb(512, 1024))
        cb3b = c_cb3[:].unsqueeze(-1).broadcast_to([1, H, 7])
        S.emit("dve", lambda v: v.tensor_tensor(
            out=c_cwt[:], in0=c_ps2[0:1, 0:H * 7].rearrange("p (h w) -> p h w", w=7),
            in1=cb3b, op=AT.add),
            reads=pb(512, 1024) + [B("c_cb3")], writes=[B("c_cwt")])
        S.emit("dve", lambda v: v.reduce_max(out=c_max[:], in_=c_cwt[:], axis=AX.X),
               reads=[B("c_cwt")], writes=[B("c_max")])
        cmaxb = c_max[:].unsqueeze(-1).broadcast_to([1, H, 7])
        S.emit("dve", lambda v: v.tensor_tensor(out=c_cwt[:], in0=c_cwt[:], in1=cmaxb,
                                                op=AT.subtract),
               reads=[B("c_cwt"), B("c_max")], writes=[B("c_cwt")])
        S.emit("act", lambda a: a.activation(out=c_cwt[:], in_=c_cwt[:], func=AF.Exp),
               reads=[B("c_cwt")], writes=[B("c_cwt")])
        S.emit("dve", lambda v: v.reduce_sum(out=c_max[:], in_=c_cwt[:], axis=AX.X),
               reads=[B("c_cwt")], writes=[B("c_max")])
        S.emit("dve", lambda v: v.reciprocal(out=c_max[:], in_=c_max[:]),
               reads=[B("c_max")], writes=[B("c_max")])
        csumb = c_max[:].unsqueeze(-1).broadcast_to([1, H, 7])
        S.emit("dve", lambda v: v.tensor_tensor(out=c_conn[:], in0=c_cwt[:], in1=csumb,
                                                op=AT.mult),
               reads=[B("c_cwt"), B("c_max")], writes=[B("c_conn")])
        S.emit("te", lambda t: t.matmul(c_ps[:, 0:H * 7], c_onesf[:],
                                        c_conn[:].rearrange("p h w -> p (h w)"),
                                        start=True, stop=True),
               reads=[B("c_onesf"), B("c_conn")], writes=pb(0, 512))
        S.emit("dve", lambda v: v.tensor_copy(out=c_connb[:].rearrange("p h w -> p (h w)"),
                                              in_=c_ps[:, 0:H * 7]),
               reads=pb(0, 512), writes=[B("c_connb")])
        S.emit("pool", lambda g: g.iota(c_dj[:], pattern=[[1, 134]], base=0,
                                        channel_multiplier=-1, allow_small_or_imprecise_dtypes=True),
               writes=[B("c_dj")])
        S.emit("dve", lambda v: v.memset(conn_d[:], 0.0),
               writes=[B("conn_d")])
        for w in range(WIN):
            S.emit("dve", lambda v, w=w: v.tensor_single_scalar(out=c_mask[:], in_=c_dj[:],
                                                                scalar=float(w), op=AT.is_equal),
                   reads=[B("c_dj")], writes=[B("c_mask")])
            for h in range(H):
                S.emit("dve", lambda v, w=w, h=h: v.scalar_tensor_tensor(
                    out=conn_d[:, h, :], in0=c_mask[:], scalar=c_connb[:, h, w:w + 1],
                    in1=conn_d[:, h, :], op0=AT.mult, op1=AT.add),
                    reads=[B("c_mask"), B("c_connb"), B("conn_d")], writes=[B("conn_d")])
        S.dma("sp", lambda e: e.dma_start(out=dr["connd"],
                                          in_=conn_d[:].rearrange("p h w -> p (h w)")),
              reads=[B("conn_d")])

    S.barrier()

    wgT_s = [B(f"wgT_s{s}") for s in range(22)]
    wupT_s = [B(f"wupT_s{s}") for s in range(22)]
    wdT_s = [[B(f"wdT_s{s}_{seg}") for seg in range(NQ)] for s in range(8)]

    def emit_ln_tile(t):
        par = t % 2
        S.emit("dve", lambda v, t=t, par=par: v.bn_stats(
            out=ln_bn[:, par, 0:6], in_=x_res[:, t, 0:512]),
            reads=[xb[t]], writes=[B(f"ln_bn{par}")])
        S.emit("dve", lambda v, t=t, par=par: v.bn_stats(
            out=ln_bn[:, par, 6:12], in_=x_res[:, t, 512:1024]),
            reads=[xb[t]], writes=[B(f"ln_bn{par}")])
        S.emit("dve", lambda v, t=t, par=par: v.bn_aggr(
            out=ln_mv[:, t, :], in_=ln_bn[:, par, :]),
            reads=[B(f"ln_bn{par}")], writes=[B("ln_mv")])

    def emit_ln_final():
        S.emit("dve", lambda v: v.tensor_single_scalar(
            out=ln_var[:], in_=ln_mv[:, :, 1], scalar=1e-5, op=AT.add),
            reads=[B("ln_mv")], writes=[B("ln_var")])
        S.emit("act", lambda a: a.activation(out=ln_rstd[:], in_=ln_var[:], func=AF.Ln),
               reads=[B("ln_var")], writes=[B("ln_rstd")])
        S.emit("dve", lambda v: v.tensor_single_scalar(out=ln_rstd[:], in_=ln_rstd[:],
                                                       scalar=-0.5, op=AT.mult),
               reads=[B("ln_rstd")], writes=[B("ln_rstd")])
        S.emit("act", lambda a: a.activation(out=ln_rstd[:], in_=ln_rstd[:], func=AF.Exp),
               reads=[B("ln_rstd")], writes=[B("ln_rstd")])

    def emit_ln_stats():
        for t in range(NT):
            emit_ln_tile(t)
        emit_ln_final()

    def store_tile(t):
        qe = ("sp", "act", "pool")[t % 3]
        S.dma(qe, lambda e, t=t: e.dma_start(
            out=dr["out"].rearrange("(t p) d -> p t d", p=128)[:, t, :],
            in_=x_res[:, t, :]), reads=[xb[t]])

    def emit_ffn(phase, with_prep=False, store_out=False, skip_ln=False):
        with (nc.sbuf_tensor(f"f{phase}_wg", [128, 2, 8, 768], BF16) as wg_sb,
              nc.sbuf_tensor(f"f{phase}_wu", [128, 2, 8, 768], BF16) as wu_sb,
              nc.sbuf_tensor(f"f{phase}_wd", [128, 6, D], BF16) as wd_sb,
              nc.sbuf_tensor(f"f{phase}_xqT", [128, NT, 8, 128], BF16) as xqT_sb,
              nc.sbuf_tensor(f"f{phase}_xq", [128, D], BF16) as xq_sb,
              nc.sbuf_tensor(f"f{phase}_h", [128, 2, 768], BF16) as h_sb,
              nc.sbuf_tensor(f"f{phase}_hq", [128, 2, 768], BF16) as hq_sb,
              nc.sbuf_tensor(f"f{phase}_hqT", [128, 2, 6, 128], BF16) as hqT_sb,
              ExitStack() as f_stk):
            qt2 = [make_qt(f_stk, f"f{phase}qa", D, with_t1=False),
                   make_qt(f_stk, f"f{phase}qb", 768, with_t1=False)]
            qt = qt2[0]
            if with_prep:
                qt_wp = qt2[1]
                WPD = 4
                wp_in = f_stk.enter_context(nc.sbuf_tensor("wp_in", [128, WPD, 768], F32))
                wp_q = f_stk.enter_context(nc.sbuf_tensor("wp_q", [128, 2, 768], BF16))
                wp_st = f_stk.enter_context(nc.sbuf_tensor("wp_st", [128, 2, 4, 128], BF16))
                win_b = [B(f"wp_in{i}") for i in range(WPD)]
                wq_b = [B("wp_q0"), B("wp_q1")]
                wp_tps = PC[:, 512:1024].bitcast(BF16)
                piece_ctr = [0]
                st_ctr = [0]
                slice_jobs = []
                for s in range(2):
                    slice_jobs.append(("w_gate", s))
                    slice_jobs.append(("w_up", s))
                for s in range(8):
                    slice_jobs.append(("w_down", (s, 0)))
                need_at = {}
                gu_done = 2
                for seg in range(1, NQ):
                    for s in range(8):
                        slice_jobs.append(("w_down", (s, seg)))
                    es = (FSL[seg][0] + FSL[seg][1]) // 128
                    for s in range(gu_done, es):
                        slice_jobs.append(("w_gate", s))
                        slice_jobs.append(("w_up", s))
                    gu_done = es
                    need_at[seg] = len(slice_jobs)

                def prep_job(j):
                    name, s = slice_jobs[j]
                    coff = 0
                    if name == "w_down":
                        s, seg = s
                        coff, ncols = FSL[seg]
                        src_t, dst, dstb = dr["w_down"], dr["wdT"], wdT_s[s][seg]
                    elif name == "w_gate":
                        src_t, dst, dstb, ncols = dr["w_gate"], dr["wgT"], wgT_s[s], D
                    else:
                        src_t, dst, dstb, ncols = dr["w_up"], dr["wupT"], wupT_s[s], D
                    pieces = []
                    for p0 in range(0, ncols, 384):
                        p1 = min(ncols, p0 + 384)
                        pn = p1 - p0
                        wi = piece_ctr[0] % WPD
                        k = piece_ctr[0] % 2
                        s2 = (piece_ctr[0] // 2) % 2
                        piece_ctr[0] += 1
                        cq0 = k * 384
                        qe = ("act", "pool")[piece_ctr[0] % 2]
                        S.dma(qe, lambda e, src_t=src_t, s=s, p0=p0 + coff, pn=pn,
                              wi=wi, cq0=cq0: e.dma_start(
                            out=wp_in[:, wi, cq0:cq0 + pn],
                            in_=src_t[s * 128:(s + 1) * 128, p0:p0 + pn]),
                            writes=[win_b[wi]])
                        wqk = B(f"wp_q{s2}_{k}")
                        chain = quant_chain_ops_v2(
                            qt_wp, wp_in[:, wi, :], [win_b[wi]], wp_q[:, s2, :],
                            wqk, cq0, cq0 + pn, k, bf=BF_W)

                        def post(s=s, p0=p0, pn=pn, coff=coff, s2=s2, cq0=cq0,
                                 wqk=wqk, dst=dst, dstb=dstb):
                            nch = pn // 128
                            for c in range(nch):
                                S.emit("te", lambda t, c=c, s2=s2, cq0=cq0: t.transpose(
                                    wp_tps[:, c * 128:(c + 1) * 128],
                                    wp_q[:, s2, cq0 + c * 128:cq0 + (c + 1) * 128],
                                    ident_bf[:]),
                                    reads=[wqk, B("identb")],
                                    writes=pc(512, 1024) if (c == 0 or c == nch - 1) else ())
                            gp = st_ctr[0] % 2
                            st_ctr[0] += 1
                            wstb2 = B(f"wp_st{gp}")
                            S.emit("dve", lambda v, nch=nch, gp=gp: v.tensor_copy(
                                out=wp_st[:, gp, 0:nch, :].rearrange("p c n -> p (c n)"),
                                in_=wp_tps[:, 0:nch * 128]),
                                reads=pc(512, 1024), writes=[wstb2])
                            ca = (coff + p0) // 128
                            dst_ap = dst[ca * 128:(ca + nch) * 128,
                                         s * 128:(s + 1) * 128].rearrange("(c p) n -> p c n", p=128)
                            S.dma("sp", lambda e, dst_ap=dst_ap, nch=nch, gp=gp: e.dma_start(
                                out=dst_ap, in_=wp_st[:, gp, 0:nch, :]),
                                reads=[wstb2], writes=[dstb])
                        pieces.append((chain, post))
                    return pieces
            if not skip_ln:
                emit_ln_stats()
            g_ps = PA[:, 0:1024]
            u_ps = PA[:, 1024:2048]
            d_ps = PB[:]
            t_ps = PC[:, 0:512].bitcast(BF16)
            t_ps2 = PC[:, 512:1024].bitcast(BF16)
            wgb = [[B(f"f{phase}_wg{pp}_{qi}") for qi in range(3)] for pp in range(2)]
            wub = [[B(f"f{phase}_wu{pp}_{qi}") for qi in range(3)] for pp in range(2)]
            wdb = [B(f"f{phase}_wd{qi}") for qi in range(3)]
            xqTb = [B(f"f{phase}_xqT{t}") for t in range(NT)]
            hb = [B(f"f{phase}_h0"), B(f"f{phase}_h1")]
            hqb = [B(f"f{phase}_hq0"), B(f"f{phase}_hq1")]

            def load_gu(q, pp):
                off, ncols = FSL[q]
                sl0, sl1 = off // 128, (off + ncols) // 128
                for dst, bl, src_, sbl in ((wg_sb, wgb, dr["wgT"], wgT_s),
                                           (wu_sb, wub, dr["wupT"], wupT_s)):
                    src_ap = src_[:, off:off + ncols].rearrange("(k p) n -> p k n", p=128)
                    c3 = ncols // 3 // 16 * 16
                    for qi, (qe, d0, d1) in enumerate((("sp", 0, c3), ("act", c3, 2 * c3),
                                                       ("pool", 2 * c3, ncols))):
                        S.dma(qe, lambda e, dst=dst, pp=pp, src_ap=src_ap, d0=d0, d1=d1: e.dma_start(
                            out=dst[:, pp, :, d0:d1], in_=src_ap[:, :, d0:d1]),
                            reads=sbl[sl0:sl1], writes=[bl[pp][qi]])

            def load_wd(q):
                off, ncols = FSL[q]
                nch = ncols // 128
                src_ap = dr["wdT"][off:off + ncols, :].rearrange("(c p) n -> p c n", p=128)
                for qi, (qe, d0, d1) in enumerate((("sp", 0, nch // 3),
                                                   ("act", nch // 3, 2 * nch // 3),
                                                   ("pool", 2 * nch // 3, nch))):
                    if d0 < d1:
                        S.dma(qe, lambda e, src_ap=src_ap, d0=d0, d1=d1: e.dma_start(
                            out=wd_sb[:, d0:d1, :], in_=src_ap[:, d0:d1, :]),
                            reads=[wdT_s[s2][q] for s2 in range(8)], writes=[wdb[qi]])

            def load_ws(q, pp):
                load_gu(q, pp)
                load_wd(q)

            def s1_quant(t):
                qtp = qt2[0]
                xqb = B(f"f{phase}_xqb")
                S.emit("dve", lambda v, t=t, qtp=qtp: v.tensor_scalar(
                    out=qtp["xn"][:, 0:D], in0=x_res[:, t, :],
                    scalar1=ln_mv[:, t, 0:1], scalar2=ln_rstd[:, t:t + 1],
                    op0=AT.subtract, op1=AT.mult),
                    reads=[xb[t], B("ln_mv"), B("ln_rstd")],
                    writes=[qtp["bufs"]["xn"]])
                quant_emit(qtp, qtp["xn"][:, 0:D], [qtp["bufs"]["xn"]], xq_sb[:],
                           xqb, D, bf=BF_X)
                for c in range(8):
                    S.emit("te", lambda te, c=c: te.transpose(
                        t_ps2[:, c * 128:(c + 1) * 128], xq_sb[:, c * 128:(c + 1) * 128],
                        ident_bf[:]),
                        reads=[xqb, B("identb")],
                        writes=pc(512, 1024) if c in (0, 7) else ())
                S.emit("dve", lambda v, t=t: v.tensor_copy(
                    out=xqT_sb[:, t, :, :].rearrange("p k n -> p (k n)"),
                    in_=t_ps2[:, 0:1024]),
                    reads=pc(512, 1024), writes=[xqTb[t]])

            def s2_gateup(q, t, pp):
                off, ncols = FSL[q]
                nmm = (ncols + 511) // 512
                for mi in range(nmm):
                    n0 = mi * 512
                    n1 = min(ncols, n0 + 512)
                    for kc in range(8):
                        edge = (kc == 0 or kc == 7)
                        S.emit("te", lambda te, t=t, kc=kc, n0=n0, n1=n1, pp=pp: te.matmul(
                            g_ps[:, n0:n1], xqT_sb[:, t, kc, :], wg_sb[:, pp, kc, n0:n1],
                            start=(kc == 0), stop=(kc == 7)),
                            reads=[xqTb[t]] + wgb[pp], writes=pa(n0, n1) if edge else ())
                        S.emit("te", lambda te, t=t, kc=kc, n0=n0, n1=n1, pp=pp: te.matmul(
                            u_ps[:, n0:n1], xqT_sb[:, t, kc, :], wu_sb[:, pp, kc, n0:n1],
                            start=(kc == 0), stop=(kc == 7)),
                            reads=[xqTb[t]] + wub[pp], writes=pa(1024 + n0, 1024 + n1) if edge else ())

            def s3_silumul(q, t):
                off, ncols = FSL[q]
                par = t % 2
                for n0 in range(0, ncols, 512):
                    n1 = min(ncols, n0 + 512)
                    S.emit("act", lambda a, n0=n0, n1=n1, par=par: a.activation(
                        out=h_sb[:, par, n0:n1], in_=g_ps[:, n0:n1], func=AF.Silu),
                        reads=pa(n0, n1), writes=[hb[par]])
                    S.emit("dve", lambda v, n0=n0, n1=n1, par=par: v.tensor_mul(
                        out=h_sb[:, par, n0:n1], in0=h_sb[:, par, n0:n1],
                        in1=u_ps[:, n0:n1]),
                        reads=[hb[par]] + pa(1024 + n0, 1024 + n1), writes=[hb[par]])

            def stage_a(q, t):
                off, ncols = FSL[q]
                par = t % 2
                qtp = qt2[par]
                quant_emit(qtp, h_sb[:, par, 0:ncols], [hb[par]], hq_sb[:, par, 0:ncols],
                           hqb[par], ncols, bf=BF_H)

            def stage_c1(q, t, pp):
                off, ncols = FSL[q]
                nch = ncols // 128
                par = t % 2
                hqTp = B(f"f{phase}_hqT{par}")
                for c in range(nch):
                    S.emit("te", lambda te, c=c, par=par: te.transpose(
                        t_ps[:, c * 128:(c + 1) * 128], hq_sb[:, par, c * 128:(c + 1) * 128],
                        ident_bf[:]),
                        reads=[hqb[par], B("identb")],
                        writes=pc(0, 512) if c in (0, nch - 1) else ())
                S.emit("act", lambda a, nch=nch, par=par: a.activation(
                    out=hqT_sb[:, par, 0:nch, :].rearrange("p c n -> p (c n)"),
                    in_=t_ps[:, 0:nch * 128], func=AF.Copy),
                    reads=pc(0, 512), writes=[hqTp])
                for c in range(nch):
                    for mi in range(2):
                        n0 = mi * 512
                        edge = (c == 0 or c == nch - 1)
                        S.emit("te", lambda te, c=c, n0=n0, pp=pp, par=par, st=(c == 0),
                               sp_=(c == nch - 1): te.matmul(
                            d_ps[:, n0:n0 + 512], hqT_sb[:, par, c, :],
                            wd_sb[:, c, n0:n0 + 512],
                            start=st, stop=sp_),
                            reads=[hqTp] + wdb, writes=pb(n0, n0 + 512) if edge else ())

            def stage_res(q, t):
                S.emit("dve", lambda v, t=t: v.scalar_tensor_tensor(
                    out=x_res[:, t, :], in0=d_ps[:], scalar=0.5, in1=x_res[:, t, :],
                    op0=AT.mult, op1=AT.add),
                    reads=pb(0, 1024) + [xb[t]], writes=[xb[t]])

            next_job = [0]

            pend = []

            def prep_flush(all_=False):
                while len(pend) >= 2 or (all_ and pend):
                    pair = [pend.pop(0) for _ in range(min(2, len(pend)))]
                    emit_chains(*[c for c, _ in pair])
                    for _, post in pair:
                        post()

            def prep_steps(n, drain=False):
                if not with_prep:
                    return
                for _ in range(n):
                    if next_job[0] < len(slice_jobs):
                        pend.extend(prep_job(next_job[0]))
                        next_job[0] += 1
                        prep_flush()
                if drain:
                    prep_flush(all_=True)

            if with_prep:
                prep_steps(4, drain=True)
                load_gu(0, 0)
                for t in (0, 1, 2):
                    s1_quant(t)
                prep_steps(8, drain=True)
                load_wd(0)
            else:
                load_ws(0, 0)
                for t in (0, 1, 2):
                    s1_quant(t)
            for q in range(NQ):
                pp = q % 2
                s2_gateup(q, 0, pp)
                for t in range(NT):
                    s3_silumul(q, t)
                    if t + 1 < NT:
                        s2_gateup(q, t + 1, pp)
                    stage_a(q, t)
                    if t >= 1:
                        stage_c1(q, t - 1, pp)
                        stage_res(q, t - 1)
                        if store_out and q == NQ - 1:
                            store_tile(t - 1)
                    if q == 0 and t + 3 < NT:
                        s1_quant(t + 3)
                    prep_steps(1)
                    if q < NQ - 1 and t == 11:
                        if with_prep:
                            prep_steps(max(0, need_at[q + 1] - next_job[0]), drain=True)
                        load_gu(q + 1, (q + 1) % 2)
                stage_c1(q, NT - 1, pp)
                stage_res(q, NT - 1)
                if store_out and q == NQ - 1:
                    store_tile(NT - 1)
                if q < NQ - 1:
                    load_wd(q + 1)

    emit_ffn(1, with_prep=True)
    S.barrier()
    if debug:
        S.dma("act", lambda e: e.dma_start(out=dr["dbg_wgT"], in_=dr["wgT"]),
              reads=wgT_s)
        S.dma("sp", lambda e: e.dma_start(out=dr["dbg_ffn1"].rearrange("(t p) d -> p t d", p=128),
                                          in_=x_res[:]), reads=list(xb))

    with (nc.sbuf_tensor("a_wT", [128, 8, 1024], BF16) as wT_sb,
          nc.sbuf_tensor("a_K", [128, 8, 2056], BF16) as K_sb,
          nc.sbuf_tensor("a_Q", [128, 8, 2048], BF16) as Q_sb,
          nc.sbuf_tensor("a_Vs", [128, 17, 1024], BF16) as Vs_sb,
          ExitStack() as a_stk):
        emit_ln_stats()
        wTb = B("a_wT")
        Kb = B("a_K")
        Qb = B("a_Q")
        Vsb = B("a_Vs")
        a_tps = PC[:, 0:512].bitcast(BF16)

        S.emit("dve", lambda v: v.memset(K_sb[:, :, 0:3], 0.0),
               writes=[Kb])
        S.emit("dve", lambda v: v.memset(K_sb[:, :, 2051:2056], 0.0),
               writes=[Kb])
        Vsb_t = [B(f"a_Vs_{t}") for t in range(17)]
        S.emit("dve", lambda v: v.memset(Vs_sb[0:3, 0, :], 0.0), writes=[Vsb_t[0]])
        S.emit("dve", lambda v: v.memset(Vs_sb[:, 16, :], 0.0),
               writes=[Vsb_t[16]])

        def load_wT(name, wload_sb, wloadb, dst_sb=None, dstb=None):
            dst_sb = wT_sb if dst_sb is None else dst_sb
            dstb = wTb if dstb is None else dstb
            wlb2 = [B(f"{wloadb.name}_h0"), B(f"{wloadb.name}_h1")]
            for s in range(8):
                for qi, (qe, d0, d1) in enumerate((("pool", 0, 512), ("pool", 512, 1024))):
                    S.dma(qe, lambda e, name=name, s=s, d0=d0, d1=d1: e.dma_start(
                        out=wload_sb[:, d0:d1], in_=dr[name][s * 128:(s + 1) * 128, d0:d1]),
                        writes=[wlb2[qi]])
                for c in range(8):
                    S.emit("te", lambda te, c=c: te.transpose(
                        a_tps[:, c * 128:(c + 1) * 128], wload_sb[:, c * 128:(c + 1) * 128],
                        ident_bf[:]),
                        reads=[wlb2[c // 4], B("identb")],
                        writes=pc(0, 512) if c in (0, 7) else ())
                S.emit("dve", lambda v, s=s, dst_sb=dst_sb: v.tensor_copy(
                    out=dst_sb[:, :, s * 128:(s + 1) * 128],
                    in_=a_tps[:, 0:1024].rearrange("p (k n) -> p k n", n=128)),
                    reads=pc(0, 512), writes=[dstb])

        with (nc.sbuf_tensor("a_wload", [128, 1024], BF16) as wload_sb,
              nc.sbuf_tensor("a_xn2", [128, 1024], BF16) as xn2_sb,
              nc.sbuf_tensor("a_wT2", [128, 8, 1024], BF16) as wT2_sb,
              nc.sbuf_tensor("a_xn2T", [128, 8, 256], BF16) as xn2T_sb,
              nc.sbuf_tensor("a_vt", [128, 2, 1024], BF16) as vt_sb):
            wT2b = B("a_wT2")
            vtb = B("a_vt")
            wloadb = B("a_wload")
            xn2b = B("a_xn2")
            xn2Tb = B("a_xn2T")
            ps1 = PB[:, 0:512]
            ps2 = PB[:, 512:1024]

            def emit_xn2T(t, ti):
                S.emit("dve", lambda v, t=t: v.tensor_scalar(
                    out=xn2_sb[:], in0=x_res[:, t, :],
                    scalar1=ln_mv[:, t, 0:1], scalar2=ln_rstd[:, t:t + 1],
                    op0=AT.subtract, op1=AT.mult),
                    reads=[xb[t], B("ln_mv"), B("ln_rstd")], writes=[xn2b])
                for c in range(8):
                    S.emit("te", lambda te, c=c: te.transpose(
                        a_tps[:, c * 128:(c + 1) * 128], xn2_sb[:, c * 128:(c + 1) * 128],
                        ident_bf[:]),
                        reads=[xn2b, B("identb")],
                        writes=pc(0, 512) if c in (0, 7) else ())
                S.emit("dve", lambda v, ti=ti: v.tensor_copy(
                    out=xn2T_sb[:, :, ti * 128:(ti + 1) * 128],
                    in_=a_tps[:, 0:1024].rearrange("p (k n) -> p k n", n=128)),
                    reads=pc(0, 512), writes=[xn2Tb])

            def kq_pass():
                load_wT("wk", wload_sb, wloadb)
                load_wT("wq", wload_sb, wloadb, wT2_sb, wT2b)
                for g in range(8):
                    for ti in range(2):
                        emit_xn2T(g * 2 + ti, ti)
                    for sl in range(8):
                        for kc in range(8):
                            S.emit("te", lambda te, sl=sl, kc=kc: te.matmul(
                                ps1[:, 0:256], wT_sb[:, kc, sl * 128:(sl + 1) * 128],
                                xn2T_sb[:, kc, :],
                                start=(kc == 0), stop=(kc == 7)),
                                reads=[wTb, xn2Tb], writes=pb(0, 512) if kc in (0, 7) else ())
                        for kc in range(8):
                            S.emit("te", lambda te, sl=sl, kc=kc: te.matmul(
                                ps2[:, 0:256], wT2_sb[:, kc, sl * 128:(sl + 1) * 128],
                                xn2T_sb[:, kc, :],
                                start=(kc == 0), stop=(kc == 7)),
                                reads=[wT2b, xn2Tb], writes=pb(512, 1024) if kc in (0, 7) else ())
                        S.emit("dve", lambda v, sl=sl, g=g: v.tensor_copy(
                            out=K_sb[:, sl, 3 + g * 256: 3 + (g + 1) * 256], in_=ps1[:, 0:256]),
                            reads=pb(0, 512), writes=[Kb])
                        S.emit("dve", lambda v, sl=sl, g=g: v.tensor_scalar(
                            out=Q_sb[:, sl, g * 256:(g + 1) * 256], in0=ps2[:, 0:256],
                            scalar1=0.125, scalar2=None, op0=AT.mult),
                            reads=pb(512, 1024), writes=[Qb])

            kq_pass()
            load_wT("wv", wload_sb, wloadb)
            for g in range(8):
                for ti in range(2):
                    t = g * 2 + ti
                    emit_xn2T(t, ti)
                    vp = t % 2
                    vtb2 = B(f"a_vt{vp}")
                    for half in range(2):
                        ps = ps1 if half == 0 else ps2
                        bnk = pb(0, 512) if half == 0 else pb(512, 1024)
                        for kc in range(8):
                            S.emit("te", lambda te, ti=ti, kc=kc, half=half, ps=ps: te.matmul(
                                ps[:], xn2T_sb[:, kc, ti * 128:(ti + 1) * 128],
                                wT_sb[:, kc, half * 512:(half + 1) * 512],
                                start=(kc == 0), stop=(kc == 7)),
                                reads=[xn2Tb, wTb], writes=bnk if kc in (0, 7) else ())
                        S.emit("dve", lambda v, half=half, ps=ps, vp=vp: v.tensor_copy(
                            out=vt_sb[:, vp, half * 512:(half + 1) * 512], in_=ps[:]),
                            reads=bnk, writes=[vtb2])
                    S.dma("act", lambda e, t=t, vp=vp: e.dma_start(out=Vs_sb[3:128, t, :],
                                                                   in_=vt_sb[0:125, vp, :]),
                          reads=[vtb2], writes=[Vsb_t[t]])
                    S.dma("pool", lambda e, t=t, vp=vp: e.dma_start(out=Vs_sb[0:3, t + 1, :],
                                                                    in_=vt_sb[125:128, vp, :]),
                          reads=[vtb2], writes=[Vsb_t[t + 1]])

        S.barrier()
        with (nc.sbuf_tensor("a_sc", [128, H, 144], BF16) as sc_sb,
              nc.sbuf_tensor("a_connd", [128, H, 134], BF16) as conn_d,
              nc.sbuf_tensor("a_bo", [1, D], BF16) as bo_bf,
              nc.sbuf_tensor("a_m", [128, H], F32) as m_sb,
              nc.sbuf_tensor("a_den", [128, H], F32) as den_sb,
              nc.sbuf_tensor("a_den2", [128, H], F32) as den2_sb,
              nc.sbuf_tensor("a_fwT1", [128, H, 128], BF16) as fwT1_sb,
              nc.sbuf_tensor("a_fwT2", [16, H, 128], BF16) as fwT2_sb,
              nc.sbuf_tensor("a_O", [128, 8, 128], BF16) as O_sb,
              nc.sbuf_tensor("a_wl2", [128, 1024], BF16) as wl2_sb):
            load_wT("wo", wl2_sb, B("a_wl2"))
            S.dma("sp", lambda e: e.dma_start(out=conn_d[:].rearrange("p h w -> p (h w)"),
                                              in_=dr["connd"]), writes=[B("conn_d")])
            S.dma("pool", lambda e: e.dma_start(out=bo_bf[:],
                                                in_=dr["bo"].rearrange("(o d) -> o d", o=1)),
                  writes=[B("bo")])
            scb = [B("a_sc0"), B("a_sc1")]
            mb = [B("a_m0"), B("a_m1")]
            denb = [B("a_den0"), B("a_den1")]
            den2b = [B("a_den20"), B("a_den21")]
            fwT1b = [B("a_fwT10"), B("a_fwT11")]
            fwT2b = [B("a_fwT20"), B("a_fwT21")]
            Ob = B("a_O")
            conn_b = B("conn_d")
            S.emit("dve", lambda v: v.memset(sc_sb[:, :, 134:144], 0.0),
                   writes=[scb[0], scb[1]])
            sc_ps = PA[:]
            sc_heads = PA[:].rearrange("p (x j) -> p x j", j=256)[:, 0:8, 0:134]
            wo_ps = PC[:, 512:1024]

            def scores(t, hh):
                for h8 in range(8):
                    h = hh * 8 + h8
                    off = h8 * 256
                    hb2 = (h % 2) * 64
                    S.emit("te", lambda te, h=h, off=off, hb2=hb2, t=t: te.matmul(
                        sc_ps[:, off:off + 134],
                        Q_sb[hb2:hb2 + 64, h // 2, t * 128:(t + 1) * 128],
                        K_sb[hb2:hb2 + 64, h // 2, t * 128:t * 128 + 134],
                        start=True, stop=True),
                        reads=[Qb, Kb], writes=pa(off, off + 134))

            def softmax_pre(t, hh):
                sl = slice(hh * 8, (hh + 1) * 8)
                S.emit("dve", lambda v, sl=sl: v.reduce_max(
                    out=m_sb[:, sl], in_=sc_heads, axis=AX.X),
                    reads=pa(0, 2048), writes=[mb[hh]])
                mbc = m_sb[:, sl].unsqueeze(-1).broadcast_to([128, 8, 134])
                S.emit("dve", lambda v, sl=sl, mbc=mbc: v.tensor_tensor(
                    out=sc_sb[:, sl, 0:134], in0=sc_heads, in1=mbc, op=AT.subtract),
                    reads=pa(0, 2048) + [mb[hh]], writes=[scb[hh]])
                S.emit("act", lambda a, sl=sl: a.activation(
                    out=sc_sb[:, sl, 0:134], in_=sc_sb[:, sl, 0:134], func=AF.Exp),
                    reads=[scb[hh]], writes=[scb[hh]])

            def softmax_post(t, hh):
                sl = slice(hh * 8, (hh + 1) * 8)
                S.emit("dve", lambda v, sl=sl: v.reduce_sum(
                    out=den2_sb[:, sl], in_=sc_sb[:, sl, 0:134], axis=AX.X),
                    reads=[scb[hh]], writes=[den2b[hh]])
                S.emit("dve", lambda v, sl=sl: v.tensor_tensor(
                    out=sc_sb[:, sl, 0:134], in0=sc_sb[:, sl, 0:134],
                    in1=conn_d[:, sl, :], op=AT.mult),
                    reads=[scb[hh], conn_b], writes=[scb[hh]])
                S.emit("dve", lambda v, sl=sl: v.reduce_sum(
                    out=den_sb[:, sl], in_=sc_sb[:, sl, 0:134], axis=AX.X),
                    reads=[scb[hh]], writes=[denb[hh]])
                S.emit("dve", lambda v, sl=sl: v.scalar_tensor_tensor(
                    out=den_sb[:, sl], in0=den2_sb[:, sl], scalar=1e-9,
                    in1=den_sb[:, sl], op0=AT.mult, op1=AT.add),
                    reads=[den2b[hh], denb[hh]], writes=[denb[hh]])
                S.emit("dve", lambda v, sl=sl: v.reciprocal(
                    out=den_sb[:, sl], in_=den_sb[:, sl]),
                    reads=[denb[hh]], writes=[denb[hh]])
                rbc = den_sb[:, sl].unsqueeze(-1).broadcast_to([128, 8, 134])
                S.emit("dve", lambda v, sl=sl, rbc=rbc: v.tensor_tensor(
                    out=sc_sb[:, sl, 0:134], in0=sc_sb[:, sl, 0:134], in1=rbc, op=AT.mult),
                    reads=[scb[hh], denb[hh]], writes=[scb[hh]])

            def fwtrans(t, hh):
                for hg in range(2):
                    for hi in range(4):
                        h = hh * 8 + hg * 4 + hi
                        S.emit("te", lambda te, h=h, hi=hi: te.transpose(
                            a_tps[:, hi * 128:(hi + 1) * 128], sc_sb[:, h, 0:128], ident_bf[:]),
                            reads=[scb[hh], B("identb")], writes=pc(0, 512))
                        S.emit("te", lambda te, h=h, hi=hi: te.transpose(
                            a_tps[0:16, 512 + hi * 128: 512 + (hi + 1) * 128],
                            sc_sb[:, h, 128:144], ident_bf[:]),
                            reads=[scb[hh], B("identb")], writes=pc(0, 512))
                    h0 = hh * 8 + hg * 4
                    S.emit("dve", lambda v, h0=h0: v.tensor_copy(
                        out=fwT1_sb[:, h0:h0 + 4, :].rearrange("p h n -> p (h n)"),
                        in_=a_tps[:, 0:512]),
                        reads=pc(0, 512), writes=[fwT1b[hh]])
                    S.emit("dve", lambda v, h0=h0: v.tensor_copy(
                        out=fwT2_sb[:, h0:h0 + 4, :].rearrange("p h n -> p (h n)"),
                        in_=a_tps[0:16, 512:1024]),
                        reads=pc(0, 512), writes=[fwT2b[hh]])

            def av(t, hh):
                bnk = pb(0, 512) if hh == 0 else pb(512, 1024)
                for h8 in range(8):
                    h = hh * 8 + h8
                    hb2 = (h % 2) * 64
                    S.emit("te", lambda te, h=h, hb2=hb2, t=t: te.matmul(
                        av_ps[hb2:hb2 + 64, (h // 2) * 128:(h // 2) * 128 + 128],
                        Vs_sb[:, t, h * 64:(h + 1) * 64], fwT1_sb[:, h, :],
                        start=True, stop=False, tile_position=(0, hb2)),
                        reads=[Vsb_t[t], fwT1b[hh]], writes=pb(0, 1024))
                    S.emit("te", lambda te, h=h, hb2=hb2, t=t: te.matmul(
                        av_ps[hb2:hb2 + 64, (h // 2) * 128:(h // 2) * 128 + 128],
                        Vs_sb[0:16, t + 1, h * 64:(h + 1) * 64], fwT2_sb[:, h, :],
                        start=False, stop=True, tile_position=(0, hb2)),
                        reads=[Vsb_t[t + 1], fwT2b[hh]], writes=pb(0, 1024))

            def wo_block(t):
                S.emit("dve", lambda v: v.tensor_copy(
                    out=O_sb[:].rearrange("p k n -> p (k n)"), in_=av_ps[:]),
                    reads=pb(0, 1024), writes=[Ob])
                wo_ps2 = PC[:, 512:1024]
                for mi in range(2):
                    S.emit("te", lambda te, mi=mi: te.matmul(
                        wo_ps2[:], ones_bf[:],
                        bo_bf[:, mi * 512:(mi + 1) * 512], start=True, stop=False),
                        reads=[B("ones"), B("bo")], writes=pc(512, 1024))
                    for kc in range(8):
                        S.emit("te", lambda te, kc=kc, mi=mi: te.matmul(
                            wo_ps2[:], O_sb[:, kc, :],
                            wT_sb[:, kc, mi * 512:(mi + 1) * 512],
                            start=False, stop=(kc == 7)),
                            reads=[Ob, wTb],
                            writes=pc(512, 1024) if kc == 7 else ())
                    S.emit("dve", lambda v, t=t, mi=mi: v.tensor_add(
                        out=x_res[:, t, mi * 512:(mi + 1) * 512], in0=wo_ps2[:],
                        in1=x_res[:, t, mi * 512:(mi + 1) * 512]),
                        reads=pc(512, 1024) + [xb[t]], writes=[xb[t]])

            av_ps = PB[:]
            units = [(t, hh) for t in range(NT) for hh in range(2)]
            scores(*units[0])
            for ui, (t, hh) in enumerate(units):
                softmax_pre(t, hh)
                if ui + 1 < len(units):
                    scores(*units[ui + 1])
                if ui >= 1:
                    pt, ph = units[ui - 1]
                    softmax_post(pt, ph)
                    fwtrans(pt, ph)
                    av(pt, ph)
                    if ph == 1:
                        wo_block(pt)
                        emit_ln_tile(pt)
            pt, ph = units[-1]
            softmax_post(pt, ph)
            fwtrans(pt, ph)
            av(pt, ph)
            wo_block(pt)
            emit_ln_tile(pt)
            emit_ln_final()
    S.barrier()
    if debug:
        S.dma("sp", lambda e: e.dma_start(out=dr["dbg_attn"].rearrange("(t p) d -> p t d", p=128),
                                          in_=x_res[:]), reads=list(xb))

    emit_ffn(2, store_out=True, skip_ln=True)
    S.barrier()

    S.barrier()

    S.replay()
    ctx.close()
    return nc


_NC_CACHE = {}


def _get_nc(debug=False):
    if debug not in _NC_CACHE:
        _NC_CACHE[debug] = build_program(debug=debug)
    return _NC_CACHE[debug]


def kernel(L0_attn_x, L0_attn_wq, L0_attn_wk, L0_attn_wv, L0_attn_wo, L0_attn_bo,
           L0_attn_cw1, L0_attn_cb1, L0_attn_cw2, L0_attn_cb2, L0_attn_cw3, L0_attn_cb3,
           L0_ffn_w_gate, L0_ffn_w_up, L0_ffn_w_down, _debug=False):
    x = np.ascontiguousarray(np.asarray(L0_attn_x, dtype=np.float32))
    common = {
        "wq": np.ascontiguousarray(np.asarray(L0_attn_wq, np.float32)),
        "wk": np.ascontiguousarray(np.asarray(L0_attn_wk, np.float32)),
        "wv": np.ascontiguousarray(np.asarray(L0_attn_wv, np.float32)),
        "wo": np.ascontiguousarray(np.asarray(L0_attn_wo, np.float32)),
        "bo": np.ascontiguousarray(np.asarray(L0_attn_bo, np.float32)),
        "cw1": np.ascontiguousarray(np.asarray(L0_attn_cw1, np.float32).reshape(H, 32)),
        "cb1": np.ascontiguousarray(np.asarray(L0_attn_cb1, np.float32)),
        "cw2": np.ascontiguousarray(np.asarray(L0_attn_cw2, np.float32)),
        "cb2": np.ascontiguousarray(np.asarray(L0_attn_cb2, np.float32)),
        "cw3": np.ascontiguousarray(np.asarray(L0_attn_cw3, np.float32).reshape(H, 32)),
        "cb3": np.ascontiguousarray(np.asarray(L0_attn_cb3, np.float32).reshape(H)),
        "w_gate": np.ascontiguousarray(np.asarray(L0_ffn_w_gate, np.float32)),
        "w_up": np.ascontiguousarray(np.asarray(L0_ffn_w_up, np.float32)),
        "w_down": np.ascontiguousarray(np.asarray(L0_ffn_w_down, np.float32)),
        "ident_bf": np.eye(128, dtype=ml_dtypes.bfloat16),
        "ident_f": np.eye(128, dtype=np.float32),
    }
    nc = _get_nc(debug=_debug)
    in_maps = []
    for c in range(8):
        m = dict(common)
        m["x"] = np.ascontiguousarray(x[c])
        in_maps.append(m)
    res = run_bass_kernel_spmd(nc, in_maps, core_ids=list(range(8)))
    out = np.stack([np.asarray(res.results[c]["out"]) for c in range(8)], axis=0)
    if _debug:
        dbg = {
            "ffn1": np.stack([np.asarray(res.results[c]["dbg_ffn1"]) for c in range(8)]),
            "wgT": np.asarray(res.results[0]["dbg_wgT"]),
            "attn": np.stack([np.asarray(res.results[c]["dbg_attn"]) for c in range(8)]),
        }
        return out, dbg
    return out


# version 73 (final)
# speedup vs baseline: 1.0255x; 1.0148x over previous
import numpy as np
import ml_dtypes
import concourse.bass as bass
import concourse.mybir as mybir
from concourse.bass_utils import run_bass_kernel_spmd
from contextlib import ExitStack

F32 = mybir.dt.float32
BF16 = mybir.dt.bfloat16
AT = mybir.AluOpType
AF = mybir.ActivationFunctionType
AX = mybir.AxisListType

D = 1024
NT = 16
FF = 2816
H = 16
WIN = 7
FSL = [(0, 256), (256, 768), (1024, 768), (1792, 768), (2560, 256)]
NQ = len(FSL)
BF_H = True
BF_W = True
BF_X = False
TE_IMPLICIT = False
VELT = 4194305.0
FIXC = 6291456.0


class Buf:
    __slots__ = ("name", "writer", "readers")

    def __init__(self, name):
        self.name = name
        self.writer = None
        self.readers = {}


class Tok:
    __slots__ = ("eng", "seq", "needed", "val")

    def __init__(self, eng, seq):
        self.eng = eng
        self.seq = seq
        self.needed = False
        self.val = None


class Sched:
    ENGS = ("te", "dve", "act", "pool", "sp")

    def __init__(self, nc, ctx):
        self.nc = nc
        self.eng_obj = {"te": nc.tensor, "dve": nc.vector, "act": nc.scalar,
                        "pool": nc.gpsimd, "sp": nc.sync}
        self.items = {e: [] for e in self.ENGS}
        self.sem = {e: ctx.enter_context(nc.semaphore(f"sem_{e}")) for e in self.ENGS}
        self.NDQ = 8
        self.dsem = {e: [ctx.enter_context(nc.semaphore(f"dsem_{e}{i}")) for i in range(self.NDQ)]
                     for e in ("sp", "act", "pool")}
        self.duse = {e: [0] * self.NDQ for e in ("sp", "act", "pool")}
        self.didx = {e: 0 for e in ("sp", "act", "pool")}

    def _collect(self, eng, reads, writes):
        deps = []
        for b in reads:
            if b.writer is not None:
                deps.append(b.writer)
        for b in writes:
            if b.writer is not None:
                deps.append(b.writer)
            deps.extend(b.readers.values())
        out = []
        seen = set()
        for d in deps:
            if isinstance(d, Tok):
                if TE_IMPLICIT and d.eng == "te" and eng == "te":
                    continue
                k = (d.eng, d.seq)
                if k in seen:
                    continue
                seen.add(k)
                d.needed = True
                out.append(d)
            else:
                k = ("dma", id(d[0]), d[1])
                if k in seen:
                    continue
                seen.add(k)
                out.append(d)
        return out

    def emit(self, eng, fn, reads=(), writes=()):
        deps = self._collect(eng, reads, writes)
        tok = Tok(eng, len(self.items[eng]))
        self.items[eng].append(("op", fn, deps, tok))
        for b in writes:
            b.writer = tok
            b.readers = {}
        for b in reads:
            b.readers[eng] = tok
        return tok

    def dma(self, eng, fn, reads=(), writes=()):
        assert eng in ("sp", "act", "pool")
        deps = self._collect(eng, reads, writes)
        i = self.didx[eng]
        self.didx[eng] = (i + 1) % self.NDQ
        dsem = self.dsem[eng][i]
        if self.duse[eng][i] > 0:
            deps.append((dsem, 16 * self.duse[eng][i]))
        self.duse[eng][i] += 1
        dtok = (dsem, 16 * self.duse[eng][i])
        self.items[eng].append(("dma", fn, deps, dtok))
        for b in writes:
            b.writer = dtok
            b.readers = {}
        for b in reads:
            b.readers[("q", id(dtok[0]))] = dtok
        return dtok

    def barrier(self):
        toks = []
        for o in self.ENGS:
            for kind, fn, deps, tk in reversed(self.items[o]):
                if kind == "op":
                    tk.needed = True
                    toks.append(tk)
                    break
        for qe in ("sp", "act", "pool"):
            for i in range(self.NDQ):
                if self.duse[qe][i] > 0:
                    toks.append((self.dsem[qe][i], 16 * self.duse[qe][i]))
        for e in self.ENGS:
            tok = Tok(e, len(self.items[e]))
            self.items[e].append(("barrier", None, list(toks), tok))

    def replay(self):
        nc = self.nc
        for e in self.ENGS:
            cnt = 0
            for kind, fn, deps, tok in self.items[e]:
                if isinstance(tok, Tok) and tok.needed:
                    cnt += 1
                    tok.val = cnt

        def run_engine(eng):
            obj = self.eng_obj[eng]
            waited = {}

            def do_wait(d):
                if isinstance(d, Tok):
                    if d.eng == eng and isinstance(self.items[eng][0], tuple):
                        pass
                    sem, val, key = self.sem[d.eng], d.val, f"e:{d.eng}"
                    assert val is not None, f"unresolved tok {d.eng}:{d.seq}"
                else:
                    sem, val = d
                    key = f"d:{id(sem)}"
                if waited.get(key, 0) >= val:
                    return
                waited[key] = val
                obj.wait_ge(sem, val)

            for kind, fn, deps, tok in self.items[eng]:
                for d in deps:
                    do_wait(d)
                if kind == "op":
                    inst = fn(obj)
                    if tok.needed:
                        inst.then_inc(self.sem[eng], 1)
                elif kind == "dma":
                    fn(obj).then_inc(tok[0], 16)

        with nc.Block() as block:
            for dec, eng in ((block.tensor, "te"), (block.vector, "dve"),
                             (block.scalar, "act"), (block.gpsimd, "pool"),
                             (block.sync, "sp")):
                def run(_e, eng=eng):
                    run_engine(eng)
                dec(run)


def build_program(debug=False):
    nc = bass.Bass("TRN2", target_bir_lowering=False)
    ctx = ExitStack()
    S = Sched(nc, ctx)
    sb = ctx.enter_context

    dr = {}
    dr["x"] = nc.dram_tensor("x", [2048, D], F32, kind="ExternalInput").ap()
    for w in ("wq", "wk", "wv", "wo"):
        dr[w] = nc.dram_tensor(w, [D, D], F32, kind="ExternalInput").ap()
    dr["bo"] = nc.dram_tensor("bo", [D], F32, kind="ExternalInput").ap()
    dr["cw1"] = nc.dram_tensor("cw1", [H, 32], F32, kind="ExternalInput").ap()
    dr["cb1"] = nc.dram_tensor("cb1", [H, 32], F32, kind="ExternalInput").ap()
    dr["cw2"] = nc.dram_tensor("cw2", [H, 32, 32], F32, kind="ExternalInput").ap()
    dr["cb2"] = nc.dram_tensor("cb2", [H, 32], F32, kind="ExternalInput").ap()
    dr["cw3"] = nc.dram_tensor("cw3", [H, 32], F32, kind="ExternalInput").ap()
    dr["cb3"] = nc.dram_tensor("cb3", [H], F32, kind="ExternalInput").ap()
    dr["w_gate"] = nc.dram_tensor("w_gate", [FF, D], F32, kind="ExternalInput").ap()
    dr["w_up"] = nc.dram_tensor("w_up", [FF, D], F32, kind="ExternalInput").ap()
    dr["w_down"] = nc.dram_tensor("w_down", [D, FF], F32, kind="ExternalInput").ap()
    dr["ident_bf"] = nc.dram_tensor("ident_bf", [128, 128], BF16, kind="ExternalInput").ap()
    dr["ident_f"] = nc.dram_tensor("ident_f", [128, 128], F32, kind="ExternalInput").ap()
    dr["out"] = nc.dram_tensor("out", [2048, D], F32, kind="ExternalOutput").ap()
    if debug:
        dr["dbg_ffn1"] = nc.dram_tensor("dbg_ffn1", [2048, D], F32, kind="ExternalOutput").ap()
        dr["dbg_attn"] = nc.dram_tensor("dbg_attn", [2048, D], F32, kind="ExternalOutput").ap()
        dr["dbg_wgT"] = nc.dram_tensor("dbg_wgT", [D, FF], BF16, kind="ExternalOutput").ap()
    dr["wgT"] = nc.dram_tensor("wgT", [D, FF], BF16, kind="Internal").ap()
    dr["wupT"] = nc.dram_tensor("wupT", [D, FF], BF16, kind="Internal").ap()
    dr["wdT"] = nc.dram_tensor("wdT", [FF, D], BF16, kind="Internal").ap()
    dr["connd"] = nc.dram_tensor("connd", [128, H * 134], BF16, kind="Internal").ap()

    x_res = sb(nc.sbuf_tensor("x_res", [128, NT, D], F32))
    ident_bf = sb(nc.sbuf_tensor("identb", [128, 128], BF16))
    ones_bf = sb(nc.sbuf_tensor("ones_bf", [1, 128], BF16))
    ln_mv = sb(nc.sbuf_tensor("ln_mv", [128, NT, 2], F32))
    ln_bn = sb(nc.sbuf_tensor("ln_bn", [128, 2, 12], F32))
    ln_var = sb(nc.sbuf_tensor("ln_var", [128, NT], F32))
    ln_rstd = sb(nc.sbuf_tensor("ln_rstd", [128, NT], F32))
    qconst = sb(nc.sbuf_tensor("qconst", [128, 1], F32))
    qconst2 = sb(nc.sbuf_tensor("qconst2", [128, 1], F32))

    PA = sb(nc.psum_tensor("PA", [128, 2048], F32))
    PB = sb(nc.psum_tensor("PB", [128, 1024], F32))
    PC = sb(nc.psum_tensor("PC", [128, 1024], F32))
    PBANK = [Buf(f"pbank{i}") for i in range(8)]

    def pa(lo_el, hi_el):
        return PBANK[lo_el // 512: (hi_el + 511) // 512]

    def pb(lo_el, hi_el):
        return PBANK[4 + lo_el // 512: 4 + (hi_el + 511) // 512]

    def pc(lo_el, hi_el):
        return PBANK[6 + lo_el // 512: 6 + (hi_el + 511) // 512]

    bufs = {}

    def B(name):
        if name not in bufs:
            bufs[name] = Buf(name)
        return bufs[name]

    xb = [B(f"x_{t}") for t in range(NT)]

    for xi, xq_eng in enumerate(("sp", "sp", "sp", "sp")):
        S.dma(xq_eng, lambda e, xi=xi: e.dma_start(
            out=x_res[:, xi * 4:(xi + 1) * 4, :],
            in_=dr["x"].rearrange("(t p) d -> p t d", p=128)[:, xi * 4:(xi + 1) * 4, :]),
            writes=xb[xi * 4:(xi + 1) * 4])
    S.dma("sp", lambda e: e.dma_start(out=ident_bf[:], in_=dr["ident_bf"]), writes=[B("identb")])
    S.emit("dve", lambda v: v.memset(ones_bf[:], 1.0), writes=[B("ones")])
    S.emit("dve", lambda v: v.memset(qconst[:], 65.0), writes=[B("qconst")])
    S.emit("dve", lambda v: v.memset(qconst2[:], VELT), writes=[B("qconst")])

    def make_qt(stk, tag, width, with_t1=True):
        qt = {}
        names = ("t0", "t1", "t2", "xn") if with_t1 else ("t0", "t2", "xn")
        for n in names:
            qt[n] = stk.enter_context(nc.sbuf_tensor(f"{tag}_{n}", [128, width], F32))
        for n in ("bmax", "scale", "inv"):
            qt[n] = stk.enter_context(nc.sbuf_tensor(f"{tag}_{n}", [128, width // 16], F32))
        qt["bufs"] = {n: B(f"{tag}_{n}") for n in names + ("bmax", "scale", "inv")}
        qt["c65"] = qconst
        qt["cVELT"] = qconst2
        return qt

    def quant_chain_ops(qt, in_ap, in_bufs, out_ap, out_buf, c0, c1, tag, bf=False):
        ops = []
        Q = qt["bufs"]

        def key(n):
            return Q[n] if tag == 0 else B(f"{Q[n].name}_h{tag}")
        b0, b1 = c0 // 16, c1 // 16
        nblk = b1 - b0
        ncols = c1 - c0
        bmax = qt["bmax"][:, b0:b1]
        scale = qt["scale"][:, b0:b1]
        inv = qt["inv"][:, b0:b1]
        if bf:
            t0 = qt["t0"].ap().bitcast(BF16)[:, c0:c1]
            t1 = qt["t1"].ap().bitcast(BF16)[:, c0:c1]
            t2 = qt["t2"].ap().bitcast(BF16)[:, c0:c1]
            xn = qt["xn"].ap().bitcast(BF16)[:, c0:c1]
            m_u8 = qt["xn"].ap().bitcast(mybir.dt.uint8)[:, c0 * 2:c0 * 2 + ncols]
            velt_scale = qt["c65"]
            fixc = 96.0
        else:
            t0 = qt["t0"][:, c0:c1]
            t1 = qt["t1"][:, c0:c1]
            t2 = qt["t2"][:, c0:c1]
            xn = qt["xn"][:, c0:c1]
            m_u8 = qt["xn"].ap().bitcast(mybir.dt.uint8)[:, c0 * 4:c0 * 4 + ncols]
            velt_scale = qt["cVELT"]
            fixc = FIXC
        inap = in_ap[:, c0:c1]
        outap = out_ap[:, c0:c1]
        ing = inap.rearrange("p (b s) -> p b s", s=16)
        invb = inv.unsqueeze(-1).broadcast_to([128, nblk, 16])
        scb = scale.unsqueeze(-1).broadcast_to([128, nblk, 16])
        xng = xn.rearrange("p (b s) -> p b s", s=16)
        outg = outap.rearrange("p (b s) -> p b s", s=16)
        qg = t0.rearrange("p (b s) -> p b s", s=16)
        E = ops.append
        E(("dve", lambda v: v.reduce_max(out=bmax, in_=ing, axis=AX.X,
                                         apply_absolute_value=True),
           list(in_bufs) + base("bmax"), [key("bmax")]))
        E(("dve", lambda v: v.tensor_scalar(out=scale, in0=bmax, scalar1=1.0 / 6.0,
                                            scalar2=1e-12, op0=AT.mult, op1=AT.max),
           [key("bmax")] + base("bmax") + base("scale"), [key("scale")]))
        E(("dve", lambda v: v.reciprocal(out=inv, in_=scale),
           [key("scale")] + base("scale") + base("inv"), [key("inv")]))
        E(("dve", lambda v: v.tensor_tensor(out=xng, in0=ing, in1=invb, op=AT.mult),
           list(in_bufs) + [key("inv")], [key("xn")]))
        E(("act", lambda a: a.activation(out=t0, in_=xn, func=AF.Copy,
                                         scale=velt_scale[:, 0:1]),
           [key("xn"), B("qconst")], [key("t0")]))
        E(("dve", lambda v: v.tensor_sub(out=t1, in0=t0, in1=xn),
           [key("t0"), key("xn")], [key("t1")]))
        E(("dve", lambda v: v.tensor_sub(out=t2, in0=t0, in1=t1),
           [key("t0"), key("t1")], [key("t2")]))
        if bf:
            E(("dve", lambda v: v.tensor_single_scalar(out=t0, in_=xn, scalar=fixc, op=AT.add),
               [key("xn")], [key("t0")]))
            E(("dve", lambda v: v.tensor_single_scalar(out=t0, in_=t0, scalar=fixc,
                                                       op=AT.subtract),
               [key("t0")], [key("t0")]))
        else:
            E(("dve", lambda v: v.tensor_scalar(out=t0, in0=xn, scalar1=fixc, scalar2=fixc,
                                                op0=AT.add, op1=AT.subtract),
               [key("xn")], [key("t0")]))
        E(("act", lambda a: a.activation(out=t1, in_=xn, func=AF.Abs),
           [key("xn")], [key("t1")]))
        E(("dve", lambda v: v.tensor_single_scalar(out=m_u8, in_=t1, scalar=1.0, op=AT.is_gt),
           [key("t1")], [key("xn")]))
        E(("dve", lambda v: v.copy_predicated(out=t0, mask=m_u8, data=t2),
           [key("xn"), key("t2"), key("t0")], [key("t0")]))
        E(("dve", lambda v: v.tensor_tensor(out=outg, in0=qg, in1=scb, op=AT.mult),
           [key("t0"), key("scale")], [out_buf]))
        return ops

    def qt_all(qt, n):
        nm = qt["bufs"][n].name
        return [qt["bufs"][n], B(f"{nm}_h1"), B(f"{nm}_h2")]

    def quant_chain_ops_v2(qt, in_ap, in_bufs, out_ap, out_buf, c0, c1, tag, bf=True,
                           pool_heavy=False):
        ops = []
        Q = qt["bufs"]

        def key(n):
            return Q[n] if tag == 0 else B(f"{Q[n].name}_h{tag}")

        def base(n):
            return []
        b0, b1 = c0 // 16, c1 // 16
        nblk = b1 - b0
        bmax = qt["bmax"][:, b0:b1]
        scale = qt["scale"][:, b0:b1]
        inv = qt["inv"][:, b0:b1]
        if bf:
            co2, W = 2 * c0, c1 - c0
            t0i = qt["t0"].ap().bitcast(mybir.dt.int16)[:, co2:co2 + W]
            t0p = qt["t0"].ap().bitcast(BF16)[:, co2:co2 + W]
            t2i = qt["t2"].ap().bitcast(mybir.dt.int16)[:, co2:co2 + W]
            t2x = qt["t2"].ap().bitcast(BF16)[:, co2:co2 + W]
            mask = 32640
        else:
            t0i = qt["t0"].ap().bitcast(mybir.dt.int32)[:, c0:c1]
            t0p = qt["t0"][:, c0:c1]
            t2i = qt["t2"].ap().bitcast(mybir.dt.int32)[:, c0:c1]
            t2x = qt["t2"][:, c0:c1]
            mask = 2139095040
        t0f = qt["t0"][:, c0:c1]
        csf = qt["xn"][:, c0:c1]
        inap = in_ap[:, c0:c1]
        outap = out_ap[:, c0:c1]
        ing = inap.rearrange("p (b s) -> p b s", s=16)
        invb = inv.unsqueeze(-1).broadcast_to([128, nblk, 16])
        scb = scale.unsqueeze(-1).broadcast_to([128, nblk, 16])
        xg = t2x.rearrange("p (b s) -> p b s", s=16)
        outg = outap.rearrange("p (b s) -> p b s", s=16)
        RC = 12582912.0
        E = ops.append
        E(("dve", lambda v: v.reduce_max(out=bmax, in_=ing, axis=AX.X,
                                         apply_absolute_value=True),
           list(in_bufs) + base("bmax"), [key("bmax")]))
        E(("dve", lambda v: v.tensor_scalar(out=scale, in0=bmax, scalar1=1.0 / 6.0,
                                            scalar2=1e-12, op0=AT.mult, op1=AT.max),
           [key("bmax")] + base("bmax") + base("scale"), [key("scale")]))
        E(("dve", lambda v: v.reciprocal(out=inv, in_=scale),
           [key("scale")] + base("scale") + base("inv"), [key("inv")]))
        E(("pool", lambda g: g.tensor_tensor(out=xg, in0=ing, in1=invb, op=AT.mult),
           list(in_bufs) + [key("inv")] + base("inv") + base("t2"), [key("t2")]))
        E(("dve", lambda v: v.tensor_single_scalar(out=t0i, in_=t2i, scalar=mask,
                                                   op=AT.bitwise_and),
           [key("t2")] + base("t2") + base("t0"), [key("t0")]))
        E(("dve", lambda v: v.tensor_scalar(out=csf, in0=t0p, scalar1=RC * 0.5,
                                            scalar2=RC * 0.5, op0=AT.mult, op1=AT.max),
           [key("t0")] + base("t0") + base("xn"), [key("xn")]))
        E(("pool", lambda g: g.tensor_tensor(out=t0f, in0=t2x, in1=csf, op=AT.add),
           [key("t2"), key("xn")] + base("t2") + base("xn") + base("t0"), [key("t0")]))
        e_sub = "pool" if pool_heavy else "dve"
        E((e_sub, lambda v: v.tensor_tensor(out=t2x, in0=t0f, in1=csf, op=AT.subtract),
           [key("t0"), key("xn")] + base("t0") + base("xn") + base("t2"), [key("t2")]))
        E((e_sub, lambda v: v.tensor_tensor(out=outg, in0=xg, in1=scb, op=AT.mult),
           [key("t2"), key("scale")] + base("t2") + base("scale"), [out_buf]))
        return ops

    def emit_chains(*chains):
        chains = [list(c) for c in chains]
        idx = [0] * len(chains)
        while True:
            done = True
            for ci, c in enumerate(chains):
                if idx[ci] < len(c):
                    eng, fn, rd, wr = c[idx[ci]]
                    S.emit(eng, fn, reads=rd, writes=wr)
                    idx[ci] += 1
                    done = False
            if done:
                break

    def quant_emit(qt, in_ap, in_bufs, out_ap, out_buf, ncols, nsplit=2, bf=False,
                   pool_heavy=False):
        nb = ncols // 16
        half_b = (nb // (2 * nsplit)) * 2 if nsplit > 1 else nb
        cuts = [0]
        for i in range(nsplit - 1):
            cuts.append(min(ncols, (i + 1) * (nb // nsplit) * 16))
        cuts.append(ncols)
        chains = []
        for i in range(nsplit):
            if cuts[i] < cuts[i + 1]:
                chains.append(quant_chain_ops_v2(qt, in_ap, in_bufs, out_ap, out_buf,
                                                 cuts[i], cuts[i + 1], i, bf=bf,
                                                 pool_heavy=pool_heavy))
        emit_chains(*chains)

    with (nc.sbuf_tensor("identf", [128, 128], F32) as ident_f,
          nc.sbuf_tensor("c_lhs1", [2, 16], F32) as c_lhs1,
          nc.sbuf_tensor("c_ones7", [1, 7], F32) as c_ones7,
          nc.sbuf_tensor("c_onesf", [1, 128], F32) as c_onesf,
          nc.sbuf_tensor("c_rhs1", [2, 512], F32) as c_rhs1,
          nc.sbuf_tensor("c_h1", [7, 512], F32) as c_h1,
          nc.sbuf_tensor("c_h1T", [32, H, 7], F32) as c_h1T,
          nc.sbuf_tensor("c_w2", [32, H, 32], F32) as c_w2,
          nc.sbuf_tensor("c_w2T", [32, H, 32], F32) as c_w2T,
          nc.sbuf_tensor("c_cb2", [1, 512], F32) as c_cb2,
          nc.sbuf_tensor("c_h2", [32, H, 7], F32) as c_h2,
          nc.sbuf_tensor("c_w3T", [32, H], F32) as c_w3T,
          nc.sbuf_tensor("c_cb3", [1, H], F32) as c_cb3,
          ExitStack() as p_stk):
        c_cwt = p_stk.enter_context(nc.sbuf_tensor("c_cwt", [1, H, 7], F32))
        c_max = p_stk.enter_context(nc.sbuf_tensor("c_max", [1, H], F32))
        c_conn = p_stk.enter_context(nc.sbuf_tensor("c_conn", [1, H, 7], F32))
        c_connb = p_stk.enter_context(nc.sbuf_tensor("c_connb", [128, H, 7], F32))
        c_dj = p_stk.enter_context(nc.sbuf_tensor("c_dj", [128, 134], F32))
        c_mask = p_stk.enter_context(nc.sbuf_tensor("c_mask", [128, 134], F32))
        conn_d = p_stk.enter_context(nc.sbuf_tensor("conn_bld", [128, H, 134], BF16))
        c_ps = PB[:, 0:512]
        c_ps2 = PB[:, 512:1024]
        S.dma("sp", lambda e: e.dma_start(out=ident_f[:], in_=dr["ident_f"]), writes=[B("identf")])
        S.emit("dve", lambda v: v.memset(c_lhs1[:], 1.0), writes=[B("c_lhs1")])
        S.emit("dve", lambda v: v.memset(c_ones7[:], 1.0), writes=[B("c_ones7")])
        S.emit("dve", lambda v: v.memset(c_onesf[:], 1.0), writes=[B("c_onesf")])
        S.emit("pool", lambda g: g.iota(c_dj[0:1, 0:7], pattern=[[1, 7]], base=0,
                                        channel_multiplier=0, allow_small_or_imprecise_dtypes=True),
               writes=[B("c_dj")])
        S.emit("dve", lambda v: v.tensor_single_scalar(out=c_lhs1[0:1, 0:7], in_=c_dj[0:1, 0:7],
                                                       scalar=1.0 / 6.0, op=AT.mult),
               reads=[B("c_dj")], writes=[B("c_lhs1")])
        S.dma("sp", lambda e: e.dma_start(out=c_rhs1[0:1, :],
                                          in_=dr["cw1"].rearrange("h j -> (h j)").rearrange("(o f) -> o f", o=1)),
              writes=[B("c_rhs1")])
        S.dma("sp", lambda e: e.dma_start(out=c_rhs1[1:2, :],
                                          in_=dr["cb1"].rearrange("h j -> (h j)").rearrange("(o f) -> o f", o=1)),
              writes=[B("c_rhs1")])
        S.emit("te", lambda t: t.matmul(c_ps[0:7, :], c_lhs1[:, 0:7], c_rhs1[:],
                                        start=True, stop=True),
               reads=[B("c_lhs1"), B("c_rhs1")], writes=pb(0, 512))
        S.emit("act", lambda a: a.activation(out=c_h1[:], in_=c_ps[0:7, :], func=AF.Gelu),
               reads=pb(0, 512), writes=[B("c_h1")])
        for h in range(H):
            S.emit("te", lambda t, h=h: t.transpose(c_ps2[0:32, h * 7:(h + 1) * 7],
                                                    c_h1[:, h * 32:(h + 1) * 32],
                                                    ident_f[0:7, 0:7]),
                   reads=[B("c_h1"), B("identf")], writes=pb(512, 1024))
        S.emit("dve", lambda v: v.tensor_copy(out=c_h1T[:].rearrange("p h w -> p (h w)"),
                                              in_=c_ps2[0:32, 0:H * 7]),
               reads=pb(512, 1024), writes=[B("c_h1T")])
        S.dma("sp", lambda e: e.dma_start(out=c_w2[:], in_=dr["cw2"].rearrange("h j i -> j h i")),
              writes=[B("c_w2")])
        for h in range(H):
            S.emit("te", lambda t, h=h: t.transpose(c_ps2[0:32, h * 32:(h + 1) * 32],
                                                    c_w2[:, h, :], ident_f[0:32, 0:32]),
                   reads=[B("c_w2"), B("identf")], writes=pb(512, 1024))
        S.emit("dve", lambda v: v.tensor_copy(out=c_w2T[:].rearrange("p h i -> p (h i)"),
                                              in_=c_ps2[0:32, 0:H * 32]),
               reads=pb(512, 1024), writes=[B("c_w2T")])
        S.dma("sp", lambda e: e.dma_start(out=c_cb2[:],
                                          in_=dr["cb2"].rearrange("h j -> (h j)").rearrange("(o f) -> o f", o=1)),
              writes=[B("c_cb2")])
        for h in range(H):
            S.emit("te", lambda t, h=h: t.matmul(c_ps[0:32, h * 7:(h + 1) * 7], c_w2T[:, h, :],
                                                 c_h1T[:, h, :], start=True, stop=False),
                   reads=[B("c_w2T"), B("c_h1T")], writes=pb(0, 512))
            S.emit("te", lambda t, h=h: t.matmul(c_ps[0:32, h * 7:(h + 1) * 7],
                                                 c_cb2[:, h * 32:(h + 1) * 32], c_ones7[:],
                                                 start=False, stop=True),
                   reads=[B("c_cb2"), B("c_ones7")], writes=pb(0, 512))
        S.emit("act", lambda a: a.activation(out=c_h2[:].rearrange("p h w -> p (h w)"),
                                             in_=c_ps[0:32, 0:H * 7], func=AF.Gelu),
               reads=pb(0, 512), writes=[B("c_h2")])
        S.dma("sp", lambda e: e.dma_start(out=c_w3T[:], in_=dr["cw3"].rearrange("h i -> i h"),
                                          allow_slow_non_contiguous=True),
              writes=[B("c_w3T")])
        S.dma("sp", lambda e: e.dma_start(out=c_cb3[:], in_=dr["cb3"].rearrange("(o h) -> o h", o=1)),
              writes=[B("c_cb3")])
        for h in range(H):
            S.emit("te", lambda t, h=h: t.matmul(c_ps2[0:1, h * 7:(h + 1) * 7], c_w3T[:, h:h + 1],
                                                 c_h2[:, h, :], start=True, stop=True),
                   reads=[B("c_w3T"), B("c_h2")], writes=pb(512, 1024))
        cb3b = c_cb3[:].unsqueeze(-1).broadcast_to([1, H, 7])
        S.emit("dve", lambda v: v.tensor_tensor(
            out=c_cwt[:], in0=c_ps2[0:1, 0:H * 7].rearrange("p (h w) -> p h w", w=7),
            in1=cb3b, op=AT.add),
            reads=pb(512, 1024) + [B("c_cb3")], writes=[B("c_cwt")])
        S.emit("dve", lambda v: v.reduce_max(out=c_max[:], in_=c_cwt[:], axis=AX.X),
               reads=[B("c_cwt")], writes=[B("c_max")])
        cmaxb = c_max[:].unsqueeze(-1).broadcast_to([1, H, 7])
        S.emit("dve", lambda v: v.tensor_tensor(out=c_cwt[:], in0=c_cwt[:], in1=cmaxb,
                                                op=AT.subtract),
               reads=[B("c_cwt"), B("c_max")], writes=[B("c_cwt")])
        S.emit("act", lambda a: a.activation(out=c_cwt[:], in_=c_cwt[:], func=AF.Exp),
               reads=[B("c_cwt")], writes=[B("c_cwt")])
        S.emit("dve", lambda v: v.reduce_sum(out=c_max[:], in_=c_cwt[:], axis=AX.X),
               reads=[B("c_cwt")], writes=[B("c_max")])
        S.emit("dve", lambda v: v.reciprocal(out=c_max[:], in_=c_max[:]),
               reads=[B("c_max")], writes=[B("c_max")])
        csumb = c_max[:].unsqueeze(-1).broadcast_to([1, H, 7])
        S.emit("dve", lambda v: v.tensor_tensor(out=c_conn[:], in0=c_cwt[:], in1=csumb,
                                                op=AT.mult),
               reads=[B("c_cwt"), B("c_max")], writes=[B("c_conn")])
        S.emit("te", lambda t: t.matmul(c_ps[:, 0:H * 7], c_onesf[:],
                                        c_conn[:].rearrange("p h w -> p (h w)"),
                                        start=True, stop=True),
               reads=[B("c_onesf"), B("c_conn")], writes=pb(0, 512))
        S.emit("dve", lambda v: v.tensor_copy(out=c_connb[:].rearrange("p h w -> p (h w)"),
                                              in_=c_ps[:, 0:H * 7]),
               reads=pb(0, 512), writes=[B("c_connb")])
        S.emit("pool", lambda g: g.iota(c_dj[:], pattern=[[1, 134]], base=0,
                                        channel_multiplier=-1, allow_small_or_imprecise_dtypes=True),
               writes=[B("c_dj")])
        S.emit("dve", lambda v: v.memset(conn_d[:], 0.0),
               writes=[B("conn_d")])
        for w in range(WIN):
            S.emit("dve", lambda v, w=w: v.tensor_single_scalar(out=c_mask[:], in_=c_dj[:],
                                                                scalar=float(w), op=AT.is_equal),
                   reads=[B("c_dj")], writes=[B("c_mask")])
            for h in range(H):
                S.emit("dve", lambda v, w=w, h=h: v.scalar_tensor_tensor(
                    out=conn_d[:, h, :], in0=c_mask[:], scalar=c_connb[:, h, w:w + 1],
                    in1=conn_d[:, h, :], op0=AT.mult, op1=AT.add),
                    reads=[B("c_mask"), B("c_connb"), B("conn_d")], writes=[B("conn_d")])
        S.dma("sp", lambda e: e.dma_start(out=dr["connd"],
                                          in_=conn_d[:].rearrange("p h w -> p (h w)")),
              reads=[B("conn_d")])

    S.barrier()

    wgT_s = [B(f"wgT_s{s}") for s in range(22)]
    wupT_s = [B(f"wupT_s{s}") for s in range(22)]
    wdT_s = [[B(f"wdT_s{s}_{seg}") for seg in range(NQ)] for s in range(8)]

    def emit_ln_tile(t):
        par = t % 2
        S.emit("dve", lambda v, t=t, par=par: v.bn_stats(
            out=ln_bn[:, par, 0:6], in_=x_res[:, t, 0:512]),
            reads=[xb[t]], writes=[B(f"ln_bn{par}")])
        S.emit("dve", lambda v, t=t, par=par: v.bn_stats(
            out=ln_bn[:, par, 6:12], in_=x_res[:, t, 512:1024]),
            reads=[xb[t]], writes=[B(f"ln_bn{par}")])
        S.emit("dve", lambda v, t=t, par=par: v.bn_aggr(
            out=ln_mv[:, t, :], in_=ln_bn[:, par, :]),
            reads=[B(f"ln_bn{par}")], writes=[B("ln_mv")])

    def emit_ln_final():
        S.emit("dve", lambda v: v.tensor_single_scalar(
            out=ln_var[:], in_=ln_mv[:, :, 1], scalar=1e-5, op=AT.add),
            reads=[B("ln_mv")], writes=[B("ln_var")])
        S.emit("act", lambda a: a.activation(out=ln_rstd[:], in_=ln_var[:], func=AF.Ln),
               reads=[B("ln_var")], writes=[B("ln_rstd")])
        S.emit("dve", lambda v: v.tensor_single_scalar(out=ln_rstd[:], in_=ln_rstd[:],
                                                       scalar=-0.5, op=AT.mult),
               reads=[B("ln_rstd")], writes=[B("ln_rstd")])
        S.emit("act", lambda a: a.activation(out=ln_rstd[:], in_=ln_rstd[:], func=AF.Exp),
               reads=[B("ln_rstd")], writes=[B("ln_rstd")])

    def emit_ln_stats():
        for t in range(NT):
            emit_ln_tile(t)
        emit_ln_final()

    def store_tile(t):
        qe = ("sp", "act", "pool")[t % 3]
        S.dma(qe, lambda e, t=t: e.dma_start(
            out=dr["out"].rearrange("(t p) d -> p t d", p=128)[:, t, :],
            in_=x_res[:, t, :]), reads=[xb[t]])

    def emit_ffn(phase, with_prep=False, store_out=False, skip_ln=False):
        with (nc.sbuf_tensor(f"f{phase}_wg", [128, 2, 8, 768], BF16) as wg_sb,
              nc.sbuf_tensor(f"f{phase}_wu", [128, 2, 8, 768], BF16) as wu_sb,
              nc.sbuf_tensor(f"f{phase}_wd", [128, 6, D], BF16) as wd_sb,
              nc.sbuf_tensor(f"f{phase}_xqT", [128, NT, 8, 128], BF16) as xqT_sb,
              nc.sbuf_tensor(f"f{phase}_xq", [128, D], BF16) as xq_sb,
              nc.sbuf_tensor(f"f{phase}_h", [128, 2, 768], BF16) as h_sb,
              nc.sbuf_tensor(f"f{phase}_hq", [128, 2, 768], BF16) as hq_sb,
              nc.sbuf_tensor(f"f{phase}_hqT", [128, 2, 6, 128], BF16) as hqT_sb,
              ExitStack() as f_stk):
            qt2 = [make_qt(f_stk, f"f{phase}qa", D, with_t1=False),
                   make_qt(f_stk, f"f{phase}qb", 768, with_t1=False)]
            qt = qt2[0]
            if with_prep:
                qt_wp = qt2[1]
                WPD = 4
                wp_in = f_stk.enter_context(nc.sbuf_tensor("wp_in", [128, WPD, 768], F32))
                wp_q = f_stk.enter_context(nc.sbuf_tensor("wp_q", [128, 2, 768], BF16))
                wp_st = f_stk.enter_context(nc.sbuf_tensor("wp_st", [128, 2, 4, 128], BF16))
                win_b = [B(f"wp_in{i}") for i in range(WPD)]
                wq_b = [B("wp_q0"), B("wp_q1")]
                wp_tps = PC[:, 512:1024].bitcast(BF16)
                piece_ctr = [0]
                st_ctr = [0]
                slice_jobs = []
                for s in range(2):
                    slice_jobs.append(("w_gate", s))
                    slice_jobs.append(("w_up", s))
                for s in range(8):
                    slice_jobs.append(("w_down", (s, 0)))
                need_at = {}
                gu_done = 2
                for seg in range(1, NQ):
                    for s in range(8):
                        slice_jobs.append(("w_down", (s, seg)))
                    es = (FSL[seg][0] + FSL[seg][1]) // 128
                    for s in range(gu_done, es):
                        slice_jobs.append(("w_gate", s))
                        slice_jobs.append(("w_up", s))
                    gu_done = es
                    need_at[seg] = len(slice_jobs)

                def prep_job(j):
                    name, s = slice_jobs[j]
                    coff = 0
                    if name == "w_down":
                        s, seg = s
                        coff, ncols = FSL[seg]
                        src_t, dst, dstb = dr["w_down"], dr["wdT"], wdT_s[s][seg]
                    elif name == "w_gate":
                        src_t, dst, dstb, ncols = dr["w_gate"], dr["wgT"], wgT_s[s], D
                    else:
                        src_t, dst, dstb, ncols = dr["w_up"], dr["wupT"], wupT_s[s], D
                    pieces = []
                    for p0 in range(0, ncols, 384):
                        p1 = min(ncols, p0 + 384)
                        pn = p1 - p0
                        wi = piece_ctr[0] % WPD
                        k = piece_ctr[0] % 2
                        s2 = (piece_ctr[0] // 2) % 2
                        piece_ctr[0] += 1
                        cq0 = k * 384
                        qe = ("act", "pool")[piece_ctr[0] % 2]
                        S.dma(qe, lambda e, src_t=src_t, s=s, p0=p0 + coff, pn=pn,
                              wi=wi, cq0=cq0: e.dma_start(
                            out=wp_in[:, wi, cq0:cq0 + pn],
                            in_=src_t[s * 128:(s + 1) * 128, p0:p0 + pn]),
                            writes=[win_b[wi]])
                        wqk = B(f"wp_q{s2}_{k}")
                        chain = quant_chain_ops_v2(
                            qt_wp, wp_in[:, wi, :], [win_b[wi]], wp_q[:, s2, :],
                            wqk, cq0, cq0 + pn, k, bf=BF_W)

                        def post(s=s, p0=p0, pn=pn, coff=coff, s2=s2, cq0=cq0,
                                 wqk=wqk, dst=dst, dstb=dstb):
                            nch = pn // 128
                            for c in range(nch):
                                S.emit("te", lambda t, c=c, s2=s2, cq0=cq0: t.transpose(
                                    wp_tps[:, c * 128:(c + 1) * 128],
                                    wp_q[:, s2, cq0 + c * 128:cq0 + (c + 1) * 128],
                                    ident_bf[:]),
                                    reads=[wqk, B("identb")],
                                    writes=pc(512, 1024) if (c == 0 or c == nch - 1) else ())
                            gp = st_ctr[0] % 2
                            st_ctr[0] += 1
                            wstb2 = B(f"wp_st{gp}")
                            S.emit("dve", lambda v, nch=nch, gp=gp: v.tensor_copy(
                                out=wp_st[:, gp, 0:nch, :].rearrange("p c n -> p (c n)"),
                                in_=wp_tps[:, 0:nch * 128]),
                                reads=pc(512, 1024), writes=[wstb2])
                            ca = (coff + p0) // 128
                            dst_ap = dst[ca * 128:(ca + nch) * 128,
                                         s * 128:(s + 1) * 128].rearrange("(c p) n -> p c n", p=128)
                            S.dma("sp", lambda e, dst_ap=dst_ap, nch=nch, gp=gp: e.dma_start(
                                out=dst_ap, in_=wp_st[:, gp, 0:nch, :]),
                                reads=[wstb2], writes=[dstb])
                        pieces.append((chain, post))
                    return pieces
            if not skip_ln:
                emit_ln_stats()
            g_ps = PA[:, 0:1024]
            u_ps = PA[:, 1024:2048]
            d_ps = PB[:]
            t_ps = PC[:, 0:512].bitcast(BF16)
            t_ps2 = PC[:, 512:1024].bitcast(BF16)
            wgb = [[B(f"f{phase}_wg{pp}_{qi}") for qi in range(3)] for pp in range(2)]
            wub = [[B(f"f{phase}_wu{pp}_{qi}") for qi in range(3)] for pp in range(2)]
            wdb = [B(f"f{phase}_wd{qi}") for qi in range(3)]
            xqTb = [B(f"f{phase}_xqT{t}") for t in range(NT)]
            hb = [B(f"f{phase}_h0"), B(f"f{phase}_h1")]
            hqb = [B(f"f{phase}_hq0"), B(f"f{phase}_hq1")]

            def load_gu(q, pp):
                off, ncols = FSL[q]
                sl0, sl1 = off // 128, (off + ncols) // 128
                for dst, bl, src_, sbl in ((wg_sb, wgb, dr["wgT"], wgT_s),
                                           (wu_sb, wub, dr["wupT"], wupT_s)):
                    src_ap = src_[:, off:off + ncols].rearrange("(k p) n -> p k n", p=128)
                    c3 = ncols // 3 // 16 * 16
                    for qi, (qe, d0, d1) in enumerate((("sp", 0, c3), ("act", c3, 2 * c3),
                                                       ("pool", 2 * c3, ncols))):
                        S.dma(qe, lambda e, dst=dst, pp=pp, src_ap=src_ap, d0=d0, d1=d1: e.dma_start(
                            out=dst[:, pp, :, d0:d1], in_=src_ap[:, :, d0:d1]),
                            reads=sbl[sl0:sl1], writes=[bl[pp][qi]])

            def load_wd(q):
                off, ncols = FSL[q]
                nch = ncols // 128
                src_ap = dr["wdT"][off:off + ncols, :].rearrange("(c p) n -> p c n", p=128)
                for qi, (qe, d0, d1) in enumerate((("sp", 0, nch // 3),
                                                   ("act", nch // 3, 2 * nch // 3),
                                                   ("pool", 2 * nch // 3, nch))):
                    if d0 < d1:
                        S.dma(qe, lambda e, src_ap=src_ap, d0=d0, d1=d1: e.dma_start(
                            out=wd_sb[:, d0:d1, :], in_=src_ap[:, d0:d1, :]),
                            reads=[wdT_s[s2][q] for s2 in range(8)], writes=[wdb[qi]])

            def load_ws(q, pp):
                load_gu(q, pp)
                load_wd(q)

            def s1_quant(t):
                qtp = qt2[0]
                xqb = B(f"f{phase}_xqb")
                S.emit("dve", lambda v, t=t, qtp=qtp: v.tensor_scalar(
                    out=qtp["xn"][:, 0:D], in0=x_res[:, t, :],
                    scalar1=ln_mv[:, t, 0:1], scalar2=ln_rstd[:, t:t + 1],
                    op0=AT.subtract, op1=AT.mult),
                    reads=[xb[t], B("ln_mv"), B("ln_rstd")],
                    writes=[qtp["bufs"]["xn"]])
                quant_emit(qtp, qtp["xn"][:, 0:D], [qtp["bufs"]["xn"]], xq_sb[:],
                           xqb, D, bf=BF_X)
                for c in range(8):
                    S.emit("te", lambda te, c=c: te.transpose(
                        t_ps2[:, c * 128:(c + 1) * 128], xq_sb[:, c * 128:(c + 1) * 128],
                        ident_bf[:]),
                        reads=[xqb, B("identb")],
                        writes=pc(512, 1024) if c in (0, 7) else ())
                S.emit("dve", lambda v, t=t: v.tensor_copy(
                    out=xqT_sb[:, t, :, :].rearrange("p k n -> p (k n)"),
                    in_=t_ps2[:, 0:1024]),
                    reads=pc(512, 1024), writes=[xqTb[t]])

            def s2_gateup(q, t, pp):
                off, ncols = FSL[q]
                nmm = (ncols + 511) // 512
                for mi in range(nmm):
                    n0 = mi * 512
                    n1 = min(ncols, n0 + 512)
                    for kc in range(8):
                        edge = (kc == 0 or kc == 7)
                        S.emit("te", lambda te, t=t, kc=kc, n0=n0, n1=n1, pp=pp: te.matmul(
                            g_ps[:, n0:n1], xqT_sb[:, t, kc, :], wg_sb[:, pp, kc, n0:n1],
                            start=(kc == 0), stop=(kc == 7)),
                            reads=[xqTb[t]] + wgb[pp], writes=pa(n0, n1) if edge else ())
                        S.emit("te", lambda te, t=t, kc=kc, n0=n0, n1=n1, pp=pp: te.matmul(
                            u_ps[:, n0:n1], xqT_sb[:, t, kc, :], wu_sb[:, pp, kc, n0:n1],
                            start=(kc == 0), stop=(kc == 7)),
                            reads=[xqTb[t]] + wub[pp], writes=pa(1024 + n0, 1024 + n1) if edge else ())

            def s3_silumul(q, t):
                off, ncols = FSL[q]
                par = t % 2
                for n0 in range(0, ncols, 512):
                    n1 = min(ncols, n0 + 512)
                    S.emit("act", lambda a, n0=n0, n1=n1, par=par: a.activation(
                        out=h_sb[:, par, n0:n1], in_=g_ps[:, n0:n1], func=AF.Silu),
                        reads=pa(n0, n1), writes=[hb[par]])
                    S.emit("dve", lambda v, n0=n0, n1=n1, par=par: v.tensor_mul(
                        out=h_sb[:, par, n0:n1], in0=h_sb[:, par, n0:n1],
                        in1=u_ps[:, n0:n1]),
                        reads=[hb[par]] + pa(1024 + n0, 1024 + n1), writes=[hb[par]])

            def stage_a(q, t):
                off, ncols = FSL[q]
                par = t % 2
                qtp = qt2[par]
                quant_emit(qtp, h_sb[:, par, 0:ncols], [hb[par]], hq_sb[:, par, 0:ncols],
                           hqb[par], ncols, bf=BF_H)

            def stage_c1(q, t, pp):
                off, ncols = FSL[q]
                nch = ncols // 128
                par = t % 2
                hqTp = B(f"f{phase}_hqT{par}")
                for c in range(nch):
                    S.emit("te", lambda te, c=c, par=par: te.transpose(
                        t_ps[:, c * 128:(c + 1) * 128], hq_sb[:, par, c * 128:(c + 1) * 128],
                        ident_bf[:]),
                        reads=[hqb[par], B("identb")],
                        writes=pc(0, 512) if c in (0, nch - 1) else ())
                S.emit("act", lambda a, nch=nch, par=par: a.activation(
                    out=hqT_sb[:, par, 0:nch, :].rearrange("p c n -> p (c n)"),
                    in_=t_ps[:, 0:nch * 128], func=AF.Copy),
                    reads=pc(0, 512), writes=[hqTp])
                for c in range(nch):
                    for mi in range(2):
                        n0 = mi * 512
                        edge = (c == 0 or c == nch - 1)
                        S.emit("te", lambda te, c=c, n0=n0, pp=pp, par=par, st=(c == 0),
                               sp_=(c == nch - 1): te.matmul(
                            d_ps[:, n0:n0 + 512], hqT_sb[:, par, c, :],
                            wd_sb[:, c, n0:n0 + 512],
                            start=st, stop=sp_),
                            reads=[hqTp] + wdb, writes=pb(n0, n0 + 512) if edge else ())

            def stage_res(q, t):
                S.emit("dve", lambda v, t=t: v.scalar_tensor_tensor(
                    out=x_res[:, t, :], in0=d_ps[:], scalar=0.5, in1=x_res[:, t, :],
                    op0=AT.mult, op1=AT.add),
                    reads=pb(0, 1024) + [xb[t]], writes=[xb[t]])

            next_job = [0]

            pend = []

            def prep_flush(all_=False):
                while len(pend) >= 2 or (all_ and pend):
                    pair = [pend.pop(0) for _ in range(min(2, len(pend)))]
                    emit_chains(*[c for c, _ in pair])
                    for _, post in pair:
                        post()

            def prep_steps(n, drain=False):
                if not with_prep:
                    return
                for _ in range(n):
                    if next_job[0] < len(slice_jobs):
                        pend.extend(prep_job(next_job[0]))
                        next_job[0] += 1
                        prep_flush()
                if drain:
                    prep_flush(all_=True)

            if with_prep:
                prep_steps(4, drain=True)
                load_gu(0, 0)
                for t in (0, 1, 2):
                    s1_quant(t)
                prep_steps(8, drain=True)
                load_wd(0)
            else:
                load_ws(0, 0)
                for t in (0, 1, 2):
                    s1_quant(t)
            for q in range(NQ):
                pp = q % 2
                s2_gateup(q, 0, pp)
                for t in range(NT):
                    s3_silumul(q, t)
                    if t + 1 < NT:
                        s2_gateup(q, t + 1, pp)
                    stage_a(q, t)
                    if t >= 1:
                        stage_c1(q, t - 1, pp)
                        stage_res(q, t - 1)
                        if store_out and q == NQ - 1:
                            store_tile(t - 1)
                    if q == 0 and t + 3 < NT:
                        s1_quant(t + 3)
                    prep_steps(1)
                    if q < NQ - 1 and t == 11:
                        if with_prep:
                            prep_steps(max(0, need_at[q + 1] - next_job[0]), drain=True)
                        load_gu(q + 1, (q + 1) % 2)
                stage_c1(q, NT - 1, pp)
                stage_res(q, NT - 1)
                if store_out and q == NQ - 1:
                    store_tile(NT - 1)
                if q < NQ - 1:
                    load_wd(q + 1)

    emit_ffn(1, with_prep=True)
    S.barrier()
    if debug:
        S.dma("act", lambda e: e.dma_start(out=dr["dbg_wgT"], in_=dr["wgT"]),
              reads=wgT_s)
        S.dma("sp", lambda e: e.dma_start(out=dr["dbg_ffn1"].rearrange("(t p) d -> p t d", p=128),
                                          in_=x_res[:]), reads=list(xb))

    with (nc.sbuf_tensor("a_wT", [128, 8, 1024], BF16) as wT_sb,
          nc.sbuf_tensor("a_K", [128, 8, 2056], BF16) as K_sb,
          nc.sbuf_tensor("a_Q", [128, 8, 2048], BF16) as Q_sb,
          nc.sbuf_tensor("a_Vs", [128, 17, 1024], BF16) as Vs_sb,
          ExitStack() as a_stk):
        emit_ln_stats()
        wTb = B("a_wT")
        Kb = B("a_K")
        Qb = B("a_Q")
        Vsb = B("a_Vs")
        a_tps = PC[:, 0:512].bitcast(BF16)

        S.emit("dve", lambda v: v.memset(K_sb[:, :, 0:3], 0.0),
               writes=[Kb])
        S.emit("dve", lambda v: v.memset(K_sb[:, :, 2051:2056], 0.0),
               writes=[Kb])
        Vsb_t = [B(f"a_Vs_{t}") for t in range(17)]
        S.emit("dve", lambda v: v.memset(Vs_sb[0:3, 0, :], 0.0), writes=[Vsb_t[0]])
        S.emit("dve", lambda v: v.memset(Vs_sb[:, 16, :], 0.0),
               writes=[Vsb_t[16]])

        def load_wT(name, wload_sb, wloadb, dst_sb=None, dstb=None):
            dst_sb = wT_sb if dst_sb is None else dst_sb
            dstb = wTb if dstb is None else dstb
            wlb2 = [B(f"{wloadb.name}_h0"), B(f"{wloadb.name}_h1")]
            for s in range(8):
                for qi, (qe, d0, d1) in enumerate((("pool", 0, 512), ("pool", 512, 1024))):
                    S.dma(qe, lambda e, name=name, s=s, d0=d0, d1=d1: e.dma_start(
                        out=wload_sb[:, d0:d1], in_=dr[name][s * 128:(s + 1) * 128, d0:d1]),
                        writes=[wlb2[qi]])
                for c in range(8):
                    S.emit("te", lambda te, c=c: te.transpose(
                        a_tps[:, c * 128:(c + 1) * 128], wload_sb[:, c * 128:(c + 1) * 128],
                        ident_bf[:]),
                        reads=[wlb2[c // 4], B("identb")],
                        writes=pc(0, 512) if c in (0, 7) else ())
                S.emit("dve", lambda v, s=s, dst_sb=dst_sb: v.tensor_copy(
                    out=dst_sb[:, :, s * 128:(s + 1) * 128],
                    in_=a_tps[:, 0:1024].rearrange("p (k n) -> p k n", n=128)),
                    reads=pc(0, 512), writes=[dstb])

        with (nc.sbuf_tensor("a_wload", [128, 1024], BF16) as wload_sb,
              nc.sbuf_tensor("a_xn2", [128, 1024], BF16) as xn2_sb,
              nc.sbuf_tensor("a_wT2", [128, 8, 1024], BF16) as wT2_sb,
              nc.sbuf_tensor("a_xn2T", [128, 8, 256], BF16) as xn2T_sb,
              nc.sbuf_tensor("a_vt", [128, 2, 1024], BF16) as vt_sb):
            wT2b = B("a_wT2")
            vtb = B("a_vt")
            wloadb = B("a_wload")
            xn2b = B("a_xn2")
            xn2Tb = B("a_xn2T")
            ps1 = PB[:, 0:512]
            ps2 = PB[:, 512:1024]

            def emit_xn2T(t, ti):
                S.emit("dve", lambda v, t=t: v.tensor_scalar(
                    out=xn2_sb[:], in0=x_res[:, t, :],
                    scalar1=ln_mv[:, t, 0:1], scalar2=ln_rstd[:, t:t + 1],
                    op0=AT.subtract, op1=AT.mult),
                    reads=[xb[t], B("ln_mv"), B("ln_rstd")], writes=[xn2b])
                for c in range(8):
                    S.emit("te", lambda te, c=c: te.transpose(
                        a_tps[:, c * 128:(c + 1) * 128], xn2_sb[:, c * 128:(c + 1) * 128],
                        ident_bf[:]),
                        reads=[xn2b, B("identb")],
                        writes=pc(0, 512) if c in (0, 7) else ())
                S.emit("dve", lambda v, ti=ti: v.tensor_copy(
                    out=xn2T_sb[:, :, ti * 128:(ti + 1) * 128],
                    in_=a_tps[:, 0:1024].rearrange("p (k n) -> p k n", n=128)),
                    reads=pc(0, 512), writes=[xn2Tb])

            def kq_pass():
                load_wT("wk", wload_sb, wloadb)
                load_wT("wq", wload_sb, wloadb, wT2_sb, wT2b)
                for g in range(8):
                    for ti in range(2):
                        emit_xn2T(g * 2 + ti, ti)
                    for sl in range(8):
                        for kc in range(8):
                            S.emit("te", lambda te, sl=sl, kc=kc: te.matmul(
                                ps1[:, 0:256], wT_sb[:, kc, sl * 128:(sl + 1) * 128],
                                xn2T_sb[:, kc, :],
                                start=(kc == 0), stop=(kc == 7)),
                                reads=[wTb, xn2Tb], writes=pb(0, 512) if kc in (0, 7) else ())
                        for kc in range(8):
                            S.emit("te", lambda te, sl=sl, kc=kc: te.matmul(
                                ps2[:, 0:256], wT2_sb[:, kc, sl * 128:(sl + 1) * 128],
                                xn2T_sb[:, kc, :],
                                start=(kc == 0), stop=(kc == 7)),
                                reads=[wT2b, xn2Tb], writes=pb(512, 1024) if kc in (0, 7) else ())
                        S.emit("dve", lambda v, sl=sl, g=g: v.tensor_copy(
                            out=K_sb[:, sl, 3 + g * 256: 3 + (g + 1) * 256], in_=ps1[:, 0:256]),
                            reads=pb(0, 512), writes=[Kb])
                        S.emit("dve", lambda v, sl=sl, g=g: v.tensor_scalar(
                            out=Q_sb[:, sl, g * 256:(g + 1) * 256], in0=ps2[:, 0:256],
                            scalar1=0.125, scalar2=None, op0=AT.mult),
                            reads=pb(512, 1024), writes=[Qb])

            kq_pass()
            load_wT("wv", wload_sb, wloadb)
            for g in range(8):
                for ti in range(2):
                    t = g * 2 + ti
                    emit_xn2T(t, ti)
                    vp = t % 2
                    vtb2 = B(f"a_vt{vp}")
                    for half in range(2):
                        ps = ps1 if half == 0 else ps2
                        bnk = pb(0, 512) if half == 0 else pb(512, 1024)
                        for kc in range(8):
                            S.emit("te", lambda te, ti=ti, kc=kc, half=half, ps=ps: te.matmul(
                                ps[:], xn2T_sb[:, kc, ti * 128:(ti + 1) * 128],
                                wT_sb[:, kc, half * 512:(half + 1) * 512],
                                start=(kc == 0), stop=(kc == 7)),
                                reads=[xn2Tb, wTb], writes=bnk if kc in (0, 7) else ())
                        S.emit("dve", lambda v, half=half, ps=ps, vp=vp: v.tensor_copy(
                            out=vt_sb[:, vp, half * 512:(half + 1) * 512], in_=ps[:]),
                            reads=bnk, writes=[vtb2])
                    S.dma("act", lambda e, t=t, vp=vp: e.dma_start(out=Vs_sb[3:128, t, :],
                                                                   in_=vt_sb[0:125, vp, :]),
                          reads=[vtb2], writes=[Vsb_t[t]])
                    S.dma("pool", lambda e, t=t, vp=vp: e.dma_start(out=Vs_sb[0:3, t + 1, :],
                                                                    in_=vt_sb[125:128, vp, :]),
                          reads=[vtb2], writes=[Vsb_t[t + 1]])

        S.barrier()
        with (nc.sbuf_tensor("a_sc", [128, H, 144], BF16) as sc_sb,
              nc.sbuf_tensor("a_connd", [128, H, 134], BF16) as conn_d,
              nc.sbuf_tensor("a_bo", [1, D], BF16) as bo_bf,
              nc.sbuf_tensor("a_m", [128, H], F32) as m_sb,
              nc.sbuf_tensor("a_den", [128, H], F32) as den_sb,
              nc.sbuf_tensor("a_den2", [128, H], F32) as den2_sb,
              nc.sbuf_tensor("a_fwT1", [128, H, 128], BF16) as fwT1_sb,
              nc.sbuf_tensor("a_fwT2", [16, H, 128], BF16) as fwT2_sb,
              nc.sbuf_tensor("a_O", [128, 8, 128], BF16) as O_sb,
              nc.sbuf_tensor("a_wl2", [128, 1024], BF16) as wl2_sb):
            load_wT("wo", wl2_sb, B("a_wl2"))
            S.dma("sp", lambda e: e.dma_start(out=conn_d[:].rearrange("p h w -> p (h w)"),
                                              in_=dr["connd"]), writes=[B("conn_d")])
            S.dma("pool", lambda e: e.dma_start(out=bo_bf[:],
                                                in_=dr["bo"].rearrange("(o d) -> o d", o=1)),
                  writes=[B("bo")])
            scb = [B("a_sc0"), B("a_sc1")]
            mb = [B("a_m0"), B("a_m1")]
            denb = [B("a_den0"), B("a_den1")]
            den2b = [B("a_den20"), B("a_den21")]
            fwT1b = [B("a_fwT10"), B("a_fwT11")]
            fwT2b = [B("a_fwT20"), B("a_fwT21")]
            Ob = B("a_O")
            conn_b = B("conn_d")
            S.emit("dve", lambda v: v.memset(sc_sb[:, :, 134:144], 0.0),
                   writes=[scb[0], scb[1]])
            sc_ps = PA[:]
            sc_heads = PA[:].rearrange("p (x j) -> p x j", j=256)[:, 0:8, 0:134]
            wo_ps = PC[:, 512:1024]

            def scores(t, hh):
                for h8 in range(8):
                    h = hh * 8 + h8
                    off = h8 * 256
                    hb2 = (h % 2) * 64
                    S.emit("te", lambda te, h=h, off=off, hb2=hb2, t=t: te.matmul(
                        sc_ps[:, off:off + 134],
                        Q_sb[hb2:hb2 + 64, h // 2, t * 128:(t + 1) * 128],
                        K_sb[hb2:hb2 + 64, h // 2, t * 128:t * 128 + 134],
                        start=True, stop=True),
                        reads=[Qb, Kb], writes=pa(off, off + 134))

            def softmax_pre(t, hh):
                sl = slice(hh * 8, (hh + 1) * 8)
                S.emit("dve", lambda v, sl=sl: v.reduce_max(
                    out=m_sb[:, sl], in_=sc_heads, axis=AX.X),
                    reads=pa(0, 2048), writes=[mb[hh]])
                mbc = m_sb[:, sl].unsqueeze(-1).broadcast_to([128, 8, 134])
                S.emit("dve", lambda v, sl=sl, mbc=mbc: v.tensor_tensor(
                    out=sc_sb[:, sl, 0:134], in0=sc_heads, in1=mbc, op=AT.subtract),
                    reads=pa(0, 2048) + [mb[hh]], writes=[scb[hh]])
                S.emit("act", lambda a, sl=sl: a.activation(
                    out=sc_sb[:, sl, 0:134], in_=sc_sb[:, sl, 0:134], func=AF.Exp),
                    reads=[scb[hh]], writes=[scb[hh]])

            def softmax_post(t, hh):
                sl = slice(hh * 8, (hh + 1) * 8)
                S.emit("dve", lambda v, sl=sl: v.reduce_sum(
                    out=den2_sb[:, sl], in_=sc_sb[:, sl, 0:134], axis=AX.X),
                    reads=[scb[hh]], writes=[den2b[hh]])
                S.emit("dve", lambda v, sl=sl: v.tensor_tensor(
                    out=sc_sb[:, sl, 0:134], in0=sc_sb[:, sl, 0:134],
                    in1=conn_d[:, sl, :], op=AT.mult),
                    reads=[scb[hh], conn_b], writes=[scb[hh]])
                S.emit("dve", lambda v, sl=sl: v.reduce_sum(
                    out=den_sb[:, sl], in_=sc_sb[:, sl, 0:134], axis=AX.X),
                    reads=[scb[hh]], writes=[denb[hh]])
                S.emit("dve", lambda v, sl=sl: v.scalar_tensor_tensor(
                    out=den_sb[:, sl], in0=den2_sb[:, sl], scalar=1e-9,
                    in1=den_sb[:, sl], op0=AT.mult, op1=AT.add),
                    reads=[den2b[hh], denb[hh]], writes=[denb[hh]])
                S.emit("dve", lambda v, sl=sl: v.reciprocal(
                    out=den_sb[:, sl], in_=den_sb[:, sl]),
                    reads=[denb[hh]], writes=[denb[hh]])
                rbc = den_sb[:, sl].unsqueeze(-1).broadcast_to([128, 8, 134])
                S.emit("dve", lambda v, sl=sl, rbc=rbc: v.tensor_tensor(
                    out=sc_sb[:, sl, 0:134], in0=sc_sb[:, sl, 0:134], in1=rbc, op=AT.mult),
                    reads=[scb[hh], denb[hh]], writes=[scb[hh]])

            def fwtrans(t, hh):
                for hg in range(2):
                    for hi in range(4):
                        h = hh * 8 + hg * 4 + hi
                        S.emit("te", lambda te, h=h, hi=hi: te.transpose(
                            a_tps[:, hi * 128:(hi + 1) * 128], sc_sb[:, h, 0:128], ident_bf[:]),
                            reads=[scb[hh], B("identb")], writes=pc(0, 512))
                        S.emit("te", lambda te, h=h, hi=hi: te.transpose(
                            a_tps[0:16, 512 + hi * 128: 512 + (hi + 1) * 128],
                            sc_sb[:, h, 128:144], ident_bf[:]),
                            reads=[scb[hh], B("identb")], writes=pc(0, 512))
                    h0 = hh * 8 + hg * 4
                    S.emit("dve", lambda v, h0=h0: v.tensor_copy(
                        out=fwT1_sb[:, h0:h0 + 4, :].rearrange("p h n -> p (h n)"),
                        in_=a_tps[:, 0:512]),
                        reads=pc(0, 512), writes=[fwT1b[hh]])
                    S.emit("dve", lambda v, h0=h0: v.tensor_copy(
                        out=fwT2_sb[:, h0:h0 + 4, :].rearrange("p h n -> p (h n)"),
                        in_=a_tps[0:16, 512:1024]),
                        reads=pc(0, 512), writes=[fwT2b[hh]])

            def av(t, hh):
                bnk = pb(0, 512) if hh == 0 else pb(512, 1024)
                for h8 in range(8):
                    h = hh * 8 + h8
                    hb2 = (h % 2) * 64
                    S.emit("te", lambda te, h=h, hb2=hb2, t=t: te.matmul(
                        av_ps[hb2:hb2 + 64, (h // 2) * 128:(h // 2) * 128 + 128],
                        Vs_sb[:, t, h * 64:(h + 1) * 64], fwT1_sb[:, h, :],
                        start=True, stop=False, tile_position=(0, hb2)),
                        reads=[Vsb_t[t], fwT1b[hh]], writes=pb(0, 1024))
                    S.emit("te", lambda te, h=h, hb2=hb2, t=t: te.matmul(
                        av_ps[hb2:hb2 + 64, (h // 2) * 128:(h // 2) * 128 + 128],
                        Vs_sb[0:16, t + 1, h * 64:(h + 1) * 64], fwT2_sb[:, h, :],
                        start=False, stop=True, tile_position=(0, hb2)),
                        reads=[Vsb_t[t + 1], fwT2b[hh]], writes=pb(0, 1024))

            def wo_block(t):
                S.emit("dve", lambda v: v.tensor_copy(
                    out=O_sb[:].rearrange("p k n -> p (k n)"), in_=av_ps[:]),
                    reads=pb(0, 1024), writes=[Ob])
                wo_ps2 = PC[:, 512:1024]
                for mi in range(2):
                    S.emit("te", lambda te, mi=mi: te.matmul(
                        wo_ps2[:], ones_bf[:],
                        bo_bf[:, mi * 512:(mi + 1) * 512], start=True, stop=False),
                        reads=[B("ones"), B("bo")], writes=pc(512, 1024))
                    for kc in range(8):
                        S.emit("te", lambda te, kc=kc, mi=mi: te.matmul(
                            wo_ps2[:], O_sb[:, kc, :],
                            wT_sb[:, kc, mi * 512:(mi + 1) * 512],
                            start=False, stop=(kc == 7)),
                            reads=[Ob, wTb],
                            writes=pc(512, 1024) if kc == 7 else ())
                    S.emit("dve", lambda v, t=t, mi=mi: v.tensor_add(
                        out=x_res[:, t, mi * 512:(mi + 1) * 512], in0=wo_ps2[:],
                        in1=x_res[:, t, mi * 512:(mi + 1) * 512]),
                        reads=pc(512, 1024) + [xb[t]], writes=[xb[t]])

            av_ps = PB[:]
            units = [(t, hh) for t in range(NT) for hh in range(2)]
            scores(*units[0])
            for ui, (t, hh) in enumerate(units):
                softmax_pre(t, hh)
                if ui + 1 < len(units):
                    scores(*units[ui + 1])
                if ui >= 1:
                    pt, ph = units[ui - 1]
                    softmax_post(pt, ph)
                    fwtrans(pt, ph)
                    av(pt, ph)
                    if ph == 1:
                        wo_block(pt)
                        emit_ln_tile(pt)
            pt, ph = units[-1]
            softmax_post(pt, ph)
            fwtrans(pt, ph)
            av(pt, ph)
            wo_block(pt)
            emit_ln_tile(pt)
            emit_ln_final()
    S.barrier()
    if debug:
        S.dma("sp", lambda e: e.dma_start(out=dr["dbg_attn"].rearrange("(t p) d -> p t d", p=128),
                                          in_=x_res[:]), reads=list(xb))

    emit_ffn(2, store_out=True, skip_ln=True)
    S.barrier()

    S.barrier()

    S.replay()
    ctx.close()
    return nc


_NC_CACHE = {}


def _get_nc(debug=False):
    if debug not in _NC_CACHE:
        _NC_CACHE[debug] = build_program(debug=debug)
    return _NC_CACHE[debug]


def kernel(L0_attn_x, L0_attn_wq, L0_attn_wk, L0_attn_wv, L0_attn_wo, L0_attn_bo,
           L0_attn_cw1, L0_attn_cb1, L0_attn_cw2, L0_attn_cb2, L0_attn_cw3, L0_attn_cb3,
           L0_ffn_w_gate, L0_ffn_w_up, L0_ffn_w_down, _debug=False):
    x = np.ascontiguousarray(np.asarray(L0_attn_x, dtype=np.float32))
    common = {
        "wq": np.ascontiguousarray(np.asarray(L0_attn_wq, np.float32)),
        "wk": np.ascontiguousarray(np.asarray(L0_attn_wk, np.float32)),
        "wv": np.ascontiguousarray(np.asarray(L0_attn_wv, np.float32)),
        "wo": np.ascontiguousarray(np.asarray(L0_attn_wo, np.float32)),
        "bo": np.ascontiguousarray(np.asarray(L0_attn_bo, np.float32)),
        "cw1": np.ascontiguousarray(np.asarray(L0_attn_cw1, np.float32).reshape(H, 32)),
        "cb1": np.ascontiguousarray(np.asarray(L0_attn_cb1, np.float32)),
        "cw2": np.ascontiguousarray(np.asarray(L0_attn_cw2, np.float32)),
        "cb2": np.ascontiguousarray(np.asarray(L0_attn_cb2, np.float32)),
        "cw3": np.ascontiguousarray(np.asarray(L0_attn_cw3, np.float32).reshape(H, 32)),
        "cb3": np.ascontiguousarray(np.asarray(L0_attn_cb3, np.float32).reshape(H)),
        "w_gate": np.ascontiguousarray(np.asarray(L0_ffn_w_gate, np.float32)),
        "w_up": np.ascontiguousarray(np.asarray(L0_ffn_w_up, np.float32)),
        "w_down": np.ascontiguousarray(np.asarray(L0_ffn_w_down, np.float32)),
        "ident_bf": np.eye(128, dtype=ml_dtypes.bfloat16),
        "ident_f": np.eye(128, dtype=np.float32),
    }
    nc = _get_nc(debug=_debug)
    in_maps = []
    for c in range(8):
        m = dict(common)
        m["x"] = np.ascontiguousarray(x[c])
        in_maps.append(m)
    res = run_bass_kernel_spmd(nc, in_maps, core_ids=list(range(8)))
    out = np.stack([np.asarray(res.results[c]["out"]) for c in range(8)], axis=0)
    if _debug:
        dbg = {
            "ffn1": np.stack([np.asarray(res.results[c]["dbg_ffn1"]) for c in range(8)]),
            "wgT": np.asarray(res.results[0]["dbg_wgT"]),
            "attn": np.stack([np.asarray(res.results[c]["dbg_attn"]) for c in range(8)]),
        }
        return out, dbg
    return out
